# version 20
# baseline (speedup 1.0000x reference)
import contextlib
import numpy as np
import ml_dtypes

import concourse.bass as bass
import concourse.mybir as mybir
from concourse.bass_utils import run_bass_kernel_spmd

F32 = mybir.dt.float32
BF16 = mybir.dt.bfloat16
AF = mybir.ActivationFunctionType
ALU = mybir.AluOpType
AX = mybir.AxisListType
BF = ml_dtypes.bfloat16

NCORE = 8
L = 16384
TPC = L // NCORE
DM = 1024
HALO = 33
TH = TPC + HALO
THP = 2112
ENGINES = ("sync", "tensor", "vector", "scalar", "gpsimd")


class Sem:
    def __init__(self, h):
        self.h = h
        self.val = 0


class Prog:
    def __init__(self, nc):
        self.nc = nc
        self.ops = {e: [] for e in ENGINES}
        self.stack = contextlib.ExitStack()
        self._semn = 0
        self.eng = {"sync": nc.sync, "tensor": nc.tensor, "vector": nc.vector,
                    "scalar": nc.scalar, "gpsimd": nc.gpsimd}

    def _alloc_stack(self):
        sc = getattr(self, "_scopes", None)
        return sc[-1] if sc else self.stack

    def sbuf(self, name, shape, dtype):
        return self._alloc_stack().enter_context(
            self.nc.sbuf_tensor(name, shape, dtype))

    def scope_open(self):
        if not hasattr(self, "_scopes"):
            self._scopes = []
        self._scopes.append(contextlib.ExitStack())

    def scope_close(self, barrier_sem):
        self._scopes.pop().close()
        self._ccols = {}
        self.barrier(barrier_sem)

    def barrier(self, barrier_sem):
        base = barrier_sem.val
        for e in ENGINES:
            self.op(e, "nop", sem=barrier_sem)
        for e in ENGINES:
            self.wait(e, barrier_sem, base + len(ENGINES))

    def psum(self, name, shape, dtype=F32):
        return self._alloc_stack().enter_context(
            self.nc.psum_tensor(name, shape, dtype))

    def sem(self, name=None):
        self._semn += 1
        return Sem(self.stack.enter_context(
            self.nc.semaphore(name or f"s{self._semn}")))

    def on(self, engine, fn):
        self.ops[engine].append(fn)

    def wait(self, engine, sem, val=None):
        v = sem.val if val is None else val
        self.on(engine, lambda e=engine, s=sem.h, vv=v: self.eng[e].wait_ge(s, vv))

    def dma(self, engine, out, in_, sem=None, **kw):
        if sem is not None:
            sem.val += 16
        def f(out=out, in_=in_, sem=sem, kw=kw, e=engine):
            inst = self.eng[e].dma_start(out=out, in_=in_, **kw)
            if sem is not None:
                inst.then_inc(sem.h, 16)
        self.on(engine, f)

    def op(self, engine, method, *args, sem=None, semn=1, **kw):
        if sem is not None:
            sem.val += semn
        def f(args=args, kw=kw, sem=sem, semn=semn, e=engine, m=method):
            inst = getattr(self.eng[e], m)(*args, **kw)
            if sem is not None:
                inst.then_inc(sem.h, semn)
        self.on(engine, f)

    def opw(self, engine, sem, method, *args, **kw):
        self.op(engine, method, *args, sem=sem, **kw)
        self.wait(engine, sem)

    def constcol(self, value, sem):
        if not hasattr(self, "_ccols"):
            self._ccols = {}
        if value not in self._ccols:
            self._ccn = getattr(self, "_ccn", 0) + 1
            t = self.sbuf(f"cc{self._ccn}", [128, 1], F32)
            self.op("vector", "memset", t[:], float(value), sem=sem)
            self._ccols[value] = t
        return self._ccols[value]

    def emit_and_finish(self):
        with self.nc.Block() as block:
            for eng_name in ENGINES:
                thunks = self.ops[eng_name]
                getattr(block, eng_name)(lambda e, th=thunks: [f() for f in th])
        self.stack.close()


class PP:

    def __init__(self, p, name, n, shape, dtype=F32, writer="tensor"):
        self.p = p
        self.tiles = [p.psum(f"{name}{i}", shape, dtype) for i in range(n)]
        self.free_at = [None] * n
        self.idx = 0
        self.writer = writer

    def get(self):
        i = self.idx % len(self.tiles)
        self.idx += 1
        if self.free_at[i] is not None:
            sem, val = self.free_at[i]
            self.p.wait(self.writer, sem, val)
            self.free_at[i] = None
        return self.tiles[i], i

    def done(self, i, sem):
        self.free_at[i] = (sem, sem.val)


def nblocks(total, blk=512, base=0):
    out = []
    o = 0
    while o < total:
        n = min(blk, total - o)
        out.append((base + o, n))
        o += n
    return out


def stage_a(p, io, hook_tm=None, hook_attn=None):
    nc = p.nc
    xb = [p.sbuf(f"A_xb{i}", [128, THP], BF16) for i in range(8)]
    xn = [p.sbuf(f"A_xn{i}", [128, THP], BF16) for i in range(8)]
    sqb = [p.sbuf(f"A_sqb{i}", [128, 512], BF16) for i in range(2)]
    rowbuf = p.sbuf("A_rows", [1, 3 * THP], F32)
    brep = [p.sbuf(f"A_brep{i}", [128, THP], BF16) for i in range(2)]
    onescol = p.sbuf("A_ones", [128, 2], BF16)
    onesrf = p.sbuf("A_onesrf", [1, 128], F32)
    ident = p.sbuf("A_ident", [128, 128], BF16)
    w1t = [p.sbuf(f"A_w1t{i}", [128, 1024], BF16) for i in range(8)]
    w2t = [p.sbuf(f"A_w2t{i}", [128, 1024], BF16) for i in range(8)]
    win = xb
    cwq = [p.sbuf(f"A_cwq{i}", [128, 4], F32) for i in range(8)]
    cwk = [p.sbuf(f"A_cwk{i}", [128, 4], F32) for i in range(8)]
    cwv = [p.sbuf(f"A_cwv{i}", [128, 4], F32) for i in range(8)]
    wbt = [p.sbuf(f"A_wbt{i}", [128, 4], BF16) for i in range(8)]
    gw1x = [p.sbuf(f"A_gw1x{i}", [128, 512], BF16) for i in range(8)]
    zraw = p.sbuf("A_zraw", [128, THP], BF16)
    zconv = p.sbuf("A_zconv", [128, THP], BF16)
    zfin = p.sbuf("A_zfin", [128, THP], BF16)
    ttmp = p.sbuf("A_ttmp", [128, 512], BF16)
    cosq = p.sbuf("A_cosq", [64, TPC], BF16)
    sinq = p.sbuf("A_sinq", [64, TPC], BF16)
    cosk = p.sbuf("A_cosk", [64, TPC], BF16)
    sink = p.sbuf("A_sink", [64, TPC], BF16)
    rtile = p.sbuf("A_rt", [128, 16], F32)
    rtile2 = p.sbuf("A_rt2", [128, 16], F32)
    rtile3 = p.sbuf("A_rt3", [128, 16], F32)
    rcol = p.sbuf("A_rcol", [128, 2], F32)
    gtm = p.sbuf("A_gtm", [128, 16], F32)

    pp = PP(p, "A_pp", 6, [128, 512])
    pps = pp
    ppb = pp
    ppt = PP(p, "A_ppt", 2, [128, 512], BF16)
    ppg = pp

    s_x = p.sem("sx"); s_c = p.sem("sc"); s_w = p.sem("sw")
    s_v = p.sem("sv"); s_t = p.sem("st"); s_sc = p.sem("ssc")
    s_out = p.sem("sout")

    p.dma("sync", onescol[:], io["ones2"], sem=s_c)
    p.dma("sync", onesrf[:], io["onesrf"], sem=s_c)
    p.dma("sync", ident[:], io["ident"], sem=s_c)
    p.dma("sync", cosq[:], io["cosq"], sem=s_c)
    p.dma("sync", sinq[:], io["sinq"], sem=s_c)
    p.dma("sync", cosk[:], io["cosk"], sem=s_c)
    p.dma("sync", sink[:], io["sink"], sem=s_c)
    for i in range(8):
        p.dma("gpsimd", xb[i][:, 0:TH], io["xT"][128 * i:128 * i + 128, :],
              sem=s_x)
    X_DONE = s_x.val

    def load_w(dst, src, cols, sem):
        for i in range(8):
            p.dma("gpsimd", dst[i][:, 0:cols],
                  src[128 * i:128 * i + 128, 0:cols], sem=sem)
    load_w(w1t, io["Wq"], 1024, s_w)
    load_w(w2t, io["Wk"], 1024, s_w)
    load_w(cwq, io["qc_w"], 4, s_w)
    load_w(cwk, io["kc_w"], 4, s_w)
    load_w(cwv, io["vc_w"], 4, s_w)
    load_w(wbt, io["Wb"], 4, s_w)
    W_QK = s_w.val

    for i in range(8):
        p.wait("vector", s_x, 16 * (i + 1))
        p.op("vector", "memset", xb[i][:, TH:THP], 0.0, sem=s_v)
    PAD_Z = s_v.val
    p.wait("tensor", s_c)
    p.wait("tensor", s_v, PAD_Z)
    for (o, n) in nblocks(THP):
        ps, pi = pps.get()
        for i in range(8):
            p.op("tensor", "matmul", ps[0:1, 0:n], onescol[:, 0:1],
                 xb[i][:, o:o + n], start=(i == 0), stop=(i == 7),
                 sem=s_t if i == 7 else None)
        p.wait("scalar", s_t)
        p.op("scalar", "copy", rowbuf[0:1, o:o + n], ps[0:1, 0:n], sem=s_sc)
        pps.done(pi, s_sc)
    SSQ = THP
    sq_lastc = [None, None]
    for (o, n) in nblocks(THP):
        ps, pi = pps.get()
        for i in range(8):
            b = i % 2
            sq = sqb[b]
            if sq_lastc[b] is not None:
                p.wait("vector", s_t, sq_lastc[b])
            p.op("vector", "tensor_mul", sq[:, 0:n], xb[i][:, o:o + n],
                 xb[i][:, o:o + n], sem=s_v)
            p.wait("tensor", s_v)
            p.op("tensor", "matmul", ps[0:1, 0:n], onescol[:, 0:1], sq[:, 0:n],
                 start=(i == 0), stop=(i == 7), sem=s_t)
            sq_lastc[b] = s_t.val
        p.wait("scalar", s_t)
        p.op("scalar", "copy", rowbuf[0:1, SSQ + o:SSQ + o + n], ps[0:1, 0:n],
             sem=s_sc)
        pps.done(pi, s_sc)
    R0, R1, R2 = 0, THP, 2 * THP
    p.wait("vector", s_sc)
    p.opw("vector", s_v, "tensor_scalar_mul", rowbuf[0:1, R0:R0 + THP],
          rowbuf[0:1, R0:R0 + THP], 1.0 / DM)
    p.opw("vector", s_v, "tensor_mul", rowbuf[0:1, R2:R2 + THP],
          rowbuf[0:1, R0:R0 + THP], rowbuf[0:1, R0:R0 + THP])
    p.opw("vector", s_v, "tensor_scalar_mul", rowbuf[0:1, R1:R1 + THP],
          rowbuf[0:1, R1:R1 + THP], 1.0 / DM)
    p.op("vector", "tensor_sub", rowbuf[0:1, R1:R1 + THP],
         rowbuf[0:1, R1:R1 + THP], rowbuf[0:1, R2:R2 + THP], sem=s_v)
    p.wait("scalar", s_v)
    eps_c = p.constcol(1e-5, s_v)
    p.wait("scalar", s_v)
    p.op("scalar", "activation", rowbuf[0:1, R1:R1 + THP],
         rowbuf[0:1, R1:R1 + THP], AF.Sqrt, bias=eps_c[0:1, 0:1], sem=s_sc)
    p.wait("vector", s_sc)
    p.opw("vector", s_v, "reciprocal", rowbuf[0:1, R1:R1 + THP],
          rowbuf[0:1, R1:R1 + THP])
    p.opw("vector", s_v, "tensor_mul", rowbuf[0:1, R2:R2 + THP],
          rowbuf[0:1, R0:R0 + THP], rowbuf[0:1, R1:R1 + THP])
    p.op("vector", "tensor_scalar_mul", rowbuf[0:1, R2:R2 + THP],
         rowbuf[0:1, R2:R2 + THP], -1.0, sem=s_v)
    ROWS_BF = s_v.val
    p.wait("tensor", s_v, ROWS_BF)
    for r in range(2):
        base = R1 if r == 0 else R2
        for (o, n) in nblocks(THP):
            ps, pi = ppb.get()
            p.op("tensor", "matmul", ps[:, 0:n], onesrf[:],
                 rowbuf[0:1, base + o:base + o + n], start=True,
                 stop=True, sem=s_t)
            p.wait("vector", s_t)
            p.op("vector", "tensor_copy", brep[r][:, o:o + n], ps[:, 0:n],
                 sem=s_v)
            ppb.done(pi, s_v)
    for i in range(8):
        p.opw("vector", s_v, "tensor_mul", xn[i][:], xb[i][:], brep[0][:])
        p.op("vector", "tensor_add", xn[i][:], xn[i][:], brep[1][:], sem=s_v)
    XN_DONE = s_v.val

    def proj_conv(wtiles, cwt, out_fm, out_tm, with_halo_out, wait_w):
        p.wait("tensor", s_w, wait_w)
        p.wait("tensor", s_v, XN_DONE)
        lo = 3 if with_halo_out else HALO
        for m in range(8):
            for (o, n) in nblocks(THP):
                ps, pi = pp.get()
                for i in range(8):
                    p.op("tensor", "matmul", ps[:, 0:n],
                         wtiles[i][:, 128 * m:128 * m + 128],
                         xn[i][:, o:o + n], start=(i == 0), stop=(i == 7),
                         sem=s_t if i == 7 else None)
                p.wait("scalar", s_t)
                p.op("scalar", "copy", zraw[:, o:o + n], ps[:, 0:n], sem=s_sc)
                pp.done(pi, s_sc)
            p.wait("vector", s_sc)
            if m == 0 and "dbg_zraw" in io:
                p.wait("sync", s_sc)
                p.dma("sync", io["dbg_zraw"], zraw[:, 0:THP], sem=s_out)
                p.wait("vector", s_out)
            p.opw("vector", s_v, "tensor_scalar_mul", zconv[:, lo:TH],
                  zraw[:, lo - 3:TH - 3], cwt[m][:, 0:1])
            for j in range(1, 4):
                p.opw("vector", s_v, "scalar_tensor_tensor", zconv[:, lo:TH],
                      zraw[:, lo - 3 + j:TH - 3 + j], cwt[m][:, j:j + 1],
                      zconv[:, lo:TH], op0=ALU.mult, op1=ALU.add)
            p.op("vector", "nop", sem=s_v)
            if m == 0 and "dbg_zconv" in io:
                p.wait("sync", s_v)
                p.dma("sync", io["dbg_zconv"], zconv[:, 0:THP], sem=s_out)
            p.wait("scalar", s_v)
            p.op("scalar", "activation", zfin[:, lo:TH], zconv[:, lo:TH],
                 AF.Silu, sem=s_sc)
            p.wait("sync", s_sc)
            p.dma("sync", out_fm[128 * m:128 * m + 128, :], zfin[:, lo:TH],
                  sem=s_out)
            if out_tm is not None:
                p.wait("tensor", s_sc)
                for tg in range(TPC // 512):
                    ps, pi = ppt.get()
                    for j in range(4):
                        tt = 4 * tg + j
                        p.op("tensor", "transpose",
                             ps[:, 128 * j:128 * j + 128],
                             zfin[:, HALO + 128 * tt:HALO + 128 * tt + 128],
                             ident[:], sem=s_t if j == 3 else None)
                    if tg > 0:
                        p.wait("vector", s_out)
                    p.wait("vector", s_t)
                    p.op("vector", "tensor_copy", ttmp[:], ps[:, 0:512],
                         sem=s_v)
                    ppt.done(pi, s_v)
                    p.wait("sync", s_v)
                    p.dma("sync",
                          out_tm[512 * tg:512 * tg + 512,
                                 128 * m:128 * m + 128].rearrange(
                              "(t p) d -> p t d", p=128),
                          ttmp[:].rearrange("p (t d) -> p t d", d=128),
                          sem=s_out)
            p.wait("scalar", s_out)
            p.wait("vector", s_out)

    proj_conv(w1t, cwq, io["cq"], None, False, W_QK)
    p.wait("gpsimd", s_t)
    load_w(w1t, io["Wv"], 1024, s_w)
    W_V = s_w.val
    proj_conv(w2t, cwk, io["ck"], io["ckt"], False, W_QK)
    proj_conv(w1t, cwv, io["cv"], io["cvt"], True, W_V)

    for (o, n) in nblocks(TPC):
        ps, pi = pp.get()
        for i in range(8):
            p.op("tensor", "matmul", ps[0:4, 0:n], wbt[i][:, 0:4],
                 xn[i][:, HALO + o:HALO + o + n], start=(i == 0),
                 stop=(i == 7), sem=s_t if i == 7 else None)
        p.wait("scalar", s_t)
        p.op("scalar", "activation", zfin[0:4, o:o + n], ps[0:4, 0:n],
             AF.Sigmoid, sem=s_sc)
        pp.done(pi, s_sc)
    p.wait("sync", s_sc)
    p.dma("sync", io["cb"], zfin[0:4, 0:TPC], sem=s_out)
    if hook_tm is not None:
        hook_tm(s_out)

    def win_mm(m_local, dest_fn):
        for (o, n) in nblocks(TPC):
            ps, pi = pp.get()
            for i in range(8):
                p.op("tensor", "matmul", ps[:, 0:n],
                     win[i][:, 128 * m_local:128 * m_local + 128],
                     xn[i][:, HALO + o:HALO + o + n], start=(i == 0),
                     stop=(i == 7), sem=s_t if i == 7 else None)
            p.wait("scalar", s_t)
            p.op("scalar", "copy", zraw[:, o:o + n], ps[:, 0:n], sem=s_sc)
            pp.done(pi, s_sc)

    rsc = p.sbuf("A_rsc", [64, TPC], BF16)
    sqa = p.sbuf("A_sqa", [64, TPC], BF16)
    nrow = p.sbuf("A_nrow", [1, TPC], F32)
    mxrow = p.sbuf("A_mxrow", [1, 4], F32)
    p.op("vector", "memset", mxrow[:], 0.0, sem=s_v)
    def rope_and_store(dst, hh, cosz, sinz, mxslot):
        p.wait("vector", s_sc)
        p.op("vector", "tensor_mul", zfin[0:64, 0:TPC], zraw[0:64, 0:TPC],
             cosz[:])
        p.opw("vector", s_v, "tensor_copy", zconv[0:64, 0:TPC],
              zraw[64:128, 0:TPC])
        p.opw("vector", s_v, "tensor_mul", rsc[:], zconv[0:64, 0:TPC],
              sinz[:])
        p.opw("vector", s_v, "tensor_sub", zfin[0:64, 0:TPC],
              zfin[0:64, 0:TPC], rsc[:])
        p.opw("vector", s_v, "tensor_mul", rsc[:], zconv[0:64, 0:TPC],
              cosz[:])
        p.opw("vector", s_v, "tensor_mul", zconv[0:64, 0:TPC],
              zraw[0:64, 0:TPC], sinz[:])
        p.op("vector", "tensor_add", zconv[0:64, 0:TPC], zconv[0:64, 0:TPC],
             rsc[:], sem=s_v)
        p.wait("sync", s_v)
        p.dma("sync", dst[128 * hh:128 * hh + 64, :], zfin[0:64, 0:TPC],
              sem=s_out)
        p.dma("sync", dst[128 * hh + 64:128 * hh + 128, :],
              zconv[0:64, 0:TPC], sem=s_out)
        p.opw("vector", s_v, "tensor_mul", sqa[:], zfin[0:64, 0:TPC],
              zfin[0:64, 0:TPC])
        p.op("vector", "tensor_mul", rsc[:], zconv[0:64, 0:TPC],
             zconv[0:64, 0:TPC], sem=s_v)
        p.wait("tensor", s_v)
        for (o, n) in nblocks(TPC):
            ps, pi = pp.get()
            p.op("tensor", "matmul", ps[0:1, 0:n], onescol[0:64, 0:1],
                 sqa[:, o:o + n], start=True, stop=False)
            p.op("tensor", "matmul", ps[0:1, 0:n], onescol[0:64, 0:1],
                 rsc[:, o:o + n], start=False, stop=True, sem=s_t)
            p.wait("vector", s_t)
            p.op("vector", "tensor_copy", nrow[0:1, o:o + n], ps[0:1, 0:n],
                 sem=s_v)
            pp.done(pi, s_v)
        p.opw("vector", s_v, "reduce_max", mxrow[0:1, 2:3], nrow[:], axis=AX.X)
        p.opw("vector", s_v, "tensor_max", mxrow[0:1, mxslot:mxslot + 1],
              mxrow[0:1, mxslot:mxslot + 1], mxrow[0:1, 2:3])
        p.wait("vector", s_out)
        p.wait("scalar", s_out)

    p.wait("gpsimd", s_v, XN_DONE)
    p.wait("gpsimd", s_t)
    for i in range(8):
        p.dma("gpsimd", win[i][:, 0:2048],
              io["w_inT"][128 * i:128 * i + 128, 0:2048], sem=s_w)
    W_H0 = s_w.val
    p.wait("tensor", s_w, W_H0)
    p.wait("sync", s_out)
    for m in range(0, 8):
        win_mm(m, None)
        p.wait("sync", s_sc)
        p.dma("sync", io["qf"][128 * m:128 * m + 128, :], zraw[:, 0:TPC],
              sem=s_out)
        p.wait("scalar", s_out)
    for m in range(8, 16):
        win_mm(m, None)
        rope_and_store(io["caq"], m - 8, cosq, sinq, 0)
    p.wait("gpsimd", s_t)
    for i in range(8):
        p.dma("gpsimd", win[i][:, 0:2064],
              io["w_inT"][128 * i:128 * i + 128, 2048:4112], sem=s_w)
    W_H1 = s_w.val
    p.wait("tensor", s_w, W_H1)
    for m in range(0, 8):
        win_mm(m, None)
        rope_and_store(io["cak"], m, cosk, sink, 1)
    for m in range(8, 16):
        hh = m - 8
        win_mm(m, None)
        p.wait("tensor", s_sc)
        for tg in range(TPC // 512):
            ps, pi = ppt.get()
            for j in range(4):
                tt = 4 * tg + j
                p.op("tensor", "transpose", ps[:, 128 * j:128 * j + 128],
                     zraw[:, 128 * tt:128 * tt + 128], ident[:],
                     sem=s_t if j == 3 else None)
            if tg > 0:
                p.wait("vector", s_out)
            p.wait("vector", s_t)
            p.op("vector", "tensor_copy", ttmp[:], ps[:, 0:512], sem=s_v)
            ppt.done(pi, s_v)
            p.wait("sync", s_v)
            p.dma("sync",
                  io["cav"][512 * tg:512 * tg + 512,
                            128 * hh:128 * hh + 128].rearrange(
                      "(t p) d -> p t d", p=128),
                  ttmp[:].rearrange("p (t d) -> p t d", d=128), sem=s_out)
        p.wait("scalar", s_out)
    if hook_attn is not None:
        hook_attn(s_out)
    for lt in range(TPC // 128):
        ps, pi = ppg.get()
        for i in range(8):
            p.op("tensor", "matmul", ps[:, 0:16],
                 xn[i][:, HALO + 128 * lt:HALO + 128 * lt + 128],
                 win[i][:, 2048:2064], start=(i == 0), stop=(i == 7),
                 sem=s_t if i == 7 else None)
        p.wait("vector", s_t)
        p.opw("vector", s_v, "tensor_copy", rtile[:], ps[:, 0:16])
        ppg.done(pi, s_v)
        p.opw("vector", s_v, "reduce_max", rcol[:, 0:1], rtile[:], axis=AX.X)
        p.opw("vector", s_v, "tensor_scalar", rtile2[:], rtile[:],
              rcol[:, 0:1], None, ALU.is_ge)
        p.opw("vector", s_v, "scalar_tensor_tensor", rtile3[:], rtile2[:],
              -1e9, rtile[:], op0=ALU.mult, op1=ALU.add)
        p.opw("vector", s_v, "reduce_max", rcol[:, 1:2], rtile3[:], axis=AX.X)
        p.op("vector", "tensor_scalar", rtile2[:], rtile[:], rcol[:, 1:2],
             None, ALU.is_ge, sem=s_v)
        p.wait("scalar", s_v)
        p.op("scalar", "activation", rtile[:], rtile[:], AF.Sigmoid, sem=s_sc)
        p.wait("vector", s_sc)
        p.wait("vector", s_v)
        p.op("vector", "tensor_mul", gtm[:], rtile[:], rtile2[:], sem=s_v)
        p.wait("sync", s_v)
        p.dma("sync", io["g16"][128 * lt:128 * lt + 128, :], gtm[:], sem=s_out)
        p.wait("vector", s_out)
        p.wait("scalar", s_out)

    for i in range(8):
        p.dma("gpsimd", gw1x[i][:], io["gW1"][128 * i:128 * i + 128, 0:512],
              sem=s_w)
    p.wait("tensor", s_w)
    for m in range(4):
        for (o, n) in nblocks(TPC):
            ps, pi = pp.get()
            for i in range(8):
                p.op("tensor", "matmul", ps[:, 0:n],
                     gw1x[i][:, 128 * m:128 * m + 128],
                     xn[i][:, HALO + o:HALO + o + n], start=(i == 0),
                     stop=(i == 7), sem=s_t if i == 7 else None)
            p.wait("scalar", s_t)
            p.op("scalar", "copy", zraw[:, o:o + n], ps[:, 0:n], sem=s_sc)
            pp.done(pi, s_sc)
        p.wait("sync", s_sc)
        p.dma("sync", io["zx"][128 * m:128 * m + 128, :], zraw[:, 0:TPC],
              sem=s_out)
        p.wait("scalar", s_out)

    p.wait("sync", s_v)
    p.dma("sync", io["qkm"], mxrow[0:1, 0:2], sem=s_out)
    p.wait("sync", s_out)


def rope_tables_np(t0, t1):
    half = 64
    inv_freq = (1.0 / 10000.0 ** (np.arange(half, dtype=np.float32) / half)
                ).astype(np.float32)
    ang = np.arange(t0, t1, dtype=np.float32)[:, None] * inv_freq[None, :]
    return (np.ascontiguousarray(np.cos(ang).astype(np.float32).T),
            np.ascontiguousarray(np.sin(ang).astype(np.float32).T))


def stage_a_inputs(inputs, c):
    x = inputs["L0_attn_x"][0]
    t0 = c * TPC
    lo = t0 - HALO
    if lo < 0:
        xs = np.concatenate([np.zeros((-lo, DM), np.float32),
                             x[0:t0 + TPC]], 0)
    else:
        xs = x[lo:t0 + TPC]
    cos, sin = rope_tables_np(t0, t0 + TPC)
    s = np.float32(1.0 / np.sqrt(128.0))
    return {
        "xT": np.ascontiguousarray(xs.T),
        "Wq": inputs["L0_attn_Wq"], "Wk": inputs["L0_attn_Wk"],
        "Wv": inputs["L0_attn_Wv"], "Wb": inputs["L0_attn_Wb"],
        "qc_w": inputs["L0_attn_qc_w"], "kc_w": inputs["L0_attn_kc_w"],
        "vc_w": inputs["L0_attn_vc_w"],
        "w_inT": np.ascontiguousarray(inputs["L0_ffn_w_in"].T),
        "gW1": inputs["L0_attn_gW1"],
        "ones2": np.ones((128, 2), BF),
        "onesrf": np.ones((1, 128), np.float32),
        "ident": np.eye(128, dtype=BF),
        "cosq": (cos * s).astype(BF), "sinq": (sin * s).astype(BF),
        "cosk": cos.astype(BF), "sink": sin.astype(BF),
    }


STAGE_A_INS = {
    "xT": ([DM, TH], F32), "Wq": ([DM, DM], F32), "Wk": ([DM, DM], F32),
    "Wv": ([DM, DM], F32), "Wb": ([DM, 4], F32),
    "qc_w": ([DM, 4], F32), "kc_w": ([DM, 4], F32), "vc_w": ([DM, 4], F32),
    "w_inT": ([DM, 4112], F32), "gW1": ([5120, 512], F32),
    "ones2": ([128, 2], BF16), "onesrf": ([1, 128], F32),
    "ident": ([128, 128], BF16),
    "cosq": ([64, TPC], BF16), "sinq": ([64, TPC], BF16),
    "cosk": ([64, TPC], BF16), "sink": ([64, TPC], BF16),
}
STAGE_A_OUTS = {
    "dbg_zraw": ([128, THP], BF16), "dbg_zconv": ([128, THP], BF16),
    "cq": ([DM, TPC], BF16), "ck": ([DM, TPC], BF16),
    "ckt": ([TPC, DM], BF16), "cvt": ([TPC, DM], BF16),
    "cv": ([DM, TH - 3], BF16),
    "cb": ([4, TPC], BF16),
    "caq": ([DM, TPC], BF16), "cak": ([DM, TPC], BF16),
    "cav": ([TPC, DM], BF16),
    "qf": ([DM, TPC], BF16), "g16": ([TPC, 16], F32),
    "zx": ([512, TPC], BF16), "qkm": ([1, 2], F32),
}


def build_stage_a():
    nc = bass.Bass("TRN2", target_bir_lowering=False)
    io = {}
    for k, (shape, dt) in STAGE_A_INS.items():
        io[k] = nc.dram_tensor(k, shape, dt, kind="ExternalInput").ap()
    for k, (shape, dt) in STAGE_A_OUTS.items():
        io[k] = nc.dram_tensor(k, shape, dt, kind="ExternalOutput").ap()
    p = Prog(nc)
    stage_a(p, io)
    p.emit_and_finish()
    return nc


CH = 128
NCHUNK = L // CH
DB = 16


def stage_delta(p, io, fz=None):
    kfm = [[p.sbuf(f"D_kfm{b}_{i}", [128, DB * CH], BF16) for i in range(2)]
           for b in range(2)]
    qfm = [[p.sbuf(f"D_qfm{b}_{i}", [128, DB * CH], BF16) for i in range(2)]
           for b in range(2)]
    ktmb = [p.sbuf(f"D_ktm{b}", [128, DB * 256], BF16) for b in range(2)]
    vtmb = [p.sbuf(f"D_vtm{b}", [128, DB * 128], BF16) for b in range(2)]
    bcolb = [p.sbuf(f"D_bc{b}", [128, DB], F32) for b in range(2)]
    browb = [p.sbuf(f"D_br{b}", [1, DB * CH], F32) for b in range(2)]
    qn = [p.sbuf(f"D_qn{i}", [128, DB * CH], BF16) for i in range(2)]
    kn = [p.sbuf(f"D_kn{i}", [128, DB * CH], BF16) for i in range(2)]
    kb = [p.sbuf(f"D_kb{i}", [128, DB * CH], BF16) for i in range(2)]
    sqt = p.sbuf("D_sqt", [128, DB * CH], BF16)
    sqt2 = p.sbuf("D_sqt2", [128, DB * CH], BF16)
    rows = p.sbuf("D_rows", [1, 3 * DB * CH], F32)
    rep = p.sbuf("D_rep", [128, DB * CH], BF16)
    rkcol = p.sbuf("D_rkcol", [128, DB], F32)
    skbcol = p.sbuf("D_skb", [128, DB], F32)
    sqtm = p.sbuf("D_sqtm", [128, DB * 256], BF16)
    Mt = [p.sbuf(f"D_M{i}", [128, 512], BF16) for i in range(2)]
    MTt = [p.sbuf(f"D_MT{i}", [128, 512], BF16) for i in range(2)]
    Qt = [p.sbuf(f"D_Q{i}", [128, 512], BF16) for i in range(2)]
    TTs = p.sbuf("D_TTs", [128, 512], BF16)
    nwf = [p.sbuf(f"D_nwf{i}", [128, 512], BF16) for i in range(2)]
    upt = p.sbuf("D_upt", [128, 128], BF16)
    upst = p.sbuf("D_upst", [128, 128], BF16)
    attnT = p.sbuf("D_attnT", [128, 512], BF16)
    vbt = p.sbuf("D_vbt", [128, 512], BF16)
    S0t = [p.sbuf(f"D_S0t{i}", [128, 128], BF16) for i in range(2)]
    S1t = [p.sbuf(f"D_S1t{i}", [128, 128], BF16) for i in range(2)]
    obat = p.sbuf("D_obat", [128, DB * 128], BF16)
    identbf = p.sbuf("D_id", [128, 128], BF16)
    onescol = p.sbuf("D_ones", [128, 2], BF16)
    onesrf = p.sbuf("D_onesrf", [1, 128], F32)
    mnu1 = p.sbuf("D_mnu1", [128, 128], F32)
    mnl1 = p.sbuf("D_mnl1", [128, 128], F32)
    mu0 = p.sbuf("D_mu0", [128, 128], F32)

    ofm = p.sbuf("D_ofm", [128, DB * 128], BF16)

    ppS = PP(p, "D_ppS", 2, [128, 128])
    pp = PP(p, "D_pp", 4, [128, 512])
    ppt = PP(p, "D_ppt", 2, [128, 512], BF16)
    S0p, S1p = ppS.tiles[0], ppS.tiles[1]

    s_in = p.sem("din")
    s_v = p.sem("dv")
    s_t = p.sem("dt")
    s_sc = p.sem("dsc")
    s_out = p.sem("dout")
    s_of = p.sem("dof")
    s_c = p.sem("dc")

    p.dma("sync", identbf[:], io["ident"], sem=s_c)
    p.dma("sync", onescol[:], io["ones2"], sem=s_c)
    p.dma("sync", onesrf[:], io["onesrf"], sem=s_c)
    p.dma("sync", mnu1[:], io["mnu1"], sem=s_c)
    p.dma("sync", mnl1[:], io["mnl1"], sem=s_c)
    p.dma("sync", mu0[:], io["mu0"], sem=s_c)
    p.wait("tensor", s_c)
    p.wait("vector", s_c)
    eps6 = p.constcol(1e-6, s_v)
    p.wait("scalar", s_v)

    NB = NCHUNK // DB

    if fz is not None:
        rowt16 = p.sbuf("D_rt16", [16, 128], BF16)
        dcolb = p.sbuf("D_dcolb", [128, DB], F32)
        for s in range(NCORE):
            p.wait("gpsimd", fz["sem_g"], fz["stv_g"][s])
            if s > 0:
                p.wait("gpsimd", s_t)
            p.dma("gpsimd", rowt16[:],
                  io["bh_row"][0:1, s * TPC:(s + 1) * TPC].rearrange(
                      "r (c i) -> (r c) i", i=128), sem=s_in)
            p.wait("tensor", s_in)
            ps, pi = ppt.get()
            p.op("tensor", "transpose", ps[:, 0:16], rowt16[:],
                 identbf[0:16, 0:16], sem=s_t)
            p.wait("vector", s_t)
            p.op("vector", "tensor_copy", dcolb[:], ps[:, 0:16], sem=s_v)
            ppt.done(pi, s_v)
            p.wait("sync", s_v)
            p.dma("sync", io["bh_col"][:, s * DB:(s + 1) * DB], dcolb[:],
                  sem=s_c)
            p.wait("vector", s_c)
        p.wait("sync", s_c)

    def load_batch(b):
        bb = b % 2
        t0 = b * DB * CH
        if fz is not None:
            p.wait("sync", fz["sem_q"], fz["stv_q"][b])
            p.wait("sync", fz["sem_s"], fz["stv_s"][b])
        for i in range(2):
            p.dma("sync", kfm[bb][i][:],
                  io["kh_fm"][128 * i:128 * i + 128, t0:t0 + DB * CH],
                  sem=s_in)
            p.dma("sync", qfm[bb][i][:],
                  io["qh_fm"][128 * i:128 * i + 128, t0:t0 + DB * CH],
                  sem=s_in)
        p.dma("sync", ktmb[bb][:].rearrange("p (c d) -> p c d", d=256),
              io["kh_tm"][t0:t0 + DB * CH, :].rearrange(
                  "(c p) d -> p c d", p=128), sem=s_in)
        p.dma("sync", vtmb[bb][:].rearrange("p (c d) -> p c d", d=128),
              io["vh_tm"][t0:t0 + DB * CH, :].rearrange(
                  "(c p) d -> p c d", p=128), sem=s_in)
        p.dma("sync", browb[bb][:], io["bh_row"][0:1, t0:t0 + DB * CH],
              sem=s_in)
        p.dma("sync", bcolb[bb][:], io["bh_col"][:, b * DB:(b + 1) * DB],
              sem=s_in)

    S_IN0 = s_in.val
    load_batch(0)

    W = DB * CH
    for b in range(NB):
        bb = b % 2
        if b + 1 < NB:
            p.wait("sync", s_t)
            p.wait("sync", s_v)
            load_batch(b + 1)
        p.wait("vector", s_in, S_IN0 + (b + 1) * 8 * 16)
        p.wait("tensor", s_in, S_IN0 + (b + 1) * 8 * 16)
        RQ, RK, RKB = 0, W, 2 * W
        for (dst, srct) in ((RQ, qfm[bb]), (RK, kfm[bb])):
            p.opw("vector", s_v, "tensor_mul", sqt[:], srct[0][:], srct[0][:])
            p.op("vector", "tensor_mul", sqt2[:], srct[1][:], srct[1][:],
                 sem=s_v)
            p.wait("tensor", s_v)
            for (o, n) in nblocks(W):
                ps, pi = pp.get()
                p.op("tensor", "matmul", ps[0:1, 0:n], onescol[:, 0:1],
                     sqt[:, o:o + n], start=True, stop=False)
                p.op("tensor", "matmul", ps[0:1, 0:n], onescol[:, 0:1],
                     sqt2[:, o:o + n], start=False, stop=True, sem=s_t)
                p.wait("scalar", s_t)
                p.op("scalar", "activation", rows[0:1, dst + o:dst + o + n],
                     ps[0:1, 0:n], AF.Sqrt, bias=eps6[0:1, 0:1], sem=s_sc)
                pp.done(pi, s_sc)
            p.wait("vector", s_t)
        p.wait("vector", s_sc)
        p.opw("vector", s_v, "reciprocal", rows[0:1, RQ:RQ + W],
              rows[0:1, RQ:RQ + W])
        p.opw("vector", s_v, "reciprocal", rows[0:1, RK:RK + W],
              rows[0:1, RK:RK + W])
        p.op("vector", "tensor_mul", rows[0:1, RKB:RKB + W],
             rows[0:1, RK:RK + W], browb[bb][:], sem=s_v)
        ROWS_D = s_v.val
        p.opw("vector", s_v, "tensor_mul", sqtm[:], ktmb[bb][:], ktmb[bb][:])
        p.opw("vector", s_v, "tensor_reduce",
              rkcol.rearrange("p (g o) -> p g o", o=1),
              sqtm.rearrange("p (g d) -> p g d", d=256), AX.X, ALU.add)
        p.wait("scalar", s_v)
        p.op("scalar", "activation", rkcol[:], rkcol[:], AF.Sqrt,
             bias=eps6[:, 0:1], sem=s_sc)
        p.wait("vector", s_sc)
        p.opw("vector", s_v, "reciprocal", rkcol[:], rkcol[:])
        p.op("vector", "tensor_mul", skbcol[:], rkcol[:], bcolb[bb][:],
             sem=s_v)
        COLS_D = s_v.val
        p.wait("tensor", s_v, ROWS_D)
        for (dst, srcrow, srctiles) in ((qn, RQ, qfm[bb]), (kn, RK, kfm[bb]),
                                        (kb, RKB, kfm[bb])):
            for (o, n) in nblocks(W):
                ps, pi = pp.get()
                p.op("tensor", "matmul", ps[:, 0:n], onesrf[:],
                     rows[0:1, srcrow + o:srcrow + o + n], start=True,
                     stop=True, sem=s_t)
                p.wait("vector", s_t)
                p.op("vector", "tensor_copy", rep[:, o:o + n], ps[:, 0:n],
                     sem=s_v)
                pp.done(pi, s_v)
            p.wait("vector", s_v)
            for i in range(2):
                p.op("vector", "tensor_mul", dst[i][:], srctiles[i][:],
                     rep[:], sem=s_v)
            p.wait("vector", s_v)
        NORM_D = s_v.val
        if b == 0 and "dbg_rows" in io:
            p.wait("sync", s_v)
            p.dma("sync", io["dbg_rows"], rows[0:1, 0:3 * W], sem=s_out)
            p.dma("sync", io["dbg_rkcol"], rkcol[:], sem=s_out)
            p.dma("sync", io["dbg_qn"], qn[0][:, 0:128], sem=s_out)
            p.dma("sync", io["dbg_kb"], kb[0][:, 0:128], sem=s_out)

        p.wait("tensor", s_v, NORM_D)
        p.wait("vector", s_v, NORM_D)

        def sl2(t, j):
            return t[:, 128 * j:128 * j + 128]

        def xpair(tpair):
            ccs = [4 * tpair + jj for jj in range(4)]
            p.wait("vector", s_t)
            psAT, piAT = pp.get()
            psA, piA = pp.get()
            psat, piat = pp.get()
            for j, cc in enumerate(ccs):
                co = cc * CH
                q0 = [qn[0][:, co:co + CH], qn[1][:, co:co + CH]]
                k0 = [kn[0][:, co:co + CH], kn[1][:, co:co + CH]]
                kb0 = [kb[0][:, co:co + CH], kb[1][:, co:co + CH]]
                p.op("tensor", "matmul", sl2(psAT, j), k0[0], kb0[0],
                     start=True, stop=False)
                p.op("tensor", "matmul", sl2(psAT, j), k0[1], kb0[1],
                     start=False, stop=True)
                p.op("tensor", "matmul", sl2(psA, j), kb0[0], k0[0],
                     start=True, stop=False)
                p.op("tensor", "matmul", sl2(psA, j), kb0[1], k0[1],
                     start=False, stop=True)
                p.op("tensor", "matmul", sl2(psat, j), k0[0], q0[0],
                     start=True, stop=False)
                p.op("tensor", "matmul", sl2(psat, j), k0[1], q0[1],
                     start=False, stop=True,
                     sem=s_t if j == 3 else None)
            p.wait("vector", s_t)

            def b2(mask):
                return mask[:].rearrange(
                    "p (o d) -> p o d", o=1).broadcast_to([128, 4, 128])

            def w2v(t):
                return t[:].rearrange("p (o d) -> p o d", d=128)

            p.opw("vector", s_v, "tensor_tensor", w2v(MTt[0]),
                  psAT[:, 0:512].rearrange("p (o d) -> p o d", d=128),
                  b2(mnu1), op=ALU.mult)
            pp.done(piAT, s_v)
            p.opw("vector", s_v, "tensor_tensor", w2v(Mt[0]),
                  psA[:, 0:512].rearrange("p (o d) -> p o d", d=128),
                  b2(mnl1), op=ALU.mult)
            pp.done(piA, s_v)
            p.opw("vector", s_v, "tensor_tensor", w2v(Qt[0]),
                  w2v(MTt[0]), b2(identbf), op=ALU.add)
            p.op("vector", "tensor_tensor", w2v(attnT),
                 psat[:, 0:512].rearrange("p (o d) -> p o d", d=128),
                 b2(mu0), op=ALU.mult, sem=s_v)
            pp.done(piat, s_v)
            cur = 0
            for lvl in range(6):
                nxt = 1 - cur
                p.wait("tensor", s_v)
                psm, pim = pp.get()
                psmt, pimt = pp.get()
                for j in range(4):
                    p.op("tensor", "matmul", sl2(psm, j), sl2(MTt[cur], j),
                         sl2(Mt[cur], j), start=True, stop=True)
                    p.op("tensor", "matmul", sl2(psmt, j), sl2(Mt[cur], j),
                         sl2(MTt[cur], j), start=True, stop=True,
                         sem=s_t if j == 3 else None)
                p.wait("vector", s_t)
                p.op("vector", "tensor_copy", Mt[nxt][:], psm[:, 0:512],
                     sem=s_v)
                pp.done(pim, s_v)
                p.op("vector", "tensor_copy", MTt[nxt][:], psmt[:, 0:512],
                     sem=s_v)
                pp.done(pimt, s_v)
                p.wait("tensor", s_v)
                psq, piq = pp.get()
                for j in range(4):
                    p.op("tensor", "matmul", sl2(psq, j), sl2(Mt[nxt], j),
                         sl2(Qt[cur], j), start=True, stop=False)
                    p.op("tensor", "matmul", sl2(psq, j), identbf[:],
                         sl2(Qt[cur], j), start=False, stop=True,
                         sem=s_t if j == 3 else None)
                p.wait("vector", s_t)
                p.op("vector", "tensor_copy", Qt[nxt][:], psq[:, 0:512],
                     sem=s_v)
                pp.done(piq, s_v)
                cur = nxt
            p.wait("vector", s_v)
            for j, cc in enumerate(ccs):
                p.op("vector", "tensor_scalar_mul", sl2(TTs, j),
                     sl2(Qt[cur], j), skbcol[:, cc:cc + 1], sem=s_v)
                p.op("vector", "tensor_scalar_mul", sl2(vbt, j),
                     vtmb[bb][:, cc * 128:cc * 128 + 128],
                     bcolb[bb][:, cc:cc + 1], sem=s_v)
            p.wait("vector", s_v)
            if cur != 0:
                p.opw("vector", s_v, "tensor_copy", Qt[0][:], Qt[cur][:])
            p.wait("tensor", s_v)
            psw, piw = pp.get()
            psw2, piw2 = pp.get()
            for j, cc in enumerate(ccs):
                ktm_c = [ktmb[bb][:, cc * 256:cc * 256 + 128],
                         ktmb[bb][:, cc * 256 + 128:cc * 256 + 256]]
                p.op("tensor", "matmul", sl2(psw, j), ktm_c[0], sl2(TTs, j),
                     start=True, stop=True)
                p.op("tensor", "matmul", sl2(psw2, j), ktm_c[1], sl2(TTs, j),
                     start=True, stop=True, sem=s_t if j == 3 else None)
            p.wait("vector", s_t)
            p.op("vector", "tensor_scalar_mul", nwf[0][:], psw[:, 0:512],
                 -1.0, sem=s_v)
            pp.done(piw, s_v)
            p.op("vector", "tensor_scalar_mul", nwf[1][:], psw2[:, 0:512],
                 -1.0, sem=s_v)
            pp.done(piw2, s_v)

        def ypart(cc):
            j = cc % 4
            c = b * DB + cc
            co = cc * CH
            q0 = [qn[0][:, co:co + CH], qn[1][:, co:co + CH]]
            ktm_c = [ktmb[bb][:, cc * 256:cc * 256 + 128],
                     ktmb[bb][:, cc * 256 + 128:cc * 256 + 256]]
            p.wait("tensor", s_v)
            psu, piu = pp.get()
            p.op("tensor", "matmul", psu[:, 0:128], sl2(Qt[0], j),
                 sl2(vbt, j), start=True, stop=(c == 0),
                 sem=s_t if c == 0 else None)
            if c > 0:
                p.op("tensor", "matmul", psu[:, 0:128], sl2(nwf[0], j),
                     S0t[c % 2][:], start=False, stop=False)
                p.op("tensor", "matmul", psu[:, 0:128], sl2(nwf[1], j),
                     S1t[c % 2][:], start=False, stop=True, sem=s_t)
            p.wait("vector", s_t)
            p.op("vector", "tensor_copy", upt[:], psu[:, 0:128], sem=s_v)
            p.op("vector", "tensor_scalar_mul", upst[:], psu[:, 0:128],
                 rkcol[:, cc:cc + 1], sem=s_v)
            pp.done(piu, s_v)
            p.wait("tensor", s_v)
            pso, pio = pp.get()
            if c > 0:
                p.op("tensor", "matmul", pso[:, 0:128], q0[0], S0t[c % 2][:],
                     start=True, stop=False)
                p.op("tensor", "matmul", pso[:, 0:128], q0[1], S1t[c % 2][:],
                     start=False, stop=False)
                p.op("tensor", "matmul", pso[:, 0:128], sl2(attnT, j),
                     upt[:], start=False, stop=True, sem=s_t)
            else:
                p.op("tensor", "matmul", pso[:, 0:128], sl2(attnT, j),
                     upt[:], start=True, stop=True, sem=s_t)
            p.wait("vector", s_t)
            p.op("vector", "tensor_copy", obat[:, cc * 128:cc * 128 + 128],
                 pso[:, 0:128], sem=s_v)
            pp.done(pio, s_v)
            p.op("tensor", "matmul", S0p[:, 0:128], ktm_c[0], upst[:],
                 start=(c == 0), stop=(c == NCHUNK - 1))
            p.op("tensor", "matmul", S1p[:, 0:128], ktm_c[1], upst[:],
                 start=(c == 0), stop=(c == NCHUNK - 1), sem=s_t)
            p.wait("scalar", s_t)
            nxtp = (c + 1) % 2
            p.op("scalar", "copy", S0t[nxtp][:], S0p[:, 0:128], sem=s_sc)
            p.op("scalar", "copy", S1t[nxtp][:], S1p[:, 0:128], sem=s_sc)
            p.wait("tensor", s_sc)

        for tpair in range(DB // 4):
            xpair(tpair)
            for jj in range(4):
                ypart(4 * tpair + jj)

        p.wait("sync", s_v)
        if "do_tm" in io:
            p.dma("sync",
                  io["do_tm"][b * DB * CH:(b + 1) * DB * CH, :].rearrange(
                      "(c p) d -> p c d", p=128),
                  obat[:].rearrange("p (c d) -> p c d", d=128), sem=s_out)
        p.wait("tensor", s_v)
        if b > 0:
            p.wait("vector", s_of, s_of.val - 16)
        for cg in range(DB // 4):
            ps, pi = ppt.get()
            for j in range(4):
                cc = 4 * cg + j
                p.op("tensor", "transpose", ps[:, 128 * j:128 * j + 128],
                     obat[:, cc * 128:cc * 128 + 128], identbf[:],
                     sem=s_t if j == 3 else None)
            p.wait("vector", s_t)
            p.op("vector", "tensor_copy", ofm[:, 512 * cg:512 * cg + 512],
                 ps[:, 0:512], sem=s_v)
            ppt.done(pi, s_v)
        p.wait("sync", s_v)
        p.dma("sync", io["do_fm"][:, b * DB * CH:(b + 1) * DB * CH], ofm[:],
              sem=s_of)
        p.wait("vector", s_out, s_out.val - 16 * 1)

    p.wait("sync", s_out)
    p.wait("sync", s_of)


STAGE_D_INS = {
    "qh_fm": ([256, L], BF16), "kh_fm": ([256, L], BF16),
    "kh_tm": ([L, 256], BF16), "vh_tm": ([L, 128], BF16),
    "bh_col": ([128, NCHUNK], F32), "bh_row": ([1, L], F32),
    "ident": ([128, 128], BF16), "ones2": ([128, 2], BF16),
    "onesrf": ([1, 128], F32),
    "mnu1": ([128, 128], F32), "mnl1": ([128, 128], F32),
    "mu0": ([128, 128], F32),
}
STAGE_D_OUTS = {"do_tm": ([L, 128], BF16), "do_fm": ([128, L], BF16),
                "dbg_rows": ([1, 3 * DB * CH], F32),
                "dbg_rkcol": ([128, DB], F32),
                "dbg_qn": ([128, 128], BF16), "dbg_kb": ([128, 128], BF16),
                "dbg_Q": ([128, 128], BF16), "dbg_AT": ([128, 128], BF16),
                "dbg_up": ([128, 128], BF16),
                "dbg_AT0": ([128, 128], BF16), "dbg_A0": ([128, 128], BF16),
                "dbg_Q0": ([128, 128], BF16)}


def stage_d_inputs(aouts, c):
    h, half = c >> 1, c & 1
    q_fm = np.concatenate([aouts["cq"][s][256 * h:256 * h + 256, :]
                           for s in range(NCORE)], axis=1)
    k_fm = np.concatenate([aouts["ck"][s][256 * h:256 * h + 256, :]
                           for s in range(NCORE)], axis=1)
    k_tm = np.concatenate([aouts["ckt"][s][:, 256 * h:256 * h + 256]], axis=0) \
        if False else np.concatenate(
            [aouts["ckt"][s][:, 256 * h:256 * h + 256] for s in range(NCORE)],
            axis=0)
    v_tm = np.concatenate(
        [aouts["cvt"][s][:, 256 * h + 128 * half:256 * h + 128 * half + 128]
         for s in range(NCORE)], axis=0)
    brow = np.concatenate([aouts["cb"][s][h:h + 1, :] for s in range(NCORE)],
                          axis=1).astype(np.float32)
    bcol = np.ascontiguousarray(brow.reshape(NCHUNK, CH).T)
    iu = np.triu(np.ones((128, 128), np.float32), 1)
    il = np.tril(np.ones((128, 128), np.float32), -1)
    return {
        "qh_fm": np.ascontiguousarray(q_fm), "kh_fm": np.ascontiguousarray(k_fm),
        "kh_tm": np.ascontiguousarray(k_tm), "vh_tm": np.ascontiguousarray(v_tm),
        "bh_col": bcol, "bh_row": np.ascontiguousarray(brow),
        "ident": np.eye(128, dtype=BF), "ones2": np.ones((128, 2), BF),
        "onesrf": np.ones((1, 128), np.float32),
        "mnu1": -iu, "mnl1": -il,
        "mu0": np.triu(np.ones((128, 128), np.float32)),
    }


def build_stage_d():
    nc = bass.Bass("TRN2", target_bir_lowering=False)
    io = {}
    for k, (shape, dt) in STAGE_D_INS.items():
        io[k] = nc.dram_tensor(k, shape, dt, kind="ExternalInput").ap()
    for k, (shape, dt) in STAGE_D_OUTS.items():
        io[k] = nc.dram_tensor(k, shape, dt, kind="ExternalOutput").ap()
    p = Prog(nc)
    stage_delta(p, io)
    p.emit_and_finish()
    return nc


QB = 1024
QS = 512


def stage_attn(p, io, qblocks):
    khead = [p.sbuf(f"T_kh{i}", [128, L], BF16) for i in range(2)]
    vhead = [p.sbuf(f"T_vh{i}", [128, L], BF16) for i in range(2)]
    qt = [p.sbuf(f"T_q{i}", [128, QB], BF16) for i in range(8)]
    pbuf = p.sbuf("T_pbuf", [128, 1024], BF16)
    mbuf = [p.sbuf(f"T_mbuf{i}", [128, 512], BF16) for i in range(2)]
    attnfin = [p.sbuf(f"T_af{i}", [128, QB], BF16) for i in range(8)]
    woat = [p.sbuf(f"T_wo{i}", [128, 1024], BF16) for i in range(8)]
    masks = [p.sbuf(f"T_mask{i}", [128, 512], BF16) for i in range(4)]
    denrep = p.sbuf("T_denrep", [128, 512], BF16)
    denrow = p.sbuf("T_denrow", [1, 512], F32)
    gout = p.sbuf("T_gout", [128, 512], BF16)
    onesrf = p.sbuf("T_onesrf", [1, 128], F32)
    onescol = p.sbuf("T_ones", [128, 2], BF16)
    mrow = p.sbuf("T_mrow", [1, 16], F32)
    mcol = p.sbuf("T_mcol", [128, 1], F32)

    pssc = [p.psum(f"T_sc{i}", [128, 1024]) for i in range(2)]
    pso = [p.psum(f"T_o{i}", [128, 512]) for i in range(2)]
    psd = [p.psum(f"T_d{i}", [1, 512]) for i in range(2)]

    s_in = p.sem("tin"); s_w = p.sem("tw"); s_c = p.sem("tc")
    s_v = p.sem("tv"); s_t = p.sem("tt"); s_sc = p.sem("tsc")
    s_out = p.sem("tout")

    for i in range(4):
        p.dma("sync", masks[i][:], io[f"mask{i}"], sem=s_c)
    p.dma("sync", onesrf[:], io["onesrf"], sem=s_c)
    p.dma("sync", onescol[:], io["ones2"], sem=s_c)
    p.dma("sync", mrow[0:1, 0:8], io["qmax8"], sem=s_c)
    p.dma("sync", mrow[0:1, 8:16], io["kmax8"], sem=s_c)
    for i in range(8):
        p.dma("gpsimd", woat[i][:], io["w_outT_attn"][128 * i:128 * i + 128, :],
              sem=s_w)

    p.wait("vector", s_c)
    p.opw("vector", s_v, "reduce_max", mrow[0:1, 0:1], mrow[0:1, 0:8],
          axis=AX.X)
    p.opw("vector", s_v, "reduce_max", mrow[0:1, 1:2], mrow[0:1, 8:16],
          axis=AX.X)
    p.wait("scalar", s_v)
    p.op("scalar", "activation", mrow[0:1, 2:4], mrow[0:1, 0:2], AF.Sqrt,
         sem=s_sc)
    p.wait("vector", s_sc)
    p.opw("vector", s_v, "tensor_mul", mrow[0:1, 4:5], mrow[0:1, 2:3],
          mrow[0:1, 3:4])
    p.opw("vector", s_v, "tensor_scalar_mul", mrow[0:1, 5:6], mrow[0:1, 4:5],
          -1.0)
    p.wait("tensor", s_v)
    ps, _ = (pso[0], 0)
    p.op("tensor", "matmul", ps[:, 0:1], onesrf[:], mrow[0:1, 5:6],
         start=True, stop=True, sem=s_t)
    p.wait("vector", s_t)
    p.op("vector", "tensor_copy", mcol[:], ps[:, 0:1], sem=s_v)
    p.wait("scalar", s_v)
    p.wait("tensor", s_v)

    def load_head(qb_i, h, buf):
        span = qblocks[qb_i] * QB + QB
        p.dma("sync", khead[buf][:, 0:span],
              io["ak"][128 * h:128 * h + 128, 0:span], sem=s_in)
        p.dma("sync", vhead[buf][:, 0:span].rearrange("p (t d) -> p t d", d=128),
              io["av"][0:span, 128 * h:128 * h + 128].rearrange(
                  "(t p) d -> p t d", p=128), sem=s_in)

    load_head(0, 0, 0)
    nload = 1
    first_mm = True
    for qb_i, qb in enumerate(qblocks):
        for i in range(8):
            p.dma("sync", qt[i][:], io["aq"][128 * i:128 * i + 128,
                                             qb_i * QB:(qb_i + 1) * QB],
                  sem=s_in)
        span = qb * QB + QB
        for h in range(8):
            buf = (qb_i * 8 + h) % 2
            nh = qb_i * 8 + h + 1
            if nh < 16:
                p.wait("sync", s_t)
                load_head(nh // 8, nh % 8, nh % 2)
                nload += 1
            p.wait("tensor", s_in, s_in.val)
            for qs in range(2):
                qs0 = qb * QB + qs * QS
                kt_hi = (qs0 + QS) // 128
                o_ps = pso[qs]
                d_ps = psd[qs]
                ngroups = (kt_hi + 1) // 2
                first = True
                for g in range(ngroups):
                    sc = pssc[g % 2]
                    kts = [kt for kt in (2 * g, 2 * g + 1) if kt < kt_hi]
                    for gi, kt in enumerate(kts):
                        p.op("tensor", "matmul", sc[:, gi * 512:gi * 512 + 512],
                             khead[buf][:, kt * 128:kt * 128 + 128],
                             qt[h][:, qs * QS:qs * QS + QS],
                             start=True, stop=True,
                             sem=s_t if gi == len(kts) - 1 else None)
                    p.wait("scalar", s_t)
                    width = len(kts) * 512
                    p.op("scalar", "activation", pbuf[:, 0:width],
                         sc[:, 0:width], AF.Exp, bias=mcol[:, 0:1], sem=s_sc)
                    av_src = []
                    for gi, kt in enumerate(kts):
                        off = kt * 128 - qs0
                        if off >= 0:
                            mi = off // 128
                            p.wait("vector", s_sc)
                            p.op("vector", "tensor_mul",
                                 mbuf[gi][:], pbuf[:, gi * 512:gi * 512 + 512],
                                 masks[mi][:], sem=s_v)
                            av_src.append(mbuf[gi][:, 0:512])
                        else:
                            av_src.append(pbuf[:, gi * 512:gi * 512 + 512])
                    p.wait("tensor", s_sc)
                    p.wait("tensor", s_v)
                    for gi, kt in enumerate(kts):
                        p.op("tensor", "matmul", o_ps[:, 0:512],
                             vhead[buf][:, kt * 128:kt * 128 + 128],
                             av_src[gi], start=first, stop=False)
                        p.op("tensor", "matmul", d_ps[0:1, 0:512],
                             onescol[:, 0:1], av_src[gi],
                             start=first, stop=False,
                             sem=s_t)
                        first = False
                p.wait("vector", s_t)
                p.opw("vector", s_v, "reciprocal", denrow[:], d_ps[0:1, 0:512])
                p.wait("tensor", s_v)
                p.op("tensor", "matmul", pssc[0][:, 0:512], onesrf[:],
                     denrow[:], start=True, stop=True, sem=s_t)
                p.wait("vector", s_t)
                p.opw("vector", s_v, "tensor_copy", denrep[:],
                      pssc[0][:, 0:512])
                p.op("vector", "tensor_mul",
                     attnfin[h][:, qs * QS:qs * QS + QS], o_ps[:, 0:512],
                     denrep[:], sem=s_v)
                p.wait("tensor", s_v)
        p.wait("tensor", s_w)
        for m in range(8):
            for qh in range(2):
                sc = pssc[qh % 2]
                for i in range(8):
                    p.op("tensor", "matmul", sc[:, 0:512],
                         woat[i][:, 128 * m:128 * m + 128],
                         attnfin[i][:, qh * 512:qh * 512 + 512],
                         start=(i == 0), stop=(i == 7),
                         sem=s_sc if False else (s_t if i == 7 else None))
                p.wait("scalar", s_t)
                p.op("scalar", "copy", gout[:], sc[:, 0:512], sem=s_sc)
                p.wait("sync", s_sc)
                p.dma("sync", io["ga"][128 * m:128 * m + 128,
                                       qb_i * QB + qh * 512:
                                       qb_i * QB + qh * 512 + 512],
                      gout[:], sem=s_out)
                p.wait("scalar", s_out)
                p.wait("tensor", s_sc)
    p.wait("sync", s_out)


STAGE_T_INS = {
    "aq": ([DM, 2 * QB], BF16), "ak": ([DM, L], BF16), "av": ([L, DM], BF16),
    "w_outT_attn": ([DM, DM], F32),
    "qmax8": ([1, 8], F32), "kmax8": ([1, 8], F32),
    "onesrf": ([1, 128], F32), "ones2": ([128, 2], BF16),
    "mask0": ([128, 512], BF16), "mask1": ([128, 512], BF16),
    "mask2": ([128, 512], BF16), "mask3": ([128, 512], BF16),
}
STAGE_T_OUTS = {"ga": ([DM, 2 * QB], BF16)}


def stage_t_inputs(aouts, c):
    qblocks = [c, 15 - c]
    aq = np.concatenate(
        [aouts["caq"][b >> 1][:, (b & 1) * QB:(b & 1) * QB + QB]
         for b in qblocks], axis=1)
    ak = np.concatenate([aouts["cak"][s] for s in range(NCORE)], axis=1)
    av = np.concatenate([aouts["cav"][s] for s in range(NCORE)], axis=0)
    qmax8 = np.concatenate([aouts["qkm"][s][:, 0:1] for s in range(NCORE)],
                           axis=1).astype(np.float32)
    kmax8 = np.concatenate([aouts["qkm"][s][:, 1:2] for s in range(NCORE)],
                           axis=1).astype(np.float32)
    d = {
        "aq": np.ascontiguousarray(aq), "ak": np.ascontiguousarray(ak),
        "av": np.ascontiguousarray(av),
        "w_outT_attn": None,
        "qmax8": qmax8, "kmax8": kmax8,
        "onesrf": np.ones((1, 128), np.float32),
        "ones2": np.ones((128, 2), BF),
    }
    for off in range(4):
        j = np.arange(128)[:, None]
        i = np.arange(512)[None, :]
        d[f"mask{off}"] = (j + off * 128 <= i).astype(BF)
    return d


def build_stage_t(qblocks):
    nc = bass.Bass("TRN2", target_bir_lowering=False)
    io = {}
    for k, (shape, dt) in STAGE_T_INS.items():
        io[k] = nc.dram_tensor(k, shape, dt, kind="ExternalInput").ap()
    for k, (shape, dt) in STAGE_T_OUTS.items():
        io[k] = nc.dram_tensor(k, shape, dt, kind="ExternalOutput").ap()
    p = Prog(nc)
    stage_attn(p, io, qblocks)
    p.emit_and_finish()
    return nc




def stage_m(p, io, stop_after=None, hook_a2=None):
    def marker(words):
        pass

    s_w = p.sem("mw"); s_c = p.sem("mc"); s_v = p.sem("mv")
    s_t = p.sem("mt"); s_sc = p.sem("msc"); s_out = p.sem("mout")
    s_in = p.sem("min"); s_b = p.sem("mb")

    onescol = p.sbuf("M_ones", [128, 2], BF16)
    onesrf = p.sbuf("M_onesrf", [1, 128], F32)
    onesrb = p.sbuf("M_onesrb", [1, 128], BF16)
    identbf = p.sbuf("M_id", [128, 128], BF16)
    ones128 = p.sbuf("M_o128", [128, 128], BF16)
    diag = [p.sbuf(f"M_diag{i}", [128, 128], BF16) for i in range(2)]
    gsil = [p.sbuf(f"M_gs{i}", [128, TPC], BF16) for i in range(8)]
    wstc = [p.sbuf(f"M_wst{i}", [4, 512], BF16) for i in range(3)]
    gb1c = [p.sbuf(f"M_gb1{i}", [128, 1], F32) for i in range(4)]
    gw2t = [p.sbuf(f"M_gw2{i}", [128, 4], BF16) for i in range(4)]
    gb2r = p.sbuf("M_gb2", [1, 4], BF16)
    prow = p.sbuf("M_prow", [1, 16], F32)
    epsfr = p.sbuf("M_epsf", [1, 16], F32)
    invtc = [p.sbuf(f"M_it{h}", [128, 1], F32) for h in range(4)]
    flrep = [p.sbuf(f"M_flr{h}", [128, 4], F32) for h in range(4)]
    onrm = [p.sbuf(f"M_onrm{i}", [128, 1], F32) for i in range(2)]
    emt = [p.sbuf(f"M_em{i}", [128, 4], BF16) for i in range(4)]

    pp = PP(p, "M_pp", 2, [128, 512])
    pp2 = PP(p, "M_pp2", 2, [128, 1024])
    ppt = PP(p, "M_ppt", 2, [128, 512], BF16)

    p.dma("sync", onescol[:], io["ones2"], sem=s_c)
    p.dma("sync", onesrf[:], io["onesrf"], sem=s_c)
    p.dma("sync", onesrb[:], io["onesrb"], sem=s_c)
    p.dma("sync", identbf[:], io["ident"], sem=s_c)
    p.dma("sync", ones128[:], io["ones128"], sem=s_c)
    for i in range(4):
        p.dma("sync", emt[i][:], io["emat"][128 * i:128 * i + 128, :], sem=s_c)
        p.dma("sync", gb1c[i][:], io["gb1"][128 * i:128 * i + 128, :], sem=s_c)
        p.dma("gpsimd", gw2t[i][:], io["gW2"][128 * i:128 * i + 128, :],
              sem=s_c)
    p.dma("gpsimd", gb2r[:], io["gb2"], sem=s_c)
    p.dma("sync", prow[0:1, 0:4], io["temp4"], sem=s_c)
    p.dma("sync", epsfr[:], io["epsf16"], sem=s_c)
    p.dma("sync", onrm[0][:], io["onorm"][0:128, :], sem=s_c)
    p.dma("sync", onrm[1][:], io["onorm"][128:256, :], sem=s_c)
    p.wait("vector", s_c)
    p.wait("tensor", s_c)
    eps5 = p.constcol(1e-5, s_v)
    p.opw("vector", s_v, "tensor_scalar_max", prow[0:1, 4:8], prow[0:1, 0:4],
          0.2)
    p.opw("vector", s_v, "tensor_scalar_min", prow[0:1, 4:8], prow[0:1, 4:8],
          10.0)
    p.opw("vector", s_v, "reciprocal", prow[0:1, 4:8], prow[0:1, 4:8])
    p.opw("vector", s_v, "tensor_scalar_max", epsfr[:], epsfr[:], 1e-7)
    p.opw("vector", s_v, "tensor_scalar_min", epsfr[:], epsfr[:], 0.1)
    p.wait("tensor", s_v)
    for h in range(4):
        ps, pi = pp.get()
        p.op("tensor", "matmul", ps[:, 0:1], onesrf[:], prow[0:1, 4 + h:5 + h],
             start=True, stop=True, sem=s_t)
        ps2, pi2 = pp.get()
        p.op("tensor", "matmul", ps2[:, 0:4], onesrf[:],
             epsfr[0:1, 4 * h:4 * h + 4], start=True, stop=True, sem=s_t)
        p.wait("vector", s_t)
        p.op("vector", "tensor_copy", invtc[h][:], ps[:, 0:1], sem=s_v)
        pp.done(pi, s_v)
        p.op("vector", "tensor_copy", flrep[h][:], ps2[:, 0:4], sem=s_v)
        pp.done(pi2, s_v)

    p.scope_open()
    gst = [p.sbuf(f"M_gst{i}", [128, 4], BF16) for i in range(24)]
    w1sb = [p.sbuf(f"M_w1s{i}", [128, 512], BF16) for i in range(2)]
    for i in range(24):
        p.dma("gpsimd", gst[i][:], io["gstat"][128 * i:128 * i + 128, :],
              sem=s_w)
    p.wait("gpsimd", s_w)
    for s in range(3):
        ps, pi = pp.get()
        for jj in range(8):
            br = jj // 2
            ch = jj % 2
            gi = (br * 3 + s) * 2 + ch
            p.dma("gpsimd", w1sb[jj % 2][:],
                  io["gW1"][1024 + gi * 128:1024 + gi * 128 + 128, :], sem=s_w)
            p.wait("tensor", s_w)
            p.op("tensor", "matmul", ps[0:4, 0:512], gst[gi][:],
                 w1sb[jj % 2][:], start=(jj == 0), stop=(jj == 7), sem=s_t)
            if jj < 7:
                p.wait("gpsimd", s_t, s_t.val - 1)
        p.wait("vector", s_t)
        p.op("vector", "tensor_copy", wstc[s][:], ps[0:4, 0:512], sem=s_v)
        pp.done(pi, s_v)
    p.scope_close(s_b)

    p.scope_open()
    qf = [p.sbuf(f"M_qf{i}", [128, TPC], BF16) for i in range(8)]
    qf_addr = [p.nc.lookup_mloc(t).addr for t in qf]
    ffacc = [p.sbuf(f"M_ff{i}", [128, TPC], F32) for i in range(8)]
    kfw = [p.sbuf(f"M_kfw{i}", [128, 8 * 256], BF16) for i in range(2)]
    vfw = [p.sbuf(f"M_vfw{i}", [128, 2 * 8 * 128], BF16) for i in range(2)]
    grep = p.sbuf("M_grep", [128, TPC], BF16)
    gt4 = p.sbuf("M_gt4", [128, TPC // 128 * 16], F32)
    abuf = [p.sbuf(f"M_abuf{i}", [128, 1024], BF16) for i in range(2)]
    abuf2 = p.sbuf("M_abuf2", [128, 1024], BF16)
    for i in range(8):
        p.dma("sync", qf[i][:], io["qf"][128 * i:128 * i + 128, :], sem=s_in)
        p.op("vector", "memset", ffacc[i][:], 0.0, sem=s_v)
    p.dma("sync", gt4[:].rearrange("p (l g) -> p l g", g=16),
          io["g16"].rearrange("(l p) g -> p l g", p=128), sem=s_in)
    FF_IN = s_in.val
    ACC_Z = s_v.val

    def load_e(e):
        p.dma("gpsimd", kfw[e % 2][:].rearrange("p (h s) -> p h s", s=256),
              io["kfwT"].rearrange("(h p) s -> p h s", p=128)
              [:, 8 * e:8 * e + 8, :], sem=s_w)
        p.dma("gpsimd", vfw[e % 2][:].rearrange("p (c d) -> p c d", d=128),
              io["vfw"].rearrange("(c p) d -> p c d", p=128)
              [:, 16 * e:16 * e + 16, :], sem=s_w)

    load_e(0)
    p.wait("tensor", s_in, FF_IN)
    p.wait("vector", s_in, FF_IN)
    p.wait("vector", s_v, ACC_Z)
    dgb = p.sbuf("M_dgb", [128, 1024], BF16)
    for e in range(16):
        if e + 1 < 16:
            p.wait("gpsimd", s_t)
            load_e(e + 1)
        p.wait("tensor", s_w, 16 * 2 * (e + 1))
        for lt in range(TPC // 128):
            p.opw("vector", s_v, "tensor_scalar_mul", diag[lt % 2][:],
                  identbf[:], gt4[:, 16 * lt + e:16 * lt + e + 1])
            p.wait("tensor", s_v)
            ps, pi = pp.get()
            p.op("tensor", "matmul", ps[:, 0:128], ones128[:], diag[lt % 2][:],
                 start=True, stop=True, sem=s_t)
            p.wait("vector", s_t)
            p.op("vector", "tensor_copy", grep[:, 128 * lt:128 * lt + 128],
                 ps[:, 0:128], sem=s_v)
            pp.done(pi, s_v)
        p.wait("tensor", s_v)
        items = [(o, n, h) for (o, n) in nblocks(TPC) for h in range(8)]

        def kstep(idx):
            o, n, h = items[idx]
            ps2, pi2 = pp2.get()
            for ch in range(2):
                p.op("tensor", "matmul", ps2[:, ch * 512:ch * 512 + n],
                     kfw[e % 2][:, h * 256 + 128 * ch:
                                h * 256 + 128 * ch + 128],
                     qf[h][:, o:o + n], start=True, stop=True,
                     sem=s_t if ch == 1 else None)
            return ps2, pi2, s_t.val

        kcur = kstep(0)
        vvrec = []
        for idx, (o, n, h) in enumerate(items):
            ps2, pi2, ktv = kcur
            if idx + 1 < len(items):
                kcur = kstep(idx + 1)
            p.wait("scalar", s_t, ktv)
            if idx >= 2:
                p.wait("scalar", s_v, vvrec[idx - 2])
            ab, ab2 = abuf[idx % 2], abuf2
            p.op("scalar", "activation", ab[:, 0:1024], ps2[:, 0:1024],
                 AF.Gelu, sem=s_sc)
            pp2.done(pi2, s_sc)
            p.wait("vector", s_sc)
            for ch in range(2):
                p.op("vector", "tensor_mul",
                     ab2[:, ch * 512:ch * 512 + n],
                     ab[:, ch * 512:ch * 512 + n], grep[:, o:o + n],
                     sem=s_v)
            vvrec.append(s_v.val)
            p.wait("tensor", s_v, vvrec[-1])
            pso_, pio_ = pp.get()
            for ch in range(2):
                p.op("tensor", "matmul", pso_[:, 0:n],
                     vfw[e % 2][:, (2 * h + ch) * 128:
                                (2 * h + ch) * 128 + 128],
                     ab2[:, ch * 512:ch * 512 + n],
                     start=(ch == 0), stop=(ch == 1), sem=s_t)
            p.wait("vector", s_t)
            p.op("vector", "tensor_add", ffacc[h][:, o:o + n],
                 ffacc[h][:, o:o + n], pso_[:, 0:n], sem=s_v)
            pp.done(pio_, s_v)
    p.wait("vector", s_v)
    for i in range(8):
        p.op("vector", "tensor_copy", qf[i][:], ffacc[i][:], sem=s_v)
    p.wait("vector", s_v)
    p.scope_close(s_b)
    if stop_after == "A":
        return
    if hook_a2 is not None:
        hook_a2()
    p.scope_open()
    gfb = [p.sbuf(f"M_gfb{i}", [128, TPC], BF16) for i in range(8)]
    assert [p.nc.lookup_mloc(t).addr for t in gfb] == qf_addr
    woft = [p.sbuf(f"M_wof{i}", [128, 1024], BF16) for i in range(8)]
    woat = [p.sbuf(f"M_woa{i}", [128, 1024], BF16) for i in range(8)]
    gaft = [p.sbuf(f"M_gaf{i}", [128, TPC], BF16) for i in range(8)]
    for i in range(8):
        p.dma("gpsimd", woft[i][:], io["w_outT_ffwd"][128 * i:128 * i + 128, :],
              sem=s_w)
        p.dma("gpsimd", woat[i][:], io["woa"][128 * i:128 * i + 128, :],
              sem=s_w)
        p.dma("sync", gaft[i][:], io["gaf"][128 * i:128 * i + 128, :],
              sem=s_in)
    W_OF = s_w.val
    A2_IN = s_in.val
    p.wait("tensor", s_w, W_OF)
    p.wait("tensor", s_in, A2_IN)
    for m in range(8):
        for (o, n) in nblocks(TPC):
            ps, pi = pp.get()
            for i in range(8):
                p.op("tensor", "matmul", ps[:, 0:n],
                     woft[i][:, 128 * m:128 * m + 128], gfb[i][:, o:o + n],
                     start=(i == 0), stop=False)
            for i in range(8):
                p.op("tensor", "matmul", ps[:, 0:n],
                     woat[i][:, 128 * m:128 * m + 128], gaft[i][:, o:o + n],
                     start=False, stop=(i == 7), sem=s_t if i == 7 else None)
            p.wait("scalar", s_t)
            p.op("scalar", "activation", gsil[m][:, o:o + n], ps[:, 0:n],
                 AF.Silu, sem=s_sc)
            pp.done(pi, s_sc)
    p.wait("scalar", s_sc)
    p.wait("vector", s_sc)
    p.scope_close(s_b)
    if stop_after == "A2":
        return

    for hb in range(2):
        bo = hb * 1024
        W = 1024
        p.scope_open()
        cvh = [p.sbuf(f"B{hb}_cv{i}", [128, W + 30], BF16) for i in range(8)]
        fsh = [p.sbuf(f"B{hb}_fs{i}", [128, W], BF16) for i in range(8)]
        flh = [p.sbuf(f"B{hb}_fl{i}", [128, W], BF16) for i in range(8)]
        doh = [p.sbuf(f"B{hb}_do{i}", [128, W], BF16) for i in range(8)]
        zxh = [p.sbuf(f"B{hb}_zx{i}", [128, W], BF16) for i in range(4)]
        w1b = [p.sbuf(f"B{hb}_w1b{i}", [128, 512], BF16) for i in range(8)]
        wot = [p.sbuf(f"B{hb}_wot{i}", [128, 1024], BF16) for i in range(8)]
        fswt = [p.sbuf(f"B{hb}_fsw{i}", [128, 3], F32) for i in range(8)]
        flwt = [p.sbuf(f"B{hb}_flw{i}", [128, 31], F32) for i in range(8)]
        mrow = [p.sbuf(f"B{hb}_mr{h}", [4, W], BF16) for h in range(4)]
        rrow = [p.sbuf(f"B{hb}_rr{h}", [4, W], BF16) for h in range(4)]
        xrow = [p.sbuf(f"B{hb}_xr{h}", [4, W], BF16) for h in range(4)]
        wall = [p.sbuf(f"B{hb}_wall{h}", [128, 32], F32) for h in range(4)]
        hmid = [p.sbuf(f"B{hb}_hm{i}", [128, 512], BF16) for i in range(4)]
        sqbr = [p.sbuf(f"B{hb}_sqbr{i}", [128, 512], BF16) for i in range(2)]
        vtmt = p.sbuf(f"B{hb}_vtmt", [128, 1024], BF16)
        dotmt = p.sbuf(f"B{hb}_dotmt", [128, 1024], BF16)
        ftmt = p.sbuf(f"B{hb}_ftmt", [128, 1024], BF16)
        mx16 = p.sbuf(f"B{hb}_mx16", [128, 16], BF16)
        ltile = p.sbuf(f"B{hb}_lt", [128, 4], F32)
        lcol = p.sbuf(f"B{hb}_lc", [128, 4], F32)
        wtile = p.sbuf(f"B{hb}_wt", [128, 4], F32)
        wtile2 = p.sbuf(f"B{hb}_wt2", [128, 4], F32)
        lt16 = p.sbuf(f"B{hb}_lt16", [128, 16], F32)
        wt16 = p.sbuf(f"B{hb}_wt16", [128, 16], F32)
        gx4 = p.sbuf(f"B{hb}_gx4", [128, 4], F32)
        obuf1 = p.sbuf(f"B{hb}_ob1", [128, 512], BF16)
        obuf2 = p.sbuf(f"B{hb}_ob2", [128, 512], BF16)
        obuf3 = p.sbuf(f"B{hb}_ob3", [128, 512], BF16)
        wrep = p.sbuf(f"B{hb}_wrep", [128, 512], BF16)
        rrep = p.sbuf(f"B{hb}_rrep", [128, 512], BF16)
        rrw = p.sbuf(f"B{hb}_rrw", [1, W], F32)
        xrs = p.sbuf(f"B{hb}_xrs", [128, 512], F32)
        x1f = p.sbuf(f"B{hb}_x1f", [128, 512], F32)

        for i in range(8):
            p.dma("sync", cvh[i][:], io["cv"][128 * i:128 * i + 128,
                                              bo:bo + W + 30], sem=s_in)
            p.dma("sync", doh[i][:],
                  io["gdo_fm"][128 * i:128 * i + 128, bo:bo + W], sem=s_in)
            p.dma("sync", fswt[i][:], io["fir_s"][128 * i:128 * i + 128, :],
                  sem=s_in)
            p.dma("sync", flwt[i][:], io["fir_l"][128 * i:128 * i + 128, :],
                  sem=s_in)
            p.dma("gpsimd", w1b[i][:],
                  io["gW1"][4096 + 128 * i:4096 + 128 * i + 128, :], sem=s_w)
            p.dma("gpsimd", wot[i][:], io["Wo"][128 * i:128 * i + 128, :],
                  sem=s_w)
        for i in range(4):
            p.dma("sync", zxh[i][:], io["zx"][128 * i:128 * i + 128,
                                              bo:bo + W], sem=s_in)
        B_IN = s_in.val
        B_W = s_w.val
        p.wait("vector", s_in, B_IN)
        p.wait("tensor", s_in, B_IN)
        marker(1001 + hb)
        p.wait("vector", s_c)
        for ch in range(8):
            for (wtiles, ntap, dst) in ((fswt, 3, fsh), (flwt, 31, flh)):
                for (o, n) in nblocks(W):
                    ps2, pi2 = pp2.get()
                    dvals, tvals = [], []

                    def build_diag(j):
                        if j >= 2:
                            p.wait("vector", s_t, tvals[j - 2])
                        p.op("vector", "tensor_scalar_mul", diag[j % 2][:],
                             identbf[:], wtiles[ch][:, j:j + 1], sem=s_v)
                        dvals.append(s_v.val)

                    build_diag(0)
                    for j in range(ntap):
                        if j + 1 < ntap:
                            build_diag(j + 1)
                        p.wait("tensor", s_v, dvals[j])
                        p.op("tensor", "matmul", ps2[:, 0:n], diag[j % 2][:],
                             cvh[ch][:, o + j + 30 - ntap + 1:
                                     o + j + 30 - ntap + 1 + n],
                             start=(j == 0), stop=(j == ntap - 1), sem=s_t)
                        tvals.append(s_t.val)
                    p.wait("vector", s_t)
                    p.op("vector", "tensor_copy", dst[ch][:, o:o + n],
                         ps2[:, 0:n], sem=s_v)
                    pp2.done(pi2, s_v)
        FIR_D = s_v.val

        branches = [fsh, flh, doh, None]

        def brsl(br, gch, o, n):
            if branches[br] is None:
                return cvh[gch][:, 30 + o:30 + o + n]
            return branches[br][gch][:, o:o + n]

        p.wait("tensor", s_v, FIR_D)
        for h in range(4):
            for (o, n) in nblocks(W):
                psm, pim = pp.get()
                for br in range(4):
                    for ch in range(2):
                        p.op("tensor", "matmul", psm[0:4, 0:n], emt[br][:],
                             brsl(br, 2 * h + ch, o, n),
                             start=(br == 0 and ch == 0),
                             stop=(br == 3 and ch == 1),
                             sem=s_t if (br == 3 and ch == 1) else None)
                psr, pir = pp.get()
                for br in range(4):
                    for ch in range(2):
                        k = (br * 2 + ch) % 2
                        sl = brsl(br, 2 * h + ch, o, n)
                        if br + ch > 0:
                            p.wait("vector", s_t, s_t.val - 1)
                        p.op("vector", "tensor_mul", sqbr[k][:, 0:n], sl, sl,
                             sem=s_v)
                        p.wait("tensor", s_v)
                        p.op("tensor", "matmul", psr[0:4, 0:n], emt[br][:],
                             sqbr[k][:, 0:n], start=(br == 0 and ch == 0),
                             stop=(br == 3 and ch == 1), sem=s_t)
                p.wait("vector", s_t)
                p.op("vector", "tensor_copy", mrow[h][:, o:o + n],
                     psm[0:4, 0:n], sem=s_v)
                pp.done(pim, s_v)
                p.opw("vector", s_v, "tensor_scalar_max", sqbr[0][0:4, 0:n],
                      psr[0:4, 0:n], 1e-8)
                pp.done(pir, s_v)
                p.wait("scalar", s_v)
                p.op("scalar", "activation", rrow[h][:, o:o + n],
                     sqbr[0][0:4, 0:n], AF.Sqrt, sem=s_sc)
                p.wait("vector", s_sc)
        marker(1101 + hb)
        for ltl in range(W // 128):
            lo = bo + ltl * 128
            p.wait("sync", s_v)
            p.dma("sync", vtmt[:], io["cvt"][lo:lo + 128, :], sem=s_in)
            p.wait("vector", s_in)
            p.wait("tensor", s_in)
            def tp4(src_tiles, dst, half):
                ps, pi = ppt.get()
                for j in range(4):
                    ch = 4 * half + j
                    p.op("tensor", "transpose", ps[:, 128 * j:128 * j + 128],
                         src_tiles[ch][:, ltl * 128:ltl * 128 + 128],
                         identbf[:], sem=s_t if j == 3 else None)
                p.wait("vector", s_t)
                p.op("vector", "tensor_copy",
                     dst[:, 512 * half:512 * half + 512], ps[:, 0:512],
                     sem=s_v)
                ppt.done(pi, s_v)

            for half in range(2):
                tp4(doh, dotmt, half)
            for br, ft in ((0, fsh), (1, flh)):
                for half in range(2):
                    tp4(ft, ftmt, half)
                p.opw("vector", s_v, "tensor_reduce",
                      mx16.rearrange("p (g o) -> p g o", o=1)[:, br::4, :],
                      ftmt.rearrange("p (g d) -> p g d", d=256), AX.X, ALU.max)
            p.opw("vector", s_v, "tensor_reduce",
                  mx16.rearrange("p (g o) -> p g o", o=1)[:, 2::4, :],
                  dotmt.rearrange("p (g d) -> p g d", d=256), AX.X, ALU.max)
            p.opw("vector", s_v, "tensor_reduce",
                  mx16.rearrange("p (g o) -> p g o", o=1)[:, 3::4, :],
                  vtmt.rearrange("p (g d) -> p g d", d=256), AX.X, ALU.max)
            p.wait("tensor", s_v)
            for h in range(4):
                ps, pi = ppt.get()
                p.op("tensor", "transpose", ps[0:4, 0:128],
                     mx16[:, 4 * h:4 * h + 4], identbf[:], sem=s_t)
                p.wait("vector", s_t)
                p.op("vector", "tensor_copy",
                     xrow[h][:, ltl * 128:ltl * 128 + 128], ps[0:4, 0:128],
                     sem=s_v)
                ppt.done(pi, s_v)
        STATS_D = s_v.val

        marker(1201 + hb)
        p.wait("tensor", s_w, B_W)
        p.wait("tensor", s_v, STATS_D)
        for h in range(4):
            for bi, (o, n) in enumerate(nblocks(W)):
                for mch in range(4):
                    ps, pi = pp.get()
                    p.op("tensor", "matmul", ps[:, 0:n], identbf[:],
                         zxh[mch][:, o:o + n], start=True, stop=False)
                    for br in range(4):
                        for ch in range(2):
                            p.op("tensor", "matmul", ps[:, 0:n],
                                 w1b[br * 2 + ch][:, 128 * mch:128 * mch + 128],
                                 brsl(br, 2 * h + ch, o, n),
                                 start=False, stop=False)
                    p.op("tensor", "matmul", ps[:, 0:n],
                         wstc[0][:, 128 * mch:128 * mch + 128],
                         mrow[h][:, o:o + n], start=False, stop=False)
                    p.op("tensor", "matmul", ps[:, 0:n],
                         wstc[1][:, 128 * mch:128 * mch + 128],
                         rrow[h][:, o:o + n], start=False, stop=False)
                    p.op("tensor", "matmul", ps[:, 0:n],
                         wstc[2][:, 128 * mch:128 * mch + 128],
                         xrow[h][:, o:o + n], start=False, stop=True, sem=s_t)
                    p.wait("scalar", s_t)
                    p.op("scalar", "activation", hmid[mch][:, 0:n], ps[:, 0:n],
                         AF.Gelu, bias=gb1c[mch][:, 0:1], sem=s_sc)
                    pp.done(pi, s_sc)
                p.wait("tensor", s_sc)
                psg, pig = pp.get()
                for sub in range(n // 128):
                    for mch in range(4):
                        p.op("tensor", "matmul", psg[:, 4 * sub:4 * sub + 4],
                             hmid[mch][:, sub * 128:sub * 128 + 128],
                             gw2t[mch][:], start=(mch == 0), stop=False)
                    p.op("tensor", "matmul", psg[:, 4 * sub:4 * sub + 4],
                         onesrb[:], gb2r[:], start=False, stop=True,
                         sem=s_t if sub == n // 128 - 1 else None)
                p.wait("vector", s_t)
                p.opw("vector", s_v, "tensor_copy", lt16[:], psg[:, 0:16])
                pp.done(pig, s_v)

                def g4(t):
                    return t[:].rearrange("p (g f) -> p g f", f=4)

                def gcol(t):
                    return t[:].rearrange("p (g o) -> p g o", o=1)

                p.opw("vector", s_v, "tensor_reduce", gcol(gx4),
                      g4(lt16), AX.X, ALU.max)
                p.opw("vector", s_v, "tensor_sub", g4(lt16), g4(lt16),
                      gcol(gx4).broadcast_to([128, 4, 4]))
                p.opw("vector", s_v, "tensor_scalar_mul", lt16[:], lt16[:],
                      invtc[h][:, 0:1])
                p.wait("scalar", s_v)
                p.op("scalar", "activation", wt16[:], lt16[:], AF.Exp,
                     sem=s_sc)
                p.wait("vector", s_sc)
                p.opw("vector", s_v, "tensor_reduce", gcol(gx4),
                      g4(wt16), AX.X, ALU.add)
                p.opw("vector", s_v, "reciprocal", gx4[:], gx4[:])
                p.opw("vector", s_v, "tensor_mul", g4(wt16), g4(wt16),
                      gcol(gx4).broadcast_to([128, 4, 4]))
                p.opw("vector", s_v, "tensor_max", g4(wt16),
                      g4(wt16),
                      flrep[h][:].rearrange("p (g f) -> p g f", g=1)
                      .broadcast_to([128, 4, 4]))
                p.opw("vector", s_v, "tensor_reduce", gcol(gx4),
                      g4(wt16), AX.X, ALU.add)
                p.opw("vector", s_v, "reciprocal", gx4[:], gx4[:])
                p.opw("vector", s_v, "tensor_mul", g4(wt16), g4(wt16),
                      gcol(gx4).broadcast_to([128, 4, 4]))
                p.op("vector", "tensor_copy",
                     wall[h][:, 16 * bi:16 * bi + 16], wt16[:], sem=s_v)
        GATE_D = s_v.val

        marker(1301 + hb)
        p.wait("tensor", s_v, GATE_D)
        for h in range(4):
            for (o, n) in nblocks(W):
                for ch in range(2):
                    gch = 2 * h + ch
                    for br in range(4):
                        for sub in range(n // 128):
                            ltg = (o // 128) + sub
                            p.opw("vector", s_v, "tensor_scalar_mul",
                                  diag[sub % 2][:], identbf[:],
                                  wall[h][:, 4 * ltg + br:4 * ltg + br + 1])
                            p.wait("tensor", s_v)
                            ps, pi = pp.get()
                            p.op("tensor", "matmul", ps[:, 0:128], ones128[:],
                                 diag[sub % 2][:], start=True, stop=True,
                                 sem=s_t)
                            p.wait("vector", s_t)
                            p.op("vector", "tensor_copy",
                                 wrep[:, 128 * sub:128 * sub + 128],
                                 ps[:, 0:128], sem=s_v)
                            pp.done(pi, s_v)
                        p.wait("vector", s_v)
                        if br == 0:
                            p.opw("vector", s_v, "tensor_mul", obuf1[:, 0:n],
                                  brsl(br, gch, o, n), wrep[:, 0:n])
                        else:
                            p.opw("vector", s_v, "tensor_mul", obuf2[:, 0:n],
                                  brsl(br, gch, o, n), wrep[:, 0:n])
                            p.opw("vector", s_v, "tensor_add", obuf1[:, 0:n],
                                  obuf1[:, 0:n], obuf2[:, 0:n])
                    p.op("vector", "tensor_copy",
                         (obuf3 if ch == 1 else obuf2)[:, 0:n], obuf1[:, 0:n],
                         sem=s_v)
                    p.wait("vector", s_v)
                p.opw("vector", s_v, "tensor_mul", obuf1[:, 0:n],
                      obuf2[:, 0:n], obuf2[:, 0:n])
                p.opw("vector", s_v, "tensor_mul", wrep[:, 0:n],
                      obuf3[:, 0:n], obuf3[:, 0:n])
                p.wait("tensor", s_v)
                ps, pi = pp.get()
                p.op("tensor", "matmul", ps[0:1, 0:n], onescol[:, 0:1],
                     obuf1[:, 0:n], start=True, stop=False)
                p.op("tensor", "matmul", ps[0:1, 0:n], onescol[:, 0:1],
                     wrep[:, 0:n], start=False, stop=True, sem=s_t)
                p.wait("vector", s_t)
                p.opw("vector", s_v, "tensor_scalar_mul", rrw[0:1, o:o + n],
                      ps[0:1, 0:n], 1.0 / 256.0)
                pp.done(pi, s_v)
                p.wait("scalar", s_v)
                p.op("scalar", "activation", rrw[0:1, o:o + n],
                     rrw[0:1, o:o + n], AF.Sqrt, bias=eps5[0:1, 0:1],
                     sem=s_sc)
                p.wait("vector", s_sc)
                p.opw("vector", s_v, "reciprocal", rrw[0:1, o:o + n],
                      rrw[0:1, o:o + n])
                p.wait("tensor", s_v)
                ps, pi = pp.get()
                p.op("tensor", "matmul", ps[:, 0:n], onesrf[:],
                     rrw[0:1, o:o + n], start=True, stop=True, sem=s_t)
                p.wait("vector", s_t)
                p.opw("vector", s_v, "tensor_copy", rrep[:, 0:n], ps[:, 0:n])
                pp.done(pi, s_v)
                for ch, ob in ((0, obuf2), (1, obuf3)):
                    p.opw("vector", s_v, "tensor_mul", ob[:, 0:n], ob[:, 0:n],
                          rrep[:, 0:n])
                    p.opw("vector", s_v, "tensor_scalar_mul", ob[:, 0:n],
                          ob[:, 0:n], onrm[ch][:, 0:1])
                    p.op("vector", "tensor_copy",
                         fsh[2 * h + ch][:, o:o + n], ob[:, 0:n], sem=s_v)
                    p.wait("vector", s_v)
        ON_D = s_v.val

        p.wait("tensor", s_v, ON_D)
        for m in range(8):
            for (o, n) in nblocks(W):
                ps, pi = pp.get()
                for i in range(8):
                    p.op("tensor", "matmul", ps[:, 0:n],
                         wot[i][:, 128 * m:128 * m + 128], fsh[i][:, o:o + n],
                         start=(i == 0), stop=(i == 7),
                         sem=s_t if i == 7 else None)
                p.dma("sync", xrs[:, 0:n],
                      io["xTr"][128 * m:128 * m + 128, bo + o:bo + o + n],
                      sem=s_in)
                p.wait("vector", s_in)
                p.wait("vector", s_t)
                p.opw("vector", s_v, "tensor_mul", x1f[:, 0:n], ps[:, 0:n],
                      gsil[m][:, bo + o:bo + o + n])
                pp.done(pi, s_v)
                p.opw("vector", s_v, "tensor_add", x1f[:, 0:n], x1f[:, 0:n],
                      xrs[:, 0:n])
                p.op("vector", "tensor_copy", gsil[m][:, bo + o:bo + o + n],
                     x1f[:, 0:n], sem=s_v)
                p.wait("sync", s_v)
                p.dma("sync", io["x1"][128 * m:128 * m + 128, bo + o:bo + o + n],
                      x1f[:, 0:n], sem=s_out)
                p.wait("vector", s_out)
                p.wait("sync", s_out)
        marker(1401 + hb)
        p.scope_close(s_b)
    if stop_after == "B":
        return

    p.scope_open()
    x1b = gsil
    rows2 = p.sbuf("C_rows2", [1, 3 * TPC], F32)
    brep2 = [p.sbuf(f"C_brep{i}", [128, TPC], BF16) for i in range(2)]
    sqb2 = [p.sbuf(f"C_sqb{i}", [128, 512], BF16) for i in range(2)]
    for (o, n) in nblocks(TPC):
        ps, pi = pp.get()
        for i in range(8):
            p.op("tensor", "matmul", ps[0:1, 0:n], onescol[:, 0:1],
                 x1b[i][:, o:o + n], start=(i == 0), stop=(i == 7),
                 sem=s_t if i == 7 else None)
        p.wait("scalar", s_t)
        p.op("scalar", "copy", rows2[0:1, o:o + n], ps[0:1, 0:n], sem=s_sc)
        pp.done(pi, s_sc)
    sq_lastc = [None, None]
    for (o, n) in nblocks(TPC):
        ps, pi = pp.get()
        for i in range(8):
            bq = i % 2
            sq = sqb2[bq]
            if sq_lastc[bq] is not None:
                p.wait("vector", s_t, sq_lastc[bq])
            p.op("vector", "tensor_mul", sq[:, 0:n], x1b[i][:, o:o + n],
                 x1b[i][:, o:o + n], sem=s_v)
            p.wait("tensor", s_v)
            p.op("tensor", "matmul", ps[0:1, 0:n], onescol[:, 0:1],
                 sq[:, 0:n], start=(i == 0), stop=(i == 7), sem=s_t)
            sq_lastc[bq] = s_t.val
        p.wait("scalar", s_t)
        p.op("scalar", "copy", rows2[0:1, TPC + o:TPC + o + n], ps[0:1, 0:n],
             sem=s_sc)
        pp.done(pi, s_sc)
    R0, R1, R2 = 0, TPC, 2 * TPC
    p.wait("vector", s_sc)
    p.opw("vector", s_v, "tensor_scalar_mul", rows2[0:1, R0:R0 + TPC],
          rows2[0:1, R0:R0 + TPC], 1.0 / DM)
    p.opw("vector", s_v, "tensor_mul", rows2[0:1, R2:R2 + TPC],
          rows2[0:1, R0:R0 + TPC], rows2[0:1, R0:R0 + TPC])
    p.opw("vector", s_v, "tensor_scalar_mul", rows2[0:1, R1:R1 + TPC],
          rows2[0:1, R1:R1 + TPC], 1.0 / DM)
    p.op("vector", "tensor_sub", rows2[0:1, R1:R1 + TPC],
         rows2[0:1, R1:R1 + TPC], rows2[0:1, R2:R2 + TPC], sem=s_v)
    p.wait("scalar", s_v)
    p.op("scalar", "activation", rows2[0:1, R1:R1 + TPC],
         rows2[0:1, R1:R1 + TPC], AF.Sqrt, bias=eps5[0:1, 0:1], sem=s_sc)
    p.wait("vector", s_sc)
    p.opw("vector", s_v, "reciprocal", rows2[0:1, R1:R1 + TPC],
          rows2[0:1, R1:R1 + TPC])
    p.opw("vector", s_v, "tensor_mul", rows2[0:1, R2:R2 + TPC],
          rows2[0:1, R0:R0 + TPC], rows2[0:1, R1:R1 + TPC])
    p.op("vector", "tensor_scalar_mul", rows2[0:1, R2:R2 + TPC],
         rows2[0:1, R2:R2 + TPC], -1.0, sem=s_v)
    p.wait("tensor", s_v)
    for r in range(2):
        base = R1 if r == 0 else R2
        for (o, n) in nblocks(TPC):
            ps, pi = pp.get()
            p.op("tensor", "matmul", ps[:, 0:n], onesrf[:],
                 rows2[0:1, base + o:base + o + n], start=True, stop=True,
                 sem=s_t)
            p.wait("vector", s_t)
            p.op("vector", "tensor_copy", brep2[r][:, o:o + n], ps[:, 0:n],
                 sem=s_v)
            pp.done(pi, s_v)
    for i in range(8):
        p.opw("vector", s_v, "tensor_mul", sqb2[0][:, 0:512],
              x1b[i][:, 0:512], x1b[i][:, 0:512]) if False else None
        p.opw("vector", s_v, "tensor_mul", x1b[i][:], x1b[i][:], brep2[0][:])
        p.op("vector", "tensor_add", x1b[i][:], x1b[i][:], brep2[1][:],
             sem=s_v)
        p.wait("sync", s_v)
        p.dma("sync", io["xn2"][128 * i:128 * i + 128, :], x1b[i][:],
              sem=s_out)
    p.wait("sync", s_out)
    p.scope_close(s_b)


STAGE_M_INS = {
    "qf": ([DM, TPC], BF16), "g16": ([TPC, 16], F32),
    "cv": ([DM, TPC + 30], BF16), "cvt": ([TPC, DM], BF16),
    "gdo_fm": ([DM, TPC], BF16), "gdo_tm": ([TPC, DM], BF16),
    "gaf": ([DM, TPC], BF16), "woa": ([DM, DM], F32),
    "zx": ([512, TPC], BF16),
    "xTr": ([DM, TPC], F32),
    "kfwT": ([16 * 8 * 128, 256], F32), "vfw": ([16 * 8 * 256, 128], F32),
    "w_outT_ffwd": ([DM, DM], F32),
    "gW1": ([5120, 512], F32), "gb1": ([512, 1], F32),
    "gW2": ([512, 4], F32), "gb2": ([1, 4], F32),
    "temp4": ([1, 4], F32), "epsf16": ([1, 16], F32),
    "onorm": ([256, 1], F32), "Wo": ([DM, DM], F32),
    "fir_s": ([DM, 3], F32), "fir_l": ([DM, 31], F32),
    "emat": ([4 * 128, 4], BF16), "gstat": ([24 * 128, 4], BF16),
    "ones2": ([128, 2], BF16), "onesrf": ([1, 128], F32),
    "onesrb": ([1, 128], BF16), "ident": ([128, 128], BF16),
    "ones128": ([128, 128], BF16),
}
STAGE_M_OUTS = {
    "x1": ([DM, TPC], F32), "xn2": ([DM, TPC], BF16),
}


def stage_m_inputs(inputs, aouts, do_full, gaf_own, c):
    t0 = c * TPC
    do_tm = do_full[t0:t0 + TPC, :]
    emat = np.zeros((4 * 128, 4), np.float32)
    for br in range(4):
        emat[128 * br:128 * br + 128, br] = 1.0 / 256.0
    gstat = np.zeros((24 * 128, 4), np.float32)
    for gi in range(24):
        gstat[128 * gi:128 * gi + 128, gi // 6] = 1.0
    return {
        "qf": aouts["qf"][c], "g16": aouts["g16"][c], "cv": aouts["cv"][c],
        "cvt": aouts["cvt"][c],
        "gdo_fm": np.ascontiguousarray(do_tm.T), "gdo_tm": do_tm,
        "gaf": gaf_own,
        "zx": aouts["zx"][c],
        **stage_m_ext(inputs, c),
    }


def stage_m_ext(inputs, c):
    t0 = c * TPC
    emat = np.zeros((4 * 128, 4), np.float32)
    for br in range(4):
        emat[128 * br:128 * br + 128, br] = 1.0 / 256.0
    gstat = np.zeros((24 * 128, 4), np.float32)
    for gi in range(24):
        gstat[128 * gi:128 * gi + 128, gi // 6] = 1.0
    return {
        "woa": np.ascontiguousarray(inputs["L0_ffn_w_out"].T[0:1024, :]),
        "xTr": np.ascontiguousarray(inputs["L0_attn_x"][0][t0:t0 + TPC].T),
        "kfwT": np.ascontiguousarray(
            inputs["L0_ffn_k_ffwd"].transpose(0, 1, 3, 2).reshape(
                16 * 8 * 128, 256)),
        "vfw": inputs["L0_ffn_v_ffwd"].reshape(16 * 8 * 256, 128),
        "w_outT_ffwd": np.ascontiguousarray(
            inputs["L0_ffn_w_out"].T[1024:2048, :]),
        "gW1": inputs["L0_attn_gW1"],
        "gb1": inputs["L0_attn_gb1"].reshape(512, 1),
        "gW2": inputs["L0_attn_gW2"],
        "gb2": inputs["L0_attn_gb2"].reshape(1, 4),
        "temp4": inputs["L0_attn_temp"].reshape(1, 4),
        "epsf16": inputs["L0_attn_eps_floor"].reshape(1, 16),
        "onorm": inputs["L0_attn_onorm_w"].reshape(256, 1),
        "Wo": inputs["L0_attn_Wo"],
        "fir_s": inputs["L0_attn_fir_s"].reshape(1024, 3),
        "fir_l": inputs["L0_attn_fir_l"].reshape(1024, 31),
        "emat": emat.astype(BF), "gstat": gstat.astype(BF),
        "ones2": np.ones((128, 2), BF),
        "onesrf": np.ones((1, 128), np.float32),
        "onesrb": np.ones((1, 128), BF),
        "ident": np.eye(128, dtype=BF),
        "ones128": np.ones((128, 128), BF),
    }


def build_stage_m(stop_after=None):
    nc = bass.Bass("TRN2", target_bir_lowering=False)
    io = {}
    for k, (shape, dt) in STAGE_M_INS.items():
        io[k] = nc.dram_tensor(k, shape, dt, kind="ExternalInput").ap()
    for k, (shape, dt) in STAGE_M_OUTS.items():
        io[k] = nc.dram_tensor(k, shape, dt, kind="ExternalOutput").ap()
    p = Prog(nc)
    stage_m(p, io, stop_after=stop_after)
    p.emit_and_finish()
    return nc


def bitonic_passes():
    out = []
    k = 2
    while k <= 32:
        j = k // 2
        while j >= 1:
            out.append((k, j))
            j //= 2
        k *= 2
    return out


def ap_set(buf, base, k, j, G):
    outer = 32 // (2 * k) if k < 32 else 1
    mid = k // (2 * j)
    low = j
    return buf[:, base:].rearrange("p (x) -> p x")[:, 0:1] if False else None


def stage_l1(p, io, fz=None):
    s_w = p.sem("lw"); s_c = p.sem("lc"); s_v = p.sem("lv")
    s_t = p.sem("lt"); s_sc = p.sem("lsc"); s_out = p.sem("lout")
    s_in = p.sem("lin"); s_b = p.sem("lb")

    onescol = p.sbuf("L_ones", [128, 2], BF16)
    onesrf = p.sbuf("L_onesrf", [1, 128], F32)
    onesrb = p.sbuf("L_onesrb", [1, 128], BF16)
    identbf = p.sbuf("L_id", [128, 128], BF16)
    ones128 = p.sbuf("L_o128", [128, 128], BF16)
    diag = [p.sbuf(f"L_diag{i}", [128, 128], BF16) for i in range(2)]
    kramp = p.sbuf("L_kramp", [128, 512], F32)
    s1t = [p.sbuf(f"L_s1{i}", [128, TPC], BF16) for i in range(8)]

    pp = PP(p, "L_pp", 2, [128, 512])
    pp2 = PP(p, "L_pp2", 2, [128, 1024])
    ppt = PP(p, "L_ppt", 2, [128, 128], BF16)

    p.dma("sync", onescol[:], io["ones2"], sem=s_c)
    p.dma("sync", onesrf[:], io["onesrf"], sem=s_c)
    p.dma("sync", onesrb[:], io["onesrb"], sem=s_c)
    p.dma("sync", identbf[:], io["ident"], sem=s_c)
    p.dma("sync", ones128[:], io["ones128"], sem=s_c)
    p.dma("sync", kramp[:], io["kramp"], sem=s_c)
    p.wait("vector", s_c)
    p.wait("tensor", s_c)

    p.scope_open()
    kpT = [p.sbuf(f"L_kp{i}", [128, 32], BF16) for i in range(8)]
    wkt = [p.sbuf(f"L_wk{i}", [128, 1024], BF16) for i in range(8)]
    keystm = p.sbuf("L_keystm", [32, 1024], BF16)
    keysfm = [p.sbuf(f"L_kf{i}", [128, 32], BF16) for i in range(8)]
    wkh = [p.sbuf(f"L_wkh{i}", [128, 128], BF16) for i in range(8)]
    bkhr = p.sbuf("L_bkhr", [1, 128], BF16)
    kslt = p.sbuf("L_kslt", [32, 128], BF16)
    kslf = p.sbuf("L_kslf", [128, 32], BF16)
    valstm = p.sbuf("L_valstm", [32, 128], BF16)
    bkr = p.sbuf("L_bkr", [1, 1024], BF16)
    bqc = [p.sbuf(f"L_bq{i}", [128, 1], F32) for i in range(1)]
    boc = p.sbuf("L_boc", [128, 8], F32)
    q2 = p.sbuf("L_q2", [128, L], BF16)
    atp = p.sbuf("L_atp", [32, L], BF16)
    sbat = p.sbuf("L_sbat", [128, 512], F32)
    zsort = p.sbuf("L_zsort", [128, 512], F32)
    ppng = [p.sbuf(f"L_png{i}", [128, 512], F32) for i in range(2)]
    cum = p.sbuf("L_cum", [128, 512], F32)
    trow = p.sbuf("L_trow", [128, 16], F32)
    trow2 = p.sbuf("L_trow2", [128, 16], F32)
    pbf = p.sbuf("L_pbf", [128, 512], BF16)
    opm = [p.sbuf(f"L_opm{i}", [128, 1024], BF16) for i in range(8)]

    for i in range(8):
        p.dma("gpsimd", kpT[i][:], io["kpT"][128 * i:128 * i + 128, :], sem=s_w)
        p.dma("gpsimd", wkt[i][:], io["Wk1"][128 * i:128 * i + 128, :], sem=s_w)
    p.dma("gpsimd", bkr[:], io["bk1"], sem=s_w)
    p.dma("sync", bqc[0][:], io["bq1h"], sem=s_w)
    p.dma("sync", boc[:], io["bo1"], sem=s_w)
    W_KV = s_w.val
    p.wait("tensor", s_w, W_KV)
    for half in range(2):
        ps2, pi2 = pp2.get()
        for i in range(8):
            p.op("tensor", "matmul", ps2[0:32, 0:512], kpT[i][:],
                 wkt[i][:, 512 * half:512 * half + 512], start=(i == 0),
                 stop=False)
        p.op("tensor", "matmul", ps2[0:32, 0:512], onesrb[0:1, 0:32],
             bkr[0:1, 512 * half:512 * half + 512], start=False, stop=True,
             sem=s_t)
        p.wait("vector", s_t)
        p.op("vector", "tensor_copy", keystm[:, 512 * half:512 * half + 512],
             ps2[0:32, 0:512], sem=s_v)
        pp2.done(pi2, s_v)
    p.wait("tensor", s_v)
    for i in range(8):
        ps, pi = ppt.get()
        p.op("tensor", "transpose", ps[0:128, 0:32],
             keystm[:, 128 * i:128 * i + 128], identbf[0:32, 0:32], sem=s_t)
        p.wait("vector", s_t)
        p.op("vector", "tensor_copy", keysfm[i][:], ps[0:128, 0:32], sem=s_v)
        ppt.done(pi, s_v)
    for i in range(8):
        p.dma("gpsimd", wkh[i][:], io["Wk1h"][128 * i:128 * i + 128, :],
              sem=s_w)
    p.dma("gpsimd", bkhr[:], io["bk1h"], sem=s_w)
    W_KH = s_w.val
    p.wait("tensor", s_w, W_KH)
    psk, pik = pp.get()
    for i in range(8):
        p.op("tensor", "matmul", psk[0:32, 0:128], kpT[i][:], wkh[i][:],
             start=(i == 0), stop=False)
    p.op("tensor", "matmul", psk[0:32, 0:128], onesrb[0:1, 0:32], bkhr[:],
         start=False, stop=True, sem=s_t)
    p.wait("vector", s_t)
    p.op("vector", "tensor_copy", kslt[:], psk[0:32, 0:128], sem=s_v)
    pp.done(pik, s_v)
    p.wait("tensor", s_v)
    psk2, pik2 = ppt.get()
    p.op("tensor", "transpose", psk2[0:128, 0:32], kslt[:],
         identbf[0:32, 0:32], sem=s_t)
    p.wait("vector", s_t)
    p.op("vector", "tensor_copy", kslf[:], psk2[0:128, 0:32], sem=s_v)
    ppt.done(pik2, s_v)
    p.wait("gpsimd", s_v)
    for i in range(8):
        p.dma("gpsimd", wkh[i][:], io["Wv1h"][128 * i:128 * i + 128, :],
              sem=s_w)
    p.dma("gpsimd", bkhr[:], io["bv1h"], sem=s_w)
    W_V1 = s_w.val
    p.wait("tensor", s_w, W_V1)
    psv, piv = pp.get()
    for i in range(8):
        p.op("tensor", "matmul", psv[0:32, 0:128], keysfm[i][:], wkh[i][:],
             start=(i == 0), stop=False)
    p.op("tensor", "matmul", psv[0:32, 0:128], onesrb[0:1, 0:32], bkhr[:],
         start=False, stop=True, sem=s_t)
    p.wait("vector", s_t)
    p.op("vector", "tensor_copy", valstm[:], psv[0:32, 0:128], sem=s_v)
    pp.done(piv, s_v)
    p.wait("gpsimd", s_v)
    for i in range(8):
        p.dma("gpsimd", wkt[i][:, 0:128],
              io["Wq1h"][128 * i:128 * i + 128, :], sem=s_w)
    W_Q1 = s_w.val
    p.wait("vector", s_w, W_Q1)
    p.opw("vector", s_v, "tensor_scalar_mul", bqc[0][:], bqc[0][:], 0.125)
    p.wait("tensor", s_w, W_Q1)
    gxn = io["gxn"]
    xch = [[p.sbuf(f"L_xch{bb}_{i}", [128, 512], BF16) for i in range(8)]
           for bb in range(2)]
    Q_IN0 = s_in.val
    Q_V0 = s_v.val

    def q2_load(blk):
        bb = blk % 2
        if blk >= 2:
            p.wait("sync", s_v, Q_V0 + blk - 1)
        sblk, lblk = divmod(blk, TPC // 512)
        for i in range(8):
            if fz is None:
                p.dma("sync", xch[bb][i][:],
                      gxn[128 * i:128 * i + 128,
                          512 * blk:512 * blk + 512], sem=s_in)
            else:
                p.dma("sync", xch[bb][i][:],
                      fz["Gx"][sblk * DM + 128 * i:sblk * DM + 128 * i + 128,
                               512 * lblk:512 * lblk + 512], sem=s_in)

    q2_load(0)
    for blk in range(L // 512):
        bb = blk % 2
        if blk + 1 < L // 512:
            q2_load(blk + 1)
        ps, pi = pp.get()
        p.wait("tensor", s_in)
        for i in range(8):
            p.op("tensor", "matmul", ps[:, 0:512], wkt[i][:, 0:128],
                 xch[bb][i][:], start=(i == 0), stop=(i == 7),
                 sem=s_t if i == 7 else None)
        p.wait("vector", s_t)
        p.op("vector", "tensor_scalar", q2[:, 512 * blk:512 * blk + 512],
             ps[:, 0:512], 0.125, bqc[0][:, 0:1], ALU.mult, op1=ALU.add,
             sem=s_v)
        pp.done(pi, s_v)
    Q2_D = s_v.val
    p.wait("tensor", s_v, Q2_D)

    for hl in range(2):
        kslice = kslf
        kbase = 64 * hl
        for bat in range(L // 2048):
            for k16 in range(16):
                lt = bat * 16 + k16
                ps, pi = pp.get()
                p.op("tensor", "matmul", ps[:, 0:32],
                     q2[64 * hl:64 * hl + 64, 128 * lt:128 * lt + 128],
                     kslice[kbase:kbase + 64, :], start=True, stop=True,
                     sem=s_t)
                p.wait("vector", s_t)
                p.op("vector", "tensor_copy", sbat[:, 32 * k16:32 * k16 + 32],
                     ps[:, 0:32], sem=s_v)
                pp.done(pi, s_v)
            p.wait("vector", s_v)
            cur = sbat
            pb = 0
            for (kk, jj) in bitonic_passes():
                nxt = ppng[pb]
                outer = 32 // (2 * kk)
                mid = kk // (2 * jj)

                def vsel(buf, base):
                    v = buf[:].rearrange(
                        "p (g o x m l) -> p g o x m l",
                        g=16, o=max(outer, 1), x=(2 if outer >= 1 else 2),
                        m=mid, l=jj) if False else None
                    return v
                def view(buf, dbit, bbit):
                    r = buf[:].rearrange(
                        "p (g o d m b l) -> p g o d m b l",
                        g=16, o=max(outer, 1), d=2 if outer >= 1 else 1,
                        m=mid, b=2, l=jj)
                    return r[:, :, :, dbit:dbit + 1, :, bbit:bbit + 1, :]
                for dbit in range(2 if kk < 32 else 1):
                    lo_in = view(cur, dbit, 0)
                    hi_in = view(cur, dbit, 1)
                    lo_out = view(nxt, dbit, 0)
                    hi_out = view(nxt, dbit, 1)
                    desc = (dbit == 0)
                    p.op("vector", "tensor_tensor", lo_out, lo_in, hi_in,
                         op=(ALU.max if desc else ALU.min), sem=s_v)
                    p.op("vector", "tensor_tensor", hi_out, lo_in, hi_in,
                         op=(ALU.min if desc else ALU.max), sem=s_v)
                p.wait("vector", s_v)
                cur = nxt
                pb = 1 - pb
            p.opw("vector", s_v, "tensor_copy", zsort[:], cur[:])
            for st in range(5):
                d = 1 << st
                nxt = ppng[pb]
                p.op("vector", "tensor_copy",
                     nxt[:].rearrange("p (g x) -> p g x", x=32)[:, :, 0:d],
                     cur[:].rearrange("p (g x) -> p g x", x=32)[:, :, 0:d],
                     sem=s_v)
                p.op("vector", "tensor_tensor",
                     nxt[:].rearrange("p (g x) -> p g x", x=32)[:, :, d:32],
                     cur[:].rearrange("p (g x) -> p g x", x=32)[:, :, d:32],
                     cur[:].rearrange("p (g x) -> p g x", x=32)[:, :, 0:32 - d],
                     op=ALU.add, sem=s_v)
                p.wait("vector", s_v)
                cur = nxt
                pb = 1 - pb
            p.opw("vector", s_v, "tensor_mul", ppng[pb][:], zsort[:], kramp[:])
            p.opw("vector", s_v, "tensor_scalar_add", ppng[pb][:],
                  ppng[pb][:], 1.0)
            p.opw("vector", s_v, "tensor_tensor", ppng[pb][:], ppng[pb][:],
                  cur[:], op=ALU.is_gt)
            p.opw("vector", s_v, "tensor_reduce",
                  trow[:].rearrange("p (g o) -> p g o", o=1),
                  ppng[pb][:].rearrange("p (g x) -> p g x", x=32), AX.X,
                  ALU.add)
            p.opw("vector", s_v, "tensor_mul", cum[:], zsort[:], ppng[pb][:])
            p.opw("vector", s_v, "tensor_reduce",
                  trow2[:].rearrange("p (g o) -> p g o", o=1),
                  cum[:].rearrange("p (g x) -> p g x", x=32), AX.X, ALU.add)
            p.opw("vector", s_v, "tensor_scalar_add", trow2[:], trow2[:], -1.0)
            p.opw("vector", s_v, "reciprocal", trow[:], trow[:])
            p.opw("vector", s_v, "tensor_mul", trow2[:], trow2[:], trow[:])
            p.opw("vector", s_v, "tensor_sub",
                  cum[:].rearrange("p (g x) -> p g x", x=32),
                  sbat[:].rearrange("p (g x) -> p g x", x=32),
                  trow2[:].rearrange("p (g o) -> p g o", o=1)
                  .broadcast_to([128, 16, 32]))
            p.opw("vector", s_v, "tensor_scalar_max", cum[:], cum[:], 0.0)
            p.op("vector", "tensor_copy", pbf[:], cum[:], sem=s_v)
            p.wait("tensor", s_v)
            for k16 in range(16):
                lt = bat * 16 + k16
                ps, pi = ppt.get()
                p.op("tensor", "transpose", ps[0:32, 0:128],
                     pbf[:, 32 * k16:32 * k16 + 32], identbf[:], sem=s_t)
                p.wait("vector", s_t)
                p.op("vector", "tensor_copy",
                     atp[0:32, :].rearrange("p (t i) -> p t i", t=16)
                     [:, :, lt * 8:lt * 8 + 8],
                     ps[0:32, 0:128].rearrange("p (a b) -> p b a", b=16),
                     sem=s_v)
                ppt.done(pi, s_v)
        p.wait("tensor", s_v)
        for jc in range(8):
            for ib in range(2):
                ps, pi = pp.get()
                p.op("tensor", "matmul", ps[0:64, 0:512],
                     valstm[:, 64 * hl:64 * hl + 64],
                     atp[0:32, (2 * jc) * 1024 + 512 * ib:
                         (2 * jc) * 1024 + 512 * ib + 512],
                     start=True, stop=True)
                p.op("tensor", "matmul", ps[64:128, 0:512],
                     valstm[:, 64 * hl:64 * hl + 64],
                     atp[0:32, (2 * jc + 1) * 1024 + 512 * ib:
                         (2 * jc + 1) * 1024 + 512 * ib + 512],
                     start=True, stop=True, sem=s_t)
                p.wait("vector", s_t)
                p.op("vector", "tensor_copy",
                     opm[jc][:, 512 * ib:512 * ib + 512], ps[:, 0:512],
                     sem=s_v)
                pp.done(pi, s_v)
        p.wait("gpsimd", s_v)
        for i in range(8):
            p.dma("gpsimd", wkt[i][:], io["Wo1"][128 * i:128 * i + 128, :],
                  sem=s_w)
        p.wait("tensor", s_w)
        p.wait("tensor", s_v)
        for m in range(8):
            for ib in range(2):
                ps, pi = pp.get()
                for jc in range(8):
                    p.op("tensor", "matmul", ps[:, 0:512],
                         wkt[jc][:, 128 * m:128 * m + 128],
                         opm[jc][:, 512 * ib:512 * ib + 512],
                         start=(jc == 0), stop=(jc == 7),
                         sem=s_t if jc == 7 else None)
                p.wait("vector", s_t)
                p.op("vector", "tensor_scalar_add",
                     s1t[m][:, hl * 1024 + 512 * ib:hl * 1024 + 512 * ib + 512],
                     ps[:, 0:512], boc[:, m:m + 1], sem=s_v)
                pp.done(pi, s_v)
    p.wait("sync", s_v)
    p.scope_close(s_b)

    p.scope_open()
    xnb = [p.sbuf(f"L_xnb{i}", [128, 512], BF16) for i in range(8)]
    gwt = [p.sbuf(f"L_gwt{i}", [128, 8], BF16) for i in range(8)]
    mall = p.sbuf("L_mall", [128, 128], F32)
    mfm = p.sbuf("L_mfm", [8, TPC], BF16)
    ltile = p.sbuf("L_lt", [128, 8], F32)
    lcol = p.sbuf("L_lc", [128, 4], F32)
    ltile2 = p.sbuf("L_lt2", [128, 8], F32)
    ltile3 = p.sbuf("L_lt3", [128, 8], F32)
    w1t = [p.sbuf(f"L_w1t{i}", [128, 512], BF16) for i in range(16)]
    b2t = p.sbuf("L_b2t", [8, 1024], BF16)
    b1all = p.sbuf("L_b1all", [128, 32], F32)
    hsb = [p.sbuf(f"L_hsb{i}", [128, 512], BF16) for i in range(32)]
    mrep = p.sbuf("L_mrep", [128, 512], BF16)
    x1c = p.sbuf("L_x1c", [128, 512], F32)
    of = p.sbuf("L_of", [128, 512], F32)

    for i in range(8):
        p.dma("gpsimd", gwt[i][:], io["gate_w1"][128 * i:128 * i + 128, :],
              sem=s_w)
    p.dma("gpsimd", b2t[:], io["b2_1"], sem=s_w)
    GW_D = s_w.val
    p.wait("tensor", s_w, GW_D)
    for lt in range(TPC // 128):
        ps, pi = pp.get()
        for i in range(8):
            p.dma("sync", xnb[i][:, 0:128],
                  io["xnloc"][128 * i:128 * i + 128,
                              128 * lt:128 * lt + 128], sem=s_in)
            p.wait("tensor", s_in)
            p.op("tensor", "matmul", ps[:, 0:8], xnb[i][:, 0:128], gwt[i][:],
                 start=(i == 0), stop=(i == 7), sem=s_t)
            if i < 7:
                p.wait("sync", s_t, s_t.val)
        p.wait("vector", s_t)
        p.opw("vector", s_v, "tensor_copy", ltile[:], ps[:, 0:8])
        pp.done(pi, s_v)
        p.opw("vector", s_v, "reduce_max", lcol[:, 0:1], ltile[:], axis=AX.X)
        p.op("vector", "tensor_scalar_mul", lcol[:, 0:1], lcol[:, 0:1], -1.0,
             sem=s_v)
        p.wait("scalar", s_v)
        p.op("scalar", "activation", ltile2[:], ltile[:], AF.Exp,
             bias=lcol[:, 0:1], sem=s_sc)
        p.wait("vector", s_sc)
        p.opw("vector", s_v, "tensor_reduce", lcol[:, 1:2], ltile2[:], AX.X,
              ALU.add)
        p.opw("vector", s_v, "reciprocal", lcol[:, 1:2], lcol[:, 1:2])
        p.opw("vector", s_v, "tensor_scalar_mul", ltile2[:], ltile2[:],
              lcol[:, 1:2])
        p.opw("vector", s_v, "reduce_max", lcol[:, 2:3], ltile2[:], axis=AX.X)
        p.opw("vector", s_v, "tensor_scalar", ltile3[:], ltile2[:],
              lcol[:, 2:3], None, ALU.is_ge)
        p.opw("vector", s_v, "scalar_tensor_tensor", ltile[:], ltile3[:],
              -1e9, ltile2[:], op0=ALU.mult, op1=ALU.add)
        p.opw("vector", s_v, "reduce_max", lcol[:, 3:4], ltile[:], axis=AX.X)
        p.opw("vector", s_v, "tensor_scalar", ltile3[:], ltile2[:],
              lcol[:, 3:4], None, ALU.is_ge)
        p.opw("vector", s_v, "tensor_add", lcol[:, 2:3], lcol[:, 2:3],
              lcol[:, 3:4])
        p.opw("vector", s_v, "tensor_scalar_mul", mall[:, 8 * lt:8 * lt + 8],
              ltile3[:], lcol[:, 2:3])
        p.opw("vector", s_v, "tensor_copy", ltile3[:],
              mall[:, 8 * lt:8 * lt + 8])
        p.wait("tensor", s_v)
        ps2, pi2 = ppt.get()
        p.opw("vector", s_v, "tensor_copy", mrep[:, 0:8],
              mall[:, 8 * lt:8 * lt + 8])
        p.wait("tensor", s_v)
        p.op("tensor", "transpose", ps2[0:8, 0:128], mrep[:, 0:8],
             identbf[:], sem=s_t)
        p.wait("vector", s_t)
        p.op("vector", "tensor_copy", mfm[:, 128 * lt:128 * lt + 128],
             ps2[0:8, 0:128], sem=s_v)
        ppt.done(pi2, s_v)
    TOP2_D = s_v.val

    p.dma("sync", b1all[:], io["b1c"], sem=s_c)
    p.wait("scalar", s_c)
    for bi, (o, n) in enumerate(nblocks(TPC)):
        for i in range(8):
            p.dma("sync", xnb[i][:, 0:n],
                  io["xnloc"][128 * i:128 * i + 128, o:o + n], sem=s_in)
        B_IN = s_in.val
        p.wait("tensor", s_in, B_IN)
        for e in range(8):
            p.wait("gpsimd", s_t)
            for i in range(8):
                p.dma("gpsimd", w1t[(e % 2) * 8 + i][:],
                      io["w1_1"][1024 * e + 128 * i:1024 * e + 128 * i + 128, :],
                      sem=s_w)
            p.wait("tensor", s_w)
            for mch in range(4):
                ps, pi = pp.get()
                for i in range(8):
                    p.op("tensor", "matmul", ps[:, 0:n],
                         w1t[(e % 2) * 8 + i][:, 128 * mch:128 * mch + 128],
                         xnb[i][:, 0:n], start=(i == 0), stop=(i == 7),
                         sem=s_t if i == 7 else None)
                p.wait("scalar", s_t)
                p.op("scalar", "activation", hsb[4 * e + mch][:, 0:n],
                     ps[:, 0:n], AF.Gelu,
                     bias=b1all[:, 4 * e + mch:4 * e + mch + 1], sem=s_sc)
                pp.done(pi, s_sc)
            p.wait("vector", s_sc)
            for sub in range(n // 128):
                ltg = (o // 128) + sub
                p.opw("vector", s_v, "tensor_scalar_mul", diag[sub % 2][:],
                      identbf[:], mall[:, 8 * ltg + e:8 * ltg + e + 1])
                p.wait("tensor", s_v)
                ps, pi = pp.get()
                p.op("tensor", "matmul", ps[:, 0:128], ones128[:],
                     diag[sub % 2][:], start=True, stop=True, sem=s_t)
                p.wait("vector", s_t)
                p.op("vector", "tensor_copy",
                     mrep[:, 128 * sub:128 * sub + 128], ps[:, 0:128], sem=s_v)
                pp.done(pi, s_v)
            p.wait("vector", s_v)
            for mch in range(4):
                p.opw("vector", s_v, "tensor_mul", hsb[4 * e + mch][:, 0:n],
                      hsb[4 * e + mch][:, 0:n], mrep[:, 0:n])
        HS_D = s_v.val
        p.wait("gpsimd", s_v, HS_D)
        p.wait("tensor", s_v, HS_D)
        seq = [(m, e, mch) for m in range(8) for e in range(8)
               for mch in range(4)]
        W2_W0 = s_w.val
        trec = []

        def w2_load(k):
            m, e, mch = seq[k]
            if k >= 16:
                p.wait("gpsimd", s_t, trec[k - 16])
            p.dma("gpsimd", w1t[k % 16][:, 0:128],
                  io["w2_1"][512 * e + 128 * mch:512 * e + 128 * mch + 128,
                             128 * m:128 * m + 128], sem=s_w)

        PREF = 8
        for k in range(PREF):
            w2_load(k)
        ps = pi = None
        for k, (m, e, mch) in enumerate(seq):
            if k + PREF < len(seq):
                w2_load(k + PREF)
            if mch == 0 and e == 0:
                ps, pi = pp.get()
            p.wait("tensor", s_w)
            p.op("tensor", "matmul", ps[:, 0:n],
                 w1t[k % 16][:, 0:128], hsb[4 * e + mch][:, 0:n],
                 start=(e == 0 and mch == 0), stop=False, sem=s_t)
            trec.append(s_t.val)
            if e == 7 and mch == 3:
                p.op("tensor", "matmul", ps[:, 0:n],
                     b2t[:, 128 * m:128 * m + 128],
                     mfm[:, o:o + n], start=False, stop=True, sem=s_t)
                p.wait("vector", s_t)
                p.wait("sync", s_v)
                p.dma("sync", x1c[:, 0:n],
                      io["x1"][128 * m:128 * m + 128, o:o + n], sem=s_in)
                p.wait("scalar", s_in)
                p.wait("scalar", s_t)
                p.op("scalar", "activation", mrep[:, 0:n], ps[:, 0:n],
                     AF.Silu, sem=s_sc)
                pp.done(pi, s_sc)
                p.wait("vector", s_sc)
                p.opw("vector", s_v, "tensor_mul", of[:, 0:n], mrep[:, 0:n],
                      s1t[m][:, o:o + n])
                p.opw("vector", s_v, "tensor_add", of[:, 0:n], of[:, 0:n],
                      x1c[:, 0:n])
                p.wait("sync", s_v)
                p.dma("sync", io["outT"][128 * m:128 * m + 128, o:o + n],
                      of[:, 0:n], sem=s_out)
                p.wait("vector", s_out)
    p.wait("sync", s_out)
    p.scope_close(s_b)


STAGE_L_INS = {
    "gxn": ([DM, L], BF16), "xnloc": ([DM, TPC], BF16),
    "x1": ([DM, TPC], F32),
    "kpT": ([DM, 32], F32), "Wk1": ([DM, DM], F32), "bk1": ([1, DM], F32),
    "Wk1h": ([DM, 128], F32), "bk1h": ([1, 128], F32),
    "Wv1h": ([DM, 128], F32), "bv1h": ([1, 128], F32),
    "Wq1h": ([DM, 128], F32), "bq1h": ([128, 1], F32),
    "Wo1": ([DM, DM], F32), "bo1": ([128, 8], F32),
    "gate_w1": ([DM, 8], F32), "w1_1": ([8 * DM, 512], F32),
    "w2_1": ([8 * 512, DM], F32), "b1c": ([128, 32], F32),
    "b2_1": ([8, DM], F32),
    "kramp": ([128, 512], F32),
    "ones2": ([128, 2], BF16), "onesrf": ([1, 128], F32),
    "onesrb": ([1, 128], BF16), "ident": ([128, 128], BF16),
    "ones128": ([128, 128], BF16),
}
STAGE_L_OUTS = {"outT": ([DM, TPC], F32)}


def stage_l_inputs(inputs, gxn, x1, c):
    return {
        "gxn": gxn, "xnloc": np.ascontiguousarray(gxn[:, c * TPC:(c + 1) * TPC]),
        "x1": x1,
        **stage_l_ext(inputs, c),
    }


def stage_l_ext(inputs, c):
    kr = np.tile(np.arange(1, 33, dtype=np.float32), 16)[None, :].repeat(128, 0)
    sl = slice(128 * c, 128 * c + 128)
    return {
        "kpT": np.ascontiguousarray(
            np.broadcast_to(inputs["L1_attn_key_param"][0], (32, DM)).T),
        "Wk1": inputs["L1_attn_Wk"], "bk1": inputs["L1_attn_bk"].reshape(1, DM),
        "Wk1h": np.ascontiguousarray(inputs["L1_attn_Wk"][:, sl]),
        "bk1h": inputs["L1_attn_bk"][sl].reshape(1, 128),
        "Wv1h": np.ascontiguousarray(inputs["L1_attn_Wv"][:, sl]),
        "bv1h": inputs["L1_attn_bv"][sl].reshape(1, 128),
        "Wq1h": np.ascontiguousarray(inputs["L1_attn_Wq"][:, sl]),
        "bq1h": inputs["L1_attn_bq"][sl].reshape(128, 1),
        "Wo1": inputs["L1_attn_Wo"],
        "bo1": np.ascontiguousarray(inputs["L1_attn_bo"].reshape(8, 128).T),
        "gate_w1": inputs["L1_ffn_gate_w"],
        "w1_1": inputs["L1_ffn_w1"].reshape(8 * DM, 512),
        "w2_1": inputs["L1_ffn_w2"].reshape(8 * 512, DM),
        "b1c": np.ascontiguousarray(
            inputs["L1_ffn_b1"].reshape(8, 4, 128).transpose(2, 0, 1)
            .reshape(128, 32)),
        "b2_1": inputs["L1_ffn_b2"],
        "kramp": np.ascontiguousarray(kr),
        "ones2": np.ones((128, 2), BF),
        "onesrf": np.ones((1, 128), np.float32),
        "onesrb": np.ones((1, 128), BF),
        "ident": np.eye(128, dtype=BF),
        "ones128": np.ones((128, 128), BF),
    }


def build_stage_l():
    nc = bass.Bass("TRN2", target_bir_lowering=False)
    io = {}
    for k, (shape, dt) in STAGE_L_INS.items():
        io[k] = nc.dram_tensor(k, shape, dt, kind="ExternalInput").ap()
    for k, (shape, dt) in STAGE_L_OUTS.items():
        io[k] = nc.dram_tensor(k, shape, dt, kind="ExternalOutput").ap()
    p = Prog(nc)
    stage_l1(p, io)
    p.emit_and_finish()
    return nc


def stage_t2(p, io, fz=None):
    s_in = p.sem("t2in"); s_c = p.sem("t2c"); s_v = p.sem("t2v")
    s_t = p.sem("t2t"); s_sc = p.sem("t2sc"); s_out = p.sem("t2out")

    kh = p.sbuf("T2_kh", [128, L], BF16)
    vh = p.sbuf("T2_vh", [128, L], BF16)
    qh = p.sbuf("T2_qh", [128, L], BF16)
    pbuf = [p.sbuf(f"T2_pbuf{i}", [128, 1024], BF16) for i in range(2)]
    mbuf = [[p.sbuf(f"T2_mb{j}_{i}", [128, 512], BF16) for i in range(2)]
            for j in range(2)]
    masks = [p.sbuf(f"T2_mask{i}", [128, 512], BF16) for i in range(4)]
    denrep = p.sbuf("T2_denrep", [128, 512], BF16)
    denrow = p.sbuf("T2_denrow", [1, 512], F32)
    afin = p.sbuf("T2_afin", [128, 512], BF16)
    onesrf = p.sbuf("T2_onesrf", [1, 128], F32)
    onescol = p.sbuf("T2_ones", [128, 2], BF16)
    mrow = p.sbuf("T2_mrow", [1, 32], F32)
    mcol = p.sbuf("T2_mcol", [128, 1], F32)

    pssc = [p.psum(f"T2_sc{i}", [128, 1024]) for i in range(2)]
    pso = p.psum("T2_o", [128, 512])
    psd = p.psum("T2_d", [1, 512])

    for i in range(4):
        p.dma("sync", masks[i][:], io[f"mask{i}"], sem=s_c)
    p.dma("sync", onesrf[:], io["onesrf"], sem=s_c)
    p.dma("sync", onescol[:], io["ones2"], sem=s_c)
    if fz is None:
        p.dma("sync", mrow[0:1, 0:8], io["qmax8"], sem=s_c)
        p.dma("sync", mrow[0:1, 8:16], io["kmax8"], sem=s_c)
    else:
        for s in range(NCORE):
            p.dma("sync", mrow[0:1, s:s + 1], fz["Gqkm"][s:s + 1, 0:1],
                  sem=s_c)
            p.dma("sync", mrow[0:1, 8 + s:9 + s], fz["Gqkm"][s:s + 1, 1:2],
                  sem=s_c)
    p.dma("sync", kh[:], io["akh"], sem=s_in)
    p.dma("sync", qh[:], io["aqh"], sem=s_in)
    p.dma("sync", vh[:].rearrange("p (t d) -> p t d", d=128),
          io["avh"].rearrange("(t p) d -> p t d", p=128), sem=s_in)
    IN_D = s_in.val

    p.wait("vector", s_c)
    p.opw("vector", s_v, "reduce_max", mrow[0:1, 16:17], mrow[0:1, 0:8],
          axis=AX.X)
    p.opw("vector", s_v, "reduce_max", mrow[0:1, 17:18], mrow[0:1, 8:16],
          axis=AX.X)
    p.wait("scalar", s_v)
    p.op("scalar", "activation", mrow[0:1, 18:20], mrow[0:1, 16:18], AF.Sqrt,
         sem=s_sc)
    p.wait("vector", s_sc)
    p.opw("vector", s_v, "tensor_mul", mrow[0:1, 20:21], mrow[0:1, 18:19],
          mrow[0:1, 19:20])
    p.opw("vector", s_v, "tensor_scalar_mul", mrow[0:1, 21:22],
          mrow[0:1, 20:21], -1.0)
    p.wait("tensor", s_v)
    p.op("tensor", "matmul", pso[:, 0:1], onesrf[:], mrow[0:1, 21:22],
         start=True, stop=True, sem=s_t)
    p.wait("vector", s_t)
    p.op("vector", "tensor_copy", mcol[:], pso[:, 0:1], sem=s_v)
    p.wait("scalar", s_v)
    p.wait("tensor", s_v)

    p.wait("tensor", s_in, IN_D)
    p.wait("scalar", s_in, IN_D)
    for qs in range(L // 512):
        qs0 = qs * 512
        kt_hi = (qs0 + 512) // 128
        ngroups = (kt_hi + 1) // 2
        first = True

        def grp(g):
            return [kt for kt in (2 * g, 2 * g + 1) if kt < kt_hi]

        def qk(g):
            kts = grp(g)
            sc = pssc[g % 2]
            for gi, kt in enumerate(kts):
                p.op("tensor", "matmul", sc[:, gi * 512:gi * 512 + 512],
                     kh[:, kt * 128:kt * 128 + 128],
                     qh[:, qs0:qs0 + 512], start=True, stop=True,
                     sem=s_t if gi == len(kts) - 1 else None)
            return s_t.val

        qkv = qk(0)
        for g in range(ngroups):
            kts = grp(g)
            sc = pssc[g % 2]
            pb = pbuf[g % 2]
            qkv_cur = qkv
            if g + 1 < ngroups:
                qkv = qk(g + 1)
            p.wait("scalar", s_t, qkv_cur)
            width = len(kts) * 512
            p.op("scalar", "activation", pb[:, 0:width], sc[:, 0:width],
                 AF.Exp, bias=mcol[:, 0:1], sem=s_sc)
            av_src = []
            for gi, kt in enumerate(kts):
                off = kt * 128 - qs0
                if off >= 0:
                    p.wait("vector", s_sc)
                    p.op("vector", "tensor_mul", mbuf[g % 2][gi][:],
                         pb[:, gi * 512:gi * 512 + 512],
                         masks[off // 128][:], sem=s_v)
                    av_src.append(mbuf[g % 2][gi][:, 0:512])
                else:
                    av_src.append(pb[:, gi * 512:gi * 512 + 512])
            p.wait("tensor", s_sc)
            p.wait("tensor", s_v)
            for gi, kt in enumerate(kts):
                p.op("tensor", "matmul", pso[:, 0:512],
                     vh[:, kt * 128:kt * 128 + 128], av_src[gi],
                     start=first, stop=False)
                p.op("tensor", "matmul", psd[0:1, 0:512], onescol[:, 0:1],
                     av_src[gi], start=first, stop=False, sem=s_t)
                first = False
        p.wait("vector", s_t)
        p.opw("vector", s_v, "reciprocal", denrow[:], psd[0:1, 0:512])
        p.wait("tensor", s_v)
        p.op("tensor", "matmul", pssc[0][:, 0:512], onesrf[:], denrow[:],
             start=True, stop=True, sem=s_t)
        p.wait("vector", s_t)
        p.opw("vector", s_v, "tensor_copy", denrep[:], pssc[0][:, 0:512])
        p.op("vector", "tensor_mul", afin[:], pso[:, 0:512], denrep[:],
             sem=s_v)
        p.wait("sync", s_v)
        p.dma("sync", io["af"][:, qs0:qs0 + 512], afin[:], sem=s_out)
        p.wait("vector", s_out)
        p.wait("tensor", s_v)
        first = True
    p.wait("sync", s_out)


STAGE_T2_INS = {
    "aqh": ([128, L], BF16), "akh": ([128, L], BF16), "avh": ([L, 128], BF16),
    "qmax8": ([1, 8], F32), "kmax8": ([1, 8], F32),
    "onesrf": ([1, 128], F32), "ones2": ([128, 2], BF16),
    "mask0": ([128, 512], BF16), "mask1": ([128, 512], BF16),
    "mask2": ([128, 512], BF16), "mask3": ([128, 512], BF16),
}
STAGE_T2_OUTS = {"af": ([128, L], BF16)}


def stage_t2_inputs(aouts, c):
    aqh = np.concatenate([aouts["caq"][s][128 * c:128 * c + 128, :]
                          for s in range(NCORE)], axis=1)
    akh = np.concatenate([aouts["cak"][s][128 * c:128 * c + 128, :]
                          for s in range(NCORE)], axis=1)
    avh = np.concatenate([aouts["cav"][s][:, 128 * c:128 * c + 128]
                          for s in range(NCORE)], axis=0)
    qmax8 = np.concatenate([aouts["qkm"][s][:, 0:1] for s in range(NCORE)],
                           axis=1).astype(np.float32)
    kmax8 = np.concatenate([aouts["qkm"][s][:, 1:2] for s in range(NCORE)],
                           axis=1).astype(np.float32)
    d = {"aqh": np.ascontiguousarray(aqh), "akh": np.ascontiguousarray(akh),
         "avh": np.ascontiguousarray(avh), "qmax8": qmax8, "kmax8": kmax8,
         "onesrf": np.ones((1, 128), np.float32),
         "ones2": np.ones((128, 2), BF)}
    for off in range(4):
        j = np.arange(128)[:, None]
        i = np.arange(512)[None, :]
        d[f"mask{off}"] = (j + off * 128 <= i).astype(BF)
    return d


def build_stage_t2():
    nc = bass.Bass("TRN2", target_bir_lowering=False)
    io = {}
    for k, (shape, dt) in STAGE_T2_INS.items():
        io[k] = nc.dram_tensor(k, shape, dt, kind="ExternalInput").ap()
    for k, (shape, dt) in STAGE_T2_OUTS.items():
        io[k] = nc.dram_tensor(k, shape, dt, kind="ExternalOutput").ap()
    p = Prog(nc)
    stage_t2(p, io)
    p.emit_and_finish()
    return nc


_BUILD_CACHE = {}


def _get(name, fn):
    if name not in _BUILD_CACHE:
        _BUILD_CACHE[name] = fn()
    return _BUILD_CACHE[name]


def kernel_multi(**inputs):
    cores = list(range(NCORE))
    ncA = _get("A", build_stage_a)
    resA = run_bass_kernel_spmd(
        ncA, [stage_a_inputs(inputs, c) for c in cores], core_ids=cores)
    aouts = {k: [np.asarray(resA.results[c][k]) for c in cores]
             for k in STAGE_A_OUTS}
    ncD = _get("D", build_stage_d)
    resD = run_bass_kernel_spmd(
        ncD, [stage_d_inputs(aouts, c) for c in cores], core_ids=cores)
    do_full = np.concatenate([np.asarray(resD.results[c]["do_tm"])
                              for c in cores], axis=1)
    ncT = _get("T2", build_stage_t2)
    resT = run_bass_kernel_spmd(
        ncT, [stage_t2_inputs(aouts, c) for c in cores], core_ids=cores)
    af_full = np.concatenate([np.asarray(resT.results[c]["af"])
                              for c in cores], axis=0)
    ncM = _get("M", build_stage_m)
    in_maps = []
    for c in cores:
        gaf_own = np.ascontiguousarray(af_full[:, c * TPC:(c + 1) * TPC])
        in_maps.append(stage_m_inputs(inputs, aouts, do_full, gaf_own, c))
    resM = run_bass_kernel_spmd(ncM, in_maps, core_ids=cores)
    gxn = np.ascontiguousarray(np.concatenate(
        [np.asarray(resM.results[c]["xn2"]) for c in cores], axis=1))
    ncL = _get("L", build_stage_l)
    in_maps = []
    for c in cores:
        x1 = np.ascontiguousarray(np.asarray(resM.results[c]["x1"]))
        in_maps.append(stage_l_inputs(inputs, gxn, x1, c))
    resL = run_bass_kernel_spmd(ncL, in_maps, core_ids=cores)
    out = np.empty((L, DM), np.float32)
    for c in cores:
        out[c * TPC:(c + 1) * TPC] = np.asarray(resL.results[c]["outT"]).T
    return out.reshape(1, L, DM)


RG8 = [list(range(8))]


def build_fused(debug=False):
    nc = bass.Bass("TRN2", target_bir_lowering=False)
    ET = mybir.EngineType
    pid = nc.partition_id([ET.SP, ET.Activation, ET.Pool])
    hh = pid >> 1
    hf = pid & 1

    ext = {}

    def xin(name, shape, dt):
        if name not in ext:
            ext[name] = nc.dram_tensor(name, shape, dt,
                                       kind="ExternalInput").ap()
        return ext[name]

    def itn(name, shape, dt, shared=False):
        kw = {"addr_space": "Shared"} if shared else {}
        return nc.dram_tensor(name, shape, dt, kind="Internal", **kw)

    cq_i = itn("cq_i", [DM, TPC], BF16)
    ck_i = itn("ck_i", [DM, TPC], BF16)
    caq_i = itn("caq_i", [DM, TPC], BF16)
    cak_i = itn("cak_i", [DM, TPC], BF16)
    ckt_i = itn("ckt_i", [TPC, DM], BF16)
    cvt_i = itn("cvt_i", [TPC, DM], BF16)
    cav_i = itn("cav_i", [TPC, DM], BF16)
    cb_i = itn("cb_i", [4, TPC], BF16)
    qkm_i = itn("qkm_i", [1, 2], F32)
    dfm_i = itn("dfm_i", [128, L], BF16)
    af_i = itn("af_i", [128, L], BF16)
    x2_i = itn("x2_i", [DM, TPC], BF16)
    Gcq = itn("Gcq", [8 * DM, TPC], BF16, True)
    Gck = itn("Gck", [8 * DM, TPC], BF16, True)
    Gcaq = itn("Gcaq", [8 * DM, TPC], BF16, True)
    Gcak = itn("Gcak", [8 * DM, TPC], BF16, True)
    Gckt = itn("Gckt", [L, DM], BF16, True)
    Gcvt = itn("Gcvt", [L, DM], BF16, True)
    Gcav = itn("Gcav", [L, DM], BF16, True)
    Gcb = itn("Gcb", [32, TPC], BF16, True)
    Gqkm = itn("Gqkm", [8, 2], F32, True)
    Gdofm = itn("Gdofm", [DM, L], BF16, True)
    Gaf = itn("Gaf", [DM, L], BF16, True)
    Gx = itn("Gx", [8 * DM, TPC], BF16, True)
    ff_i = itn("ff_i", [1, 1], F32)
    Gff = itn("Gff", [8, 1], F32, True)
    qh_l = itn("qh_l", [256, L], BF16)
    kh_l = itn("kh_l", [256, L], BF16)
    ktm_l = itn("ktm_l", [L, 256], BF16)
    vtm_l = itn("vtm_l", [L, 128], BF16)
    brow_l = itn("brow_l", [1, L], F32)
    aqh_l = itn("aqh_l", [128, L], BF16)
    akh_l = itn("akh_l", [128, L], BF16)
    avh_l = itn("avh_l", [L, 128], BF16)
    gaf_l = itn("gaf_l", [DM, TPC], BF16)
    gdofm_l = itn("gdofm_l", [DM, TPC], BF16)
    qf_l = itn("qf_l", [DM, TPC], BF16)
    g16_l = itn("g16_l", [TPC, 16], F32)
    zx_l = itn("zx_l", [512, TPC], BF16)
    cv_l = itn("cv_l", [DM, TH - 3], BF16)
    x1_l = itn("x1_l", [DM, TPC], F32)
    bhcol_l = itn("bhcol_l", [128, NCHUNK], F32)
    dbg = {}
    for k, (shape, dt) in list(STAGE_A_OUTS.items()) + \
            list(STAGE_D_OUTS.items()):
        if k.startswith("dbg_"):
            dbg[k] = itn("f_" + k, shape, dt).ap()

    outT = nc.dram_tensor("outT", [DM, TPC], F32, kind="ExternalOutput").ap()

    fz = {"pid": pid, "h": hh, "half": hf, "Gqkm": Gqkm.ap(), "Gx": Gx.ap()}

    ioA = {k: xin(k, *v) for k, v in STAGE_A_INS.items()}
    ioA.update({"cq": cq_i.ap(), "ck": ck_i.ap(), "ckt": ckt_i.ap(),
                "cvt": cvt_i.ap(), "cv": cv_l.ap(), "cb": cb_i.ap(),
                "caq": caq_i.ap(), "cak": cak_i.ap(), "cav": cav_i.ap(),
                "qf": qf_l.ap(), "g16": g16_l.ap(), "zx": zx_l.ap(),
                "qkm": qkm_i.ap(), **dbg})
    ioD = {k: xin(k, *STAGE_D_INS[k])
           for k in ("ident", "ones2", "onesrf", "mnu1", "mnl1", "mu0")}
    ioD.update({"qh_fm": qh_l.ap(), "kh_fm": kh_l.ap(), "kh_tm": ktm_l.ap(),
                "vh_tm": vtm_l.ap(), "bh_row": brow_l.ap(),
                "bh_col": bhcol_l.ap(), "do_fm": dfm_i.ap(), **dbg})
    ioT = {k: xin(k, *STAGE_T2_INS[k])
           for k in ("onesrf", "ones2", "mask0", "mask1", "mask2", "mask3")}
    ioT.update({"aqh": aqh_l.ap(), "akh": akh_l.ap(), "avh": avh_l.ap(),
                "af": af_i.ap()})
    M_DEV = {"qf", "g16", "cv", "cvt", "gdo_fm", "gdo_tm", "gaf", "zx"}
    ioM = {k: xin(k, *v) for k, v in STAGE_M_INS.items() if k not in M_DEV}
    ioM.update({"qf": qf_l.ap(), "g16": g16_l.ap(), "cv": cv_l.ap(),
                "cvt": cvt_i.ap(), "zx": zx_l.ap(), "gaf": gaf_l.ap(),
                "gdo_fm": gdofm_l.ap(),
                "x1": x1_l.ap(), "xn2": x2_i.ap()})
    L_DEV = {"gxn", "xnloc", "x1"}
    ioL = {k: xin(k, *v) for k, v in STAGE_L_INS.items() if k not in L_DEV}
    ioL.update({"gxn": Gx.ap(), "xnloc": x2_i.ap(), "x1": x1_l.ap(),
                "outT": outT})

    p = Prog(nc)
    s_fb = p.sem("fbar")
    s_ag = p.sem("fag")
    s_st = p.sem("fst")

    def ag(t_in, t_out):
        p.op("gpsimd", "collective_compute", "AllGather",
             mybir.AluOpType.bypass, replica_groups=RG8,
             ins=[t_in.ap()], outs=[t_out.ap()], sem=s_ag)

    def ag_fence():
        for _ in range(2):
            p.wait("gpsimd", s_ag)
            p.op("gpsimd", "collective_compute", "AllGather",
                 mybir.AluOpType.bypass, replica_groups=RG8,
                 ins=[ff_i.ap()], outs=[Gff.ap()], sem=s_ag)
        p.wait("gpsimd", s_ag)

    def hook_tm(s_out):
        p.wait("gpsimd", s_out)
        for t_in, t_out in ((cq_i, Gcq), (ck_i, Gck), (ckt_i, Gckt),
                            (cvt_i, Gcvt), (cb_i, Gcb)):
            ag(t_in, t_out)

    def hook_attn(s_out):
        p.wait("gpsimd", s_out)
        for t_in, t_out in ((caq_i, Gcaq), (cak_i, Gcak), (cav_i, Gcav)):
            ag(t_in, t_out)

    p.scope_open()
    stage_a(p, ioA, hook_tm, hook_attn)
    p.scope_close(s_fb)
    ag(qkm_i, Gqkm)
    ag_fence()
    p.barrier(s_fb)

    s_stq = p.sem("fstq")
    s_sts = p.sem("fsts")
    s_stg = p.sem("fstg")
    fz["sem_q"], fz["sem_s"], fz["sem_g"] = s_stq, s_sts, s_stg
    fz["stv_q"], fz["stv_s"], fz["stv_g"] = [], [], []
    for s in range(NCORE):
        cols = slice(s * TPC, (s + 1) * TPC)
        p.dma("gpsimd", brow_l.ap()[0:1, cols],
              Gcb.ap().rearrange("(s r) c -> s r c", r=4)[
                  s:s + 1, bass.ds(hh, 1), :].rearrange("s r c -> (s r) c"),
              sem=s_stg)
        p.dma("sync", qh_l.ap()[:, cols],
              Gcq.ap().rearrange("(s r) c -> s r c", r=DM)[
                  s:s + 1, bass.ds(256 * hh, 256), :].rearrange(
                  "s r c -> (s r) c"), sem=s_stq)
        p.dma("scalar", kh_l.ap()[:, cols],
              Gck.ap().rearrange("(s r) c -> s r c", r=DM)[
                  s:s + 1, bass.ds(256 * hh, 256), :].rearrange(
                  "s r c -> (s r) c"), sem=s_sts)
        p.dma("sync", ktm_l.ap()[cols, :],
              Gckt.ap()[cols, bass.ds(256 * hh, 256)], sem=s_stq)
        p.dma("scalar", vtm_l.ap()[cols, :],
              Gcvt.ap()[cols, bass.ds(256 * hh + 128 * hf, 128)], sem=s_sts)
        fz["stv_q"].append(s_stq.val)
        fz["stv_s"].append(s_sts.val)
        fz["stv_g"].append(s_stg.val)
    p.scope_open()
    stage_delta(p, ioD, fz)
    for ha in range(2):
        rows = slice(ha * (L // 2), (ha + 1) * (L // 2))
        p.dma("gpsimd", avh_l.ap()[rows, :],
              Gcav.ap()[rows, bass.ds(128 * pid, 128)], sem=s_st)
    p.dma("gpsimd", aqh_l.ap().rearrange("r (s c) -> s r c", c=TPC),
          Gcaq.ap().rearrange("(s r) c -> s r c", r=DM)[
              :, bass.ds(128 * pid, 128), :], sem=s_st)
    p.dma("gpsimd", akh_l.ap().rearrange("r (s c) -> s r c", c=TPC),
          Gcak.ap().rearrange("(s r) c -> s r c", r=DM)[
              :, bass.ds(128 * pid, 128), :], sem=s_st)
    VAL_T = s_st.val
    p.scope_close(s_fb)
    p.wait("sync", s_st, VAL_T)
    ag(dfm_i, Gdofm)

    p.scope_open()
    stage_t2(p, ioT, fz)
    p.scope_close(s_fb)
    ag(af_i, Gaf)

    def hook_a2():
        ag_fence()
        p.dma("sync", gaf_l.ap(), Gaf.ap()[:, bass.ts(pid, TPC)], sem=s_st)
        p.wait("sync", s_st)
        p.dma("sync", gdofm_l.ap(), Gdofm.ap()[:, bass.ts(pid, TPC)],
              sem=s_st)
        p.wait("sync", s_st)

    p.scope_open()
    stage_m(p, ioM, hook_a2=hook_a2)
    p.scope_close(s_fb)
    ag(x2_i, Gx)
    ag_fence()
    p.barrier(s_fb)

    p.scope_open()
    stage_l1(p, ioL, fz)
    p.scope_close(s_fb)

    if debug:
        dumps = {"qh_l": qh_l, "kh_l": kh_l, "ktm_l": ktm_l, "vtm_l": vtm_l,
                 "brow_l": brow_l, "bhcol_l": bhcol_l, "aqh_l": aqh_l,
                 "akh_l": akh_l, "avh_l": avh_l, "gaf_l": gaf_l,
                 "gdofm_l": gdofm_l,
                 "dfm_i": dfm_i, "af_i": af_i, "x2_i": x2_i, "x1_l": x1_l,
                 "qkm_i": qkm_i, "Gqkm": Gqkm, "cq_i": cq_i, "cav_i": cav_i}
        s_dbg = p.sem("fdbg")
        for nm, t in dumps.items():
            o = nc.dram_tensor("dump_" + nm, t.shape, t.dtype,
                               kind="ExternalOutput").ap()
            p.dma("sync", o, t.ap(), sem=s_dbg)
        p.wait("sync", s_dbg)
    p.emit_and_finish()
    return nc


def fused_inputs(inputs, c):
    m = dict(stage_a_inputs(inputs, c))
    iu = np.triu(np.ones((128, 128), np.float32), 1)
    il = np.tril(np.ones((128, 128), np.float32), -1)
    m.update({"mnu1": -iu, "mnl1": -il,
              "mu0": np.triu(np.ones((128, 128), np.float32))})
    for off in range(4):
        j = np.arange(128)[:, None]
        i = np.arange(512)[None, :]
        m[f"mask{off}"] = (j + off * 128 <= i).astype(BF)
    m.update(stage_m_ext(inputs, c))
    m.update(stage_l_ext(inputs, c))
    m.setdefault("onesrb", np.ones((1, 128), BF))
    m.setdefault("ones128", np.ones((128, 128), BF))
    return m


def kernel(**inputs):
    cores = list(range(NCORE))
    ncF = _get("F", build_fused)
    res = run_bass_kernel_spmd(ncF, [fused_inputs(inputs, c) for c in cores],
                               core_ids=cores)
    out = np.empty((L, DM), np.float32)
    for c in cores:
        out[c * TPC:(c + 1) * TPC] = np.asarray(res.results[c]["outT"]).T
    return out.reshape(1, L, DM)


# version 21 (final)
# speedup vs baseline: 1.1715x; 1.1715x over previous
import contextlib
import numpy as np
import ml_dtypes

import concourse.bass as bass
import concourse.mybir as mybir
from concourse.bass_utils import run_bass_kernel_spmd

F32 = mybir.dt.float32
BF16 = mybir.dt.bfloat16
AF = mybir.ActivationFunctionType
ALU = mybir.AluOpType
AX = mybir.AxisListType
BF = ml_dtypes.bfloat16

NCORE = 8
L = 16384
TPC = L // NCORE
DM = 1024
HALO = 33
TH = TPC + HALO
THP = 2112
ENGINES = ("sync", "tensor", "vector", "scalar", "gpsimd")


class Sem:
    def __init__(self, h):
        self.h = h
        self.val = 0


class Prog:
    def __init__(self, nc):
        self.nc = nc
        self.ops = {e: [] for e in ENGINES}
        self.stack = contextlib.ExitStack()
        self._semn = 0
        self.eng = {"sync": nc.sync, "tensor": nc.tensor, "vector": nc.vector,
                    "scalar": nc.scalar, "gpsimd": nc.gpsimd}

    def _alloc_stack(self):
        sc = getattr(self, "_scopes", None)
        return sc[-1] if sc else self.stack

    def sbuf(self, name, shape, dtype):
        return self._alloc_stack().enter_context(
            self.nc.sbuf_tensor(name, shape, dtype))

    def scope_open(self):
        if not hasattr(self, "_scopes"):
            self._scopes = []
        self._scopes.append(contextlib.ExitStack())

    def scope_close(self, barrier_sem):
        self._scopes.pop().close()
        self._ccols = {}
        self.barrier(barrier_sem)

    def barrier(self, barrier_sem):
        base = barrier_sem.val
        for e in ENGINES:
            self.op(e, "nop", sem=barrier_sem)
        for e in ENGINES:
            self.wait(e, barrier_sem, base + len(ENGINES))

    def psum(self, name, shape, dtype=F32):
        return self._alloc_stack().enter_context(
            self.nc.psum_tensor(name, shape, dtype))

    def sem(self, name=None):
        self._semn += 1
        return Sem(self.stack.enter_context(
            self.nc.semaphore(name or f"s{self._semn}")))

    def on(self, engine, fn):
        self.ops[engine].append(fn)

    def wait(self, engine, sem, val=None):
        v = sem.val if val is None else val
        self.on(engine, lambda e=engine, s=sem.h, vv=v: self.eng[e].wait_ge(s, vv))

    def dma(self, engine, out, in_, sem=None, **kw):
        if sem is not None:
            sem.val += 16
        def f(out=out, in_=in_, sem=sem, kw=kw, e=engine):
            inst = self.eng[e].dma_start(out=out, in_=in_, **kw)
            if sem is not None:
                inst.then_inc(sem.h, 16)
        self.on(engine, f)

    def op(self, engine, method, *args, sem=None, semn=1, **kw):
        if sem is not None:
            sem.val += semn
        def f(args=args, kw=kw, sem=sem, semn=semn, e=engine, m=method):
            inst = getattr(self.eng[e], m)(*args, **kw)
            if sem is not None:
                inst.then_inc(sem.h, semn)
        self.on(engine, f)

    def opw(self, engine, sem, method, *args, **kw):
        self.op(engine, method, *args, sem=sem, **kw)
        self.wait(engine, sem)

    def constcol(self, value, sem):
        if not hasattr(self, "_ccols"):
            self._ccols = {}
        if value not in self._ccols:
            self._ccn = getattr(self, "_ccn", 0) + 1
            t = self.sbuf(f"cc{self._ccn}", [128, 1], F32)
            self.op("vector", "memset", t[:], float(value), sem=sem)
            self._ccols[value] = t
        return self._ccols[value]

    def emit_and_finish(self):
        with self.nc.Block() as block:
            for eng_name in ENGINES:
                thunks = self.ops[eng_name]
                getattr(block, eng_name)(lambda e, th=thunks: [f() for f in th])
        self.stack.close()


class PP:

    def __init__(self, p, name, n, shape, dtype=F32, writer="tensor"):
        self.p = p
        self.tiles = [p.psum(f"{name}{i}", shape, dtype) for i in range(n)]
        self.free_at = [None] * n
        self.idx = 0
        self.writer = writer

    def get(self):
        i = self.idx % len(self.tiles)
        self.idx += 1
        if self.free_at[i] is not None:
            sem, val = self.free_at[i]
            self.p.wait(self.writer, sem, val)
            self.free_at[i] = None
        return self.tiles[i], i

    def done(self, i, sem):
        self.free_at[i] = (sem, sem.val)


def nblocks(total, blk=512, base=0):
    out = []
    o = 0
    while o < total:
        n = min(blk, total - o)
        out.append((base + o, n))
        o += n
    return out


def stage_a(p, io, hook_tm=None, hook_attn=None):
    nc = p.nc
    xb = [p.sbuf(f"A_xb{i}", [128, THP], BF16) for i in range(8)]
    xn = [p.sbuf(f"A_xn{i}", [128, THP], BF16) for i in range(8)]
    sqb = [p.sbuf(f"A_sqb{i}", [128, 512], BF16) for i in range(2)]
    rowbuf = p.sbuf("A_rows", [1, 3 * THP], F32)
    brep = [p.sbuf(f"A_brep{i}", [128, THP], BF16) for i in range(2)]
    onescol = p.sbuf("A_ones", [128, 2], BF16)
    onesrf = p.sbuf("A_onesrf", [1, 128], F32)
    ident = p.sbuf("A_ident", [128, 128], BF16)
    w1t = [p.sbuf(f"A_w1t{i}", [128, 1024], BF16) for i in range(8)]
    w2t = [p.sbuf(f"A_w2t{i}", [128, 1024], BF16) for i in range(8)]
    win = xb
    cwq = [p.sbuf(f"A_cwq{i}", [128, 4], F32) for i in range(8)]
    cwk = [p.sbuf(f"A_cwk{i}", [128, 4], F32) for i in range(8)]
    cwv = [p.sbuf(f"A_cwv{i}", [128, 4], F32) for i in range(8)]
    wbt = [p.sbuf(f"A_wbt{i}", [128, 4], BF16) for i in range(8)]
    gw1x = [p.sbuf(f"A_gw1x{i}", [128, 512], BF16) for i in range(8)]
    zraw = p.sbuf("A_zraw", [128, THP], BF16)
    zconv = p.sbuf("A_zconv", [128, THP], BF16)
    zfin = p.sbuf("A_zfin", [128, THP], BF16)
    ttmp = p.sbuf("A_ttmp", [128, 512], BF16)
    cosq = p.sbuf("A_cosq", [64, TPC], BF16)
    sinq = p.sbuf("A_sinq", [64, TPC], BF16)
    cosk = p.sbuf("A_cosk", [64, TPC], BF16)
    sink = p.sbuf("A_sink", [64, TPC], BF16)
    rtile = p.sbuf("A_rt", [128, 16], F32)
    rtile2 = p.sbuf("A_rt2", [128, 16], F32)
    rtile3 = p.sbuf("A_rt3", [128, 16], F32)
    rcol = p.sbuf("A_rcol", [128, 2], F32)
    gtm = p.sbuf("A_gtm", [128, 16], F32)

    pp = PP(p, "A_pp", 6, [128, 512])
    pps = pp
    ppb = pp
    ppt = PP(p, "A_ppt", 2, [128, 512], BF16)
    ppg = pp

    s_x = p.sem("sx"); s_c = p.sem("sc"); s_w = p.sem("sw")
    s_v = p.sem("sv"); s_t = p.sem("st"); s_sc = p.sem("ssc")
    s_out = p.sem("sout")

    p.dma("sync", onescol[:], io["ones2"], sem=s_c)
    p.dma("sync", onesrf[:], io["onesrf"], sem=s_c)
    p.dma("sync", ident[:], io["ident"], sem=s_c)
    p.dma("sync", cosq[:], io["cosq"], sem=s_c)
    p.dma("sync", sinq[:], io["sinq"], sem=s_c)
    p.dma("sync", cosk[:], io["cosk"], sem=s_c)
    p.dma("sync", sink[:], io["sink"], sem=s_c)
    for i in range(8):
        p.dma("gpsimd", xb[i][:, 0:TH], io["xT"][128 * i:128 * i + 128, :],
              sem=s_x)
    X_DONE = s_x.val

    def load_w(dst, src, cols, sem):
        for i in range(8):
            p.dma("gpsimd", dst[i][:, 0:cols],
                  src[128 * i:128 * i + 128, 0:cols], sem=sem)
    load_w(w1t, io["Wq"], 1024, s_w)
    load_w(w2t, io["Wk"], 1024, s_w)
    load_w(cwq, io["qc_w"], 4, s_w)
    load_w(cwk, io["kc_w"], 4, s_w)
    load_w(cwv, io["vc_w"], 4, s_w)
    load_w(wbt, io["Wb"], 4, s_w)
    W_QK = s_w.val

    for i in range(8):
        p.wait("vector", s_x, 16 * (i + 1))
        p.op("vector", "memset", xb[i][:, TH:THP], 0.0, sem=s_v)
    PAD_Z = s_v.val
    p.wait("tensor", s_c)
    p.wait("tensor", s_v, PAD_Z)
    for (o, n) in nblocks(THP):
        ps, pi = pps.get()
        for i in range(8):
            p.op("tensor", "matmul", ps[0:1, 0:n], onescol[:, 0:1],
                 xb[i][:, o:o + n], start=(i == 0), stop=(i == 7),
                 sem=s_t if i == 7 else None)
        p.wait("scalar", s_t)
        p.op("scalar", "copy", rowbuf[0:1, o:o + n], ps[0:1, 0:n], sem=s_sc)
        pps.done(pi, s_sc)
    SSQ = THP
    sq_lastc = [None, None]
    for (o, n) in nblocks(THP):
        ps, pi = pps.get()
        for i in range(8):
            b = i % 2
            sq = sqb[b]
            if sq_lastc[b] is not None:
                p.wait("vector", s_t, sq_lastc[b])
            p.op("vector", "tensor_mul", sq[:, 0:n], xb[i][:, o:o + n],
                 xb[i][:, o:o + n], sem=s_v)
            p.wait("tensor", s_v)
            p.op("tensor", "matmul", ps[0:1, 0:n], onescol[:, 0:1], sq[:, 0:n],
                 start=(i == 0), stop=(i == 7), sem=s_t)
            sq_lastc[b] = s_t.val
        p.wait("scalar", s_t)
        p.op("scalar", "copy", rowbuf[0:1, SSQ + o:SSQ + o + n], ps[0:1, 0:n],
             sem=s_sc)
        pps.done(pi, s_sc)
    R0, R1, R2 = 0, THP, 2 * THP
    p.wait("vector", s_sc)
    p.opw("vector", s_v, "tensor_scalar_mul", rowbuf[0:1, R0:R0 + THP],
          rowbuf[0:1, R0:R0 + THP], 1.0 / DM)
    p.opw("vector", s_v, "tensor_mul", rowbuf[0:1, R2:R2 + THP],
          rowbuf[0:1, R0:R0 + THP], rowbuf[0:1, R0:R0 + THP])
    p.opw("vector", s_v, "tensor_scalar_mul", rowbuf[0:1, R1:R1 + THP],
          rowbuf[0:1, R1:R1 + THP], 1.0 / DM)
    p.op("vector", "tensor_sub", rowbuf[0:1, R1:R1 + THP],
         rowbuf[0:1, R1:R1 + THP], rowbuf[0:1, R2:R2 + THP], sem=s_v)
    p.wait("scalar", s_v)
    eps_c = p.constcol(1e-5, s_v)
    p.wait("scalar", s_v)
    p.op("scalar", "activation", rowbuf[0:1, R1:R1 + THP],
         rowbuf[0:1, R1:R1 + THP], AF.Sqrt, bias=eps_c[0:1, 0:1], sem=s_sc)
    p.wait("vector", s_sc)
    p.opw("vector", s_v, "reciprocal", rowbuf[0:1, R1:R1 + THP],
          rowbuf[0:1, R1:R1 + THP])
    p.opw("vector", s_v, "tensor_mul", rowbuf[0:1, R2:R2 + THP],
          rowbuf[0:1, R0:R0 + THP], rowbuf[0:1, R1:R1 + THP])
    p.op("vector", "tensor_scalar_mul", rowbuf[0:1, R2:R2 + THP],
         rowbuf[0:1, R2:R2 + THP], -1.0, sem=s_v)
    ROWS_BF = s_v.val
    p.wait("tensor", s_v, ROWS_BF)
    for r in range(2):
        base = R1 if r == 0 else R2
        for (o, n) in nblocks(THP):
            ps, pi = ppb.get()
            p.op("tensor", "matmul", ps[:, 0:n], onesrf[:],
                 rowbuf[0:1, base + o:base + o + n], start=True,
                 stop=True, sem=s_t)
            p.wait("vector", s_t)
            p.op("vector", "tensor_copy", brep[r][:, o:o + n], ps[:, 0:n],
                 sem=s_v)
            ppb.done(pi, s_v)
    for i in range(8):
        p.opw("vector", s_v, "tensor_mul", xn[i][:], xb[i][:], brep[0][:])
        p.op("vector", "tensor_add", xn[i][:], xn[i][:], brep[1][:], sem=s_v)
    XN_DONE = s_v.val

    def proj_conv(wtiles, cwt, out_fm, out_tm, with_halo_out, wait_w):
        p.wait("tensor", s_w, wait_w)
        p.wait("tensor", s_v, XN_DONE)
        lo = 3 if with_halo_out else HALO
        for m in range(8):
            for (o, n) in nblocks(THP):
                ps, pi = pp.get()
                for i in range(8):
                    p.op("tensor", "matmul", ps[:, 0:n],
                         wtiles[i][:, 128 * m:128 * m + 128],
                         xn[i][:, o:o + n], start=(i == 0), stop=(i == 7),
                         sem=s_t if i == 7 else None)
                p.wait("scalar", s_t)
                p.op("scalar", "copy", zraw[:, o:o + n], ps[:, 0:n], sem=s_sc)
                pp.done(pi, s_sc)
            p.wait("vector", s_sc)
            if m == 0 and "dbg_zraw" in io:
                p.wait("sync", s_sc)
                p.dma("sync", io["dbg_zraw"], zraw[:, 0:THP], sem=s_out)
                p.wait("vector", s_out)
            p.opw("vector", s_v, "tensor_scalar_mul", zconv[:, lo:TH],
                  zraw[:, lo - 3:TH - 3], cwt[m][:, 0:1])
            for j in range(1, 4):
                p.opw("vector", s_v, "scalar_tensor_tensor", zconv[:, lo:TH],
                      zraw[:, lo - 3 + j:TH - 3 + j], cwt[m][:, j:j + 1],
                      zconv[:, lo:TH], op0=ALU.mult, op1=ALU.add)
            p.op("vector", "nop", sem=s_v)
            if m == 0 and "dbg_zconv" in io:
                p.wait("sync", s_v)
                p.dma("sync", io["dbg_zconv"], zconv[:, 0:THP], sem=s_out)
            p.wait("scalar", s_v)
            p.op("scalar", "activation", zfin[:, lo:TH], zconv[:, lo:TH],
                 AF.Silu, sem=s_sc)
            p.wait("sync", s_sc)
            p.dma("sync", out_fm[128 * m:128 * m + 128, :], zfin[:, lo:TH],
                  sem=s_out)
            if out_tm is not None:
                p.wait("tensor", s_sc)
                for tg in range(TPC // 512):
                    ps, pi = ppt.get()
                    for j in range(4):
                        tt = 4 * tg + j
                        p.op("tensor", "transpose",
                             ps[:, 128 * j:128 * j + 128],
                             zfin[:, HALO + 128 * tt:HALO + 128 * tt + 128],
                             ident[:], sem=s_t if j == 3 else None)
                    if tg > 0:
                        p.wait("vector", s_out)
                    p.wait("vector", s_t)
                    p.op("vector", "tensor_copy", ttmp[:], ps[:, 0:512],
                         sem=s_v)
                    ppt.done(pi, s_v)
                    p.wait("sync", s_v)
                    p.dma("sync",
                          out_tm[512 * tg:512 * tg + 512,
                                 128 * m:128 * m + 128].rearrange(
                              "(t p) d -> p t d", p=128),
                          ttmp[:].rearrange("p (t d) -> p t d", d=128),
                          sem=s_out)
            p.wait("scalar", s_out)
            p.wait("vector", s_out)

    proj_conv(w1t, cwq, io["cq"], None, False, W_QK)
    p.wait("gpsimd", s_t)
    load_w(w1t, io["Wv"], 1024, s_w)
    W_V = s_w.val
    proj_conv(w2t, cwk, io["ck"], io["ckt"], False, W_QK)
    proj_conv(w1t, cwv, io["cv"], io["cvt"], True, W_V)

    for (o, n) in nblocks(TPC):
        ps, pi = pp.get()
        for i in range(8):
            p.op("tensor", "matmul", ps[0:4, 0:n], wbt[i][:, 0:4],
                 xn[i][:, HALO + o:HALO + o + n], start=(i == 0),
                 stop=(i == 7), sem=s_t if i == 7 else None)
        p.wait("scalar", s_t)
        p.op("scalar", "activation", zfin[0:4, o:o + n], ps[0:4, 0:n],
             AF.Sigmoid, sem=s_sc)
        pp.done(pi, s_sc)
    p.wait("sync", s_sc)
    p.dma("sync", io["cb"], zfin[0:4, 0:TPC], sem=s_out)
    if hook_tm is not None:
        hook_tm(s_out)

    def win_mm(m_local, dest_fn):
        for (o, n) in nblocks(TPC):
            ps, pi = pp.get()
            for i in range(8):
                p.op("tensor", "matmul", ps[:, 0:n],
                     win[i][:, 128 * m_local:128 * m_local + 128],
                     xn[i][:, HALO + o:HALO + o + n], start=(i == 0),
                     stop=(i == 7), sem=s_t if i == 7 else None)
            p.wait("scalar", s_t)
            p.op("scalar", "copy", zraw[:, o:o + n], ps[:, 0:n], sem=s_sc)
            pp.done(pi, s_sc)

    rsc = p.sbuf("A_rsc", [64, TPC], BF16)
    sqa = p.sbuf("A_sqa", [64, TPC], BF16)
    nrow = p.sbuf("A_nrow", [1, TPC], F32)
    mxrow = p.sbuf("A_mxrow", [1, 4], F32)
    p.op("vector", "memset", mxrow[:], 0.0, sem=s_v)
    def rope_and_store(dst, hh, cosz, sinz, mxslot):
        p.wait("vector", s_sc)
        p.op("vector", "tensor_mul", zfin[0:64, 0:TPC], zraw[0:64, 0:TPC],
             cosz[:])
        p.opw("vector", s_v, "tensor_copy", zconv[0:64, 0:TPC],
              zraw[64:128, 0:TPC])
        p.opw("vector", s_v, "tensor_mul", rsc[:], zconv[0:64, 0:TPC],
              sinz[:])
        p.opw("vector", s_v, "tensor_sub", zfin[0:64, 0:TPC],
              zfin[0:64, 0:TPC], rsc[:])
        p.opw("vector", s_v, "tensor_mul", rsc[:], zconv[0:64, 0:TPC],
              cosz[:])
        p.opw("vector", s_v, "tensor_mul", zconv[0:64, 0:TPC],
              zraw[0:64, 0:TPC], sinz[:])
        p.op("vector", "tensor_add", zconv[0:64, 0:TPC], zconv[0:64, 0:TPC],
             rsc[:], sem=s_v)
        p.wait("sync", s_v)
        p.dma("sync", dst[128 * hh:128 * hh + 64, :], zfin[0:64, 0:TPC],
              sem=s_out)
        p.dma("sync", dst[128 * hh + 64:128 * hh + 128, :],
              zconv[0:64, 0:TPC], sem=s_out)
        p.opw("vector", s_v, "tensor_mul", sqa[:], zfin[0:64, 0:TPC],
              zfin[0:64, 0:TPC])
        p.op("vector", "tensor_mul", rsc[:], zconv[0:64, 0:TPC],
             zconv[0:64, 0:TPC], sem=s_v)
        p.wait("tensor", s_v)
        for (o, n) in nblocks(TPC):
            ps, pi = pp.get()
            p.op("tensor", "matmul", ps[0:1, 0:n], onescol[0:64, 0:1],
                 sqa[:, o:o + n], start=True, stop=False)
            p.op("tensor", "matmul", ps[0:1, 0:n], onescol[0:64, 0:1],
                 rsc[:, o:o + n], start=False, stop=True, sem=s_t)
            p.wait("vector", s_t)
            p.op("vector", "tensor_copy", nrow[0:1, o:o + n], ps[0:1, 0:n],
                 sem=s_v)
            pp.done(pi, s_v)
        p.opw("vector", s_v, "reduce_max", mxrow[0:1, 2:3], nrow[:], axis=AX.X)
        p.opw("vector", s_v, "tensor_max", mxrow[0:1, mxslot:mxslot + 1],
              mxrow[0:1, mxslot:mxslot + 1], mxrow[0:1, 2:3])
        p.wait("vector", s_out)
        p.wait("scalar", s_out)

    p.wait("gpsimd", s_v, XN_DONE)
    p.wait("gpsimd", s_t)
    for i in range(8):
        p.dma("gpsimd", win[i][:, 0:2048],
              io["w_inT"][128 * i:128 * i + 128, 0:2048], sem=s_w)
    W_H0 = s_w.val
    p.wait("tensor", s_w, W_H0)
    p.wait("sync", s_out)
    for m in range(0, 8):
        win_mm(m, None)
        p.wait("sync", s_sc)
        p.dma("sync", io["qf"][128 * m:128 * m + 128, :], zraw[:, 0:TPC],
              sem=s_out)
        p.wait("scalar", s_out)
    for m in range(8, 16):
        win_mm(m, None)
        rope_and_store(io["caq"], m - 8, cosq, sinq, 0)
    p.wait("gpsimd", s_t)
    for i in range(8):
        p.dma("gpsimd", win[i][:, 0:2064],
              io["w_inT"][128 * i:128 * i + 128, 2048:4112], sem=s_w)
    W_H1 = s_w.val
    p.wait("tensor", s_w, W_H1)
    for m in range(0, 8):
        win_mm(m, None)
        rope_and_store(io["cak"], m, cosk, sink, 1)
    for m in range(8, 16):
        hh = m - 8
        win_mm(m, None)
        p.wait("tensor", s_sc)
        for tg in range(TPC // 512):
            ps, pi = ppt.get()
            for j in range(4):
                tt = 4 * tg + j
                p.op("tensor", "transpose", ps[:, 128 * j:128 * j + 128],
                     zraw[:, 128 * tt:128 * tt + 128], ident[:],
                     sem=s_t if j == 3 else None)
            if tg > 0:
                p.wait("vector", s_out)
            p.wait("vector", s_t)
            p.op("vector", "tensor_copy", ttmp[:], ps[:, 0:512], sem=s_v)
            ppt.done(pi, s_v)
            p.wait("sync", s_v)
            p.dma("sync",
                  io["cav"][512 * tg:512 * tg + 512,
                            128 * hh:128 * hh + 128].rearrange(
                      "(t p) d -> p t d", p=128),
                  ttmp[:].rearrange("p (t d) -> p t d", d=128), sem=s_out)
        p.wait("scalar", s_out)
    if hook_attn is not None:
        hook_attn(s_out)
    for lt in range(TPC // 128):
        ps, pi = ppg.get()
        for i in range(8):
            p.op("tensor", "matmul", ps[:, 0:16],
                 xn[i][:, HALO + 128 * lt:HALO + 128 * lt + 128],
                 win[i][:, 2048:2064], start=(i == 0), stop=(i == 7),
                 sem=s_t if i == 7 else None)
        p.wait("vector", s_t)
        p.opw("vector", s_v, "tensor_copy", rtile[:], ps[:, 0:16])
        ppg.done(pi, s_v)
        p.opw("vector", s_v, "reduce_max", rcol[:, 0:1], rtile[:], axis=AX.X)
        p.opw("vector", s_v, "tensor_scalar", rtile2[:], rtile[:],
              rcol[:, 0:1], None, ALU.is_ge)
        p.opw("vector", s_v, "scalar_tensor_tensor", rtile3[:], rtile2[:],
              -1e9, rtile[:], op0=ALU.mult, op1=ALU.add)
        p.opw("vector", s_v, "reduce_max", rcol[:, 1:2], rtile3[:], axis=AX.X)
        p.op("vector", "tensor_scalar", rtile2[:], rtile[:], rcol[:, 1:2],
             None, ALU.is_ge, sem=s_v)
        p.wait("scalar", s_v)
        p.op("scalar", "activation", rtile[:], rtile[:], AF.Sigmoid, sem=s_sc)
        p.wait("vector", s_sc)
        p.wait("vector", s_v)
        p.op("vector", "tensor_mul", gtm[:], rtile[:], rtile2[:], sem=s_v)
        p.wait("sync", s_v)
        p.dma("sync", io["g16"][128 * lt:128 * lt + 128, :], gtm[:], sem=s_out)
        p.wait("vector", s_out)
        p.wait("scalar", s_out)

    for i in range(8):
        p.dma("gpsimd", gw1x[i][:], io["gW1"][128 * i:128 * i + 128, 0:512],
              sem=s_w)
    p.wait("tensor", s_w)
    for m in range(4):
        for (o, n) in nblocks(TPC):
            ps, pi = pp.get()
            for i in range(8):
                p.op("tensor", "matmul", ps[:, 0:n],
                     gw1x[i][:, 128 * m:128 * m + 128],
                     xn[i][:, HALO + o:HALO + o + n], start=(i == 0),
                     stop=(i == 7), sem=s_t if i == 7 else None)
            p.wait("scalar", s_t)
            p.op("scalar", "copy", zraw[:, o:o + n], ps[:, 0:n], sem=s_sc)
            pp.done(pi, s_sc)
        p.wait("sync", s_sc)
        p.dma("sync", io["zx"][128 * m:128 * m + 128, :], zraw[:, 0:TPC],
              sem=s_out)
        p.wait("scalar", s_out)

    p.wait("sync", s_v)
    p.dma("sync", io["qkm"], mxrow[0:1, 0:2], sem=s_out)
    p.wait("sync", s_out)


def rope_tables_np(t0, t1):
    half = 64
    inv_freq = (1.0 / 10000.0 ** (np.arange(half, dtype=np.float32) / half)
                ).astype(np.float32)
    ang = np.arange(t0, t1, dtype=np.float32)[:, None] * inv_freq[None, :]
    return (np.ascontiguousarray(np.cos(ang).astype(np.float32).T),
            np.ascontiguousarray(np.sin(ang).astype(np.float32).T))


def stage_a_inputs(inputs, c):
    x = inputs["L0_attn_x"][0]
    t0 = c * TPC
    lo = t0 - HALO
    if lo < 0:
        xs = np.concatenate([np.zeros((-lo, DM), np.float32),
                             x[0:t0 + TPC]], 0)
    else:
        xs = x[lo:t0 + TPC]
    cos, sin = rope_tables_np(t0, t0 + TPC)
    s = np.float32(1.0 / np.sqrt(128.0))
    return {
        "xT": np.ascontiguousarray(xs.T),
        "Wq": inputs["L0_attn_Wq"], "Wk": inputs["L0_attn_Wk"],
        "Wv": inputs["L0_attn_Wv"], "Wb": inputs["L0_attn_Wb"],
        "qc_w": inputs["L0_attn_qc_w"], "kc_w": inputs["L0_attn_kc_w"],
        "vc_w": inputs["L0_attn_vc_w"],
        "w_inT": np.ascontiguousarray(inputs["L0_ffn_w_in"].T),
        "gW1": inputs["L0_attn_gW1"],
        "ones2": np.ones((128, 2), BF),
        "onesrf": np.ones((1, 128), np.float32),
        "ident": np.eye(128, dtype=BF),
        "cosq": (cos * s).astype(BF), "sinq": (sin * s).astype(BF),
        "cosk": cos.astype(BF), "sink": sin.astype(BF),
    }


STAGE_A_INS = {
    "xT": ([DM, TH], F32), "Wq": ([DM, DM], F32), "Wk": ([DM, DM], F32),
    "Wv": ([DM, DM], F32), "Wb": ([DM, 4], F32),
    "qc_w": ([DM, 4], F32), "kc_w": ([DM, 4], F32), "vc_w": ([DM, 4], F32),
    "w_inT": ([DM, 4112], F32), "gW1": ([5120, 512], F32),
    "ones2": ([128, 2], BF16), "onesrf": ([1, 128], F32),
    "ident": ([128, 128], BF16),
    "cosq": ([64, TPC], BF16), "sinq": ([64, TPC], BF16),
    "cosk": ([64, TPC], BF16), "sink": ([64, TPC], BF16),
}
STAGE_A_OUTS = {
    "dbg_zraw": ([128, THP], BF16), "dbg_zconv": ([128, THP], BF16),
    "cq": ([DM, TPC], BF16), "ck": ([DM, TPC], BF16),
    "ckt": ([TPC, DM], BF16), "cvt": ([TPC, DM], BF16),
    "cv": ([DM, TH - 3], BF16),
    "cb": ([4, TPC], BF16),
    "caq": ([DM, TPC], BF16), "cak": ([DM, TPC], BF16),
    "cav": ([TPC, DM], BF16),
    "qf": ([DM, TPC], BF16), "g16": ([TPC, 16], F32),
    "zx": ([512, TPC], BF16), "qkm": ([1, 2], F32),
}


def build_stage_a():
    nc = bass.Bass("TRN2", target_bir_lowering=False)
    io = {}
    for k, (shape, dt) in STAGE_A_INS.items():
        io[k] = nc.dram_tensor(k, shape, dt, kind="ExternalInput").ap()
    for k, (shape, dt) in STAGE_A_OUTS.items():
        io[k] = nc.dram_tensor(k, shape, dt, kind="ExternalOutput").ap()
    p = Prog(nc)
    stage_a(p, io)
    p.emit_and_finish()
    return nc


CH = 128
NCHUNK = L // CH
DB = 16


def stage_delta(p, io, fz=None):
    kfm = [[p.sbuf(f"D_kfm{b}_{i}", [128, DB * CH], BF16) for i in range(2)]
           for b in range(2)]
    qfm = [[p.sbuf(f"D_qfm{b}_{i}", [128, DB * CH], BF16) for i in range(2)]
           for b in range(2)]
    ktmb = [p.sbuf(f"D_ktm{b}", [128, DB * 256], BF16) for b in range(2)]
    vtmb = [p.sbuf(f"D_vtm{b}", [128, DB * 128], BF16) for b in range(2)]
    bcolb = [p.sbuf(f"D_bc{b}", [128, DB], F32) for b in range(2)]
    browb = [p.sbuf(f"D_br{b}", [1, DB * CH], F32) for b in range(2)]
    qn = [p.sbuf(f"D_qn{i}", [128, DB * CH], BF16) for i in range(2)]
    kn = [p.sbuf(f"D_kn{i}", [128, DB * CH], BF16) for i in range(2)]
    kb = [p.sbuf(f"D_kb{i}", [128, DB * CH], BF16) for i in range(2)]
    sqt = p.sbuf("D_sqt", [128, DB * CH], BF16)
    sqt2 = p.sbuf("D_sqt2", [128, DB * CH], BF16)
    rows = p.sbuf("D_rows", [1, 3 * DB * CH], F32)
    rep = p.sbuf("D_rep", [128, DB * CH], BF16)
    rkcol = p.sbuf("D_rkcol", [128, DB], F32)
    skbcol = p.sbuf("D_skb", [128, DB], F32)
    sqtm = p.sbuf("D_sqtm", [128, DB * 256], BF16)
    Mt = [p.sbuf(f"D_M{i}", [128, 512], BF16) for i in range(2)]
    MTt = [p.sbuf(f"D_MT{i}", [128, 512], BF16) for i in range(2)]
    Qt = [p.sbuf(f"D_Q{i}", [128, 512], BF16) for i in range(2)]
    TTs = p.sbuf("D_TTs", [128, 512], BF16)
    nwf = [p.sbuf(f"D_nwf{i}", [128, 512], BF16) for i in range(2)]
    upt = p.sbuf("D_upt", [128, 128], BF16)
    upst = p.sbuf("D_upst", [128, 128], BF16)
    attnT = p.sbuf("D_attnT", [128, 512], BF16)
    vbt = p.sbuf("D_vbt", [128, 512], BF16)
    S0t = [p.sbuf(f"D_S0t{i}", [128, 128], BF16) for i in range(2)]
    S1t = [p.sbuf(f"D_S1t{i}", [128, 128], BF16) for i in range(2)]
    obat = p.sbuf("D_obat", [128, DB * 128], BF16)
    identbf = p.sbuf("D_id", [128, 128], BF16)
    onescol = p.sbuf("D_ones", [128, 2], BF16)
    onesrf = p.sbuf("D_onesrf", [1, 128], F32)
    mnu1 = p.sbuf("D_mnu1", [128, 128], F32)
    mnl1 = p.sbuf("D_mnl1", [128, 128], F32)
    mu0 = p.sbuf("D_mu0", [128, 128], F32)

    ofm = p.sbuf("D_ofm", [128, DB * 128], BF16)

    ppS = PP(p, "D_ppS", 2, [128, 128])
    pp = PP(p, "D_pp", 4, [128, 512])
    ppt = PP(p, "D_ppt", 2, [128, 512], BF16)
    S0p, S1p = ppS.tiles[0], ppS.tiles[1]

    s_in = p.sem("din")
    s_v = p.sem("dv")
    s_t = p.sem("dt")
    s_sc = p.sem("dsc")
    s_out = p.sem("dout")
    s_of = p.sem("dof")
    s_c = p.sem("dc")

    p.dma("sync", identbf[:], io["ident"], sem=s_c)
    p.dma("sync", onescol[:], io["ones2"], sem=s_c)
    p.dma("sync", onesrf[:], io["onesrf"], sem=s_c)
    p.dma("sync", mnu1[:], io["mnu1"], sem=s_c)
    p.dma("sync", mnl1[:], io["mnl1"], sem=s_c)
    p.dma("sync", mu0[:], io["mu0"], sem=s_c)
    p.wait("tensor", s_c)
    p.wait("vector", s_c)
    eps6 = p.constcol(1e-6, s_v)
    p.wait("scalar", s_v)

    NB = NCHUNK // DB

    if fz is not None:
        rowt16 = p.sbuf("D_rt16", [16, 128], BF16)
        dcolb = p.sbuf("D_dcolb", [128, DB], F32)
        for s in range(NCORE):
            p.wait("gpsimd", fz["sem_g"], fz["stv_g"][s])
            if s > 0:
                p.wait("gpsimd", s_t)
            p.dma("gpsimd", rowt16[:],
                  io["bh_row"][0:1, s * TPC:(s + 1) * TPC].rearrange(
                      "r (c i) -> (r c) i", i=128), sem=s_in)
            p.wait("tensor", s_in)
            ps, pi = ppt.get()
            p.op("tensor", "transpose", ps[:, 0:16], rowt16[:],
                 identbf[0:16, 0:16], sem=s_t)
            p.wait("vector", s_t)
            p.op("vector", "tensor_copy", dcolb[:], ps[:, 0:16], sem=s_v)
            ppt.done(pi, s_v)
            p.wait("sync", s_v)
            p.dma("sync", io["bh_col"][:, s * DB:(s + 1) * DB], dcolb[:],
                  sem=s_c)
            p.wait("vector", s_c)
        p.wait("sync", s_c)

    def load_batch(b):
        bb = b % 2
        t0 = b * DB * CH
        if fz is not None:
            p.wait("sync", fz["sem_q"], fz["stv_q"][b])
            p.wait("sync", fz["sem_s"], fz["stv_s"][b])
        for i in range(2):
            p.dma("sync", kfm[bb][i][:],
                  io["kh_fm"][128 * i:128 * i + 128, t0:t0 + DB * CH],
                  sem=s_in)
            p.dma("sync", qfm[bb][i][:],
                  io["qh_fm"][128 * i:128 * i + 128, t0:t0 + DB * CH],
                  sem=s_in)
        p.dma("sync", ktmb[bb][:].rearrange("p (c d) -> p c d", d=256),
              io["kh_tm"][t0:t0 + DB * CH, :].rearrange(
                  "(c p) d -> p c d", p=128), sem=s_in)
        p.dma("sync", vtmb[bb][:].rearrange("p (c d) -> p c d", d=128),
              io["vh_tm"][t0:t0 + DB * CH, :].rearrange(
                  "(c p) d -> p c d", p=128), sem=s_in)
        p.dma("sync", browb[bb][:], io["bh_row"][0:1, t0:t0 + DB * CH],
              sem=s_in)
        p.dma("sync", bcolb[bb][:], io["bh_col"][:, b * DB:(b + 1) * DB],
              sem=s_in)

    S_IN0 = s_in.val
    load_batch(0)

    W = DB * CH
    for b in range(NB):
        bb = b % 2
        if b + 1 < NB:
            p.wait("sync", s_t)
            p.wait("sync", s_v)
            load_batch(b + 1)
        p.wait("vector", s_in, S_IN0 + (b + 1) * 8 * 16)
        p.wait("tensor", s_in, S_IN0 + (b + 1) * 8 * 16)
        RQ, RK, RKB = 0, W, 2 * W
        for (dst, srct) in ((RQ, qfm[bb]), (RK, kfm[bb])):
            p.opw("vector", s_v, "tensor_mul", sqt[:], srct[0][:], srct[0][:])
            p.op("vector", "tensor_mul", sqt2[:], srct[1][:], srct[1][:],
                 sem=s_v)
            p.wait("tensor", s_v)
            for (o, n) in nblocks(W):
                ps, pi = pp.get()
                p.op("tensor", "matmul", ps[0:1, 0:n], onescol[:, 0:1],
                     sqt[:, o:o + n], start=True, stop=False)
                p.op("tensor", "matmul", ps[0:1, 0:n], onescol[:, 0:1],
                     sqt2[:, o:o + n], start=False, stop=True, sem=s_t)
                p.wait("scalar", s_t)
                p.op("scalar", "activation", rows[0:1, dst + o:dst + o + n],
                     ps[0:1, 0:n], AF.Sqrt, bias=eps6[0:1, 0:1], sem=s_sc)
                pp.done(pi, s_sc)
            p.wait("vector", s_t)
        p.wait("vector", s_sc)
        p.opw("vector", s_v, "reciprocal", rows[0:1, RQ:RQ + W],
              rows[0:1, RQ:RQ + W])
        p.opw("vector", s_v, "reciprocal", rows[0:1, RK:RK + W],
              rows[0:1, RK:RK + W])
        p.op("vector", "tensor_mul", rows[0:1, RKB:RKB + W],
             rows[0:1, RK:RK + W], browb[bb][:], sem=s_v)
        ROWS_D = s_v.val
        p.opw("vector", s_v, "tensor_mul", sqtm[:], ktmb[bb][:], ktmb[bb][:])
        p.opw("vector", s_v, "tensor_reduce",
              rkcol.rearrange("p (g o) -> p g o", o=1),
              sqtm.rearrange("p (g d) -> p g d", d=256), AX.X, ALU.add)
        p.wait("scalar", s_v)
        p.op("scalar", "activation", rkcol[:], rkcol[:], AF.Sqrt,
             bias=eps6[:, 0:1], sem=s_sc)
        p.wait("vector", s_sc)
        p.opw("vector", s_v, "reciprocal", rkcol[:], rkcol[:])
        p.op("vector", "tensor_mul", skbcol[:], rkcol[:], bcolb[bb][:],
             sem=s_v)
        COLS_D = s_v.val
        p.wait("tensor", s_v, ROWS_D)
        for (dst, srcrow, srctiles) in ((qn, RQ, qfm[bb]), (kn, RK, kfm[bb]),
                                        (kb, RKB, kfm[bb])):
            for (o, n) in nblocks(W):
                ps, pi = pp.get()
                p.op("tensor", "matmul", ps[:, 0:n], onesrf[:],
                     rows[0:1, srcrow + o:srcrow + o + n], start=True,
                     stop=True, sem=s_t)
                p.wait("vector", s_t)
                p.op("vector", "tensor_copy", rep[:, o:o + n], ps[:, 0:n],
                     sem=s_v)
                pp.done(pi, s_v)
            p.wait("vector", s_v)
            for i in range(2):
                p.op("vector", "tensor_mul", dst[i][:], srctiles[i][:],
                     rep[:], sem=s_v)
            p.wait("vector", s_v)
        NORM_D = s_v.val
        if b == 0 and "dbg_rows" in io:
            p.wait("sync", s_v)
            p.dma("sync", io["dbg_rows"], rows[0:1, 0:3 * W], sem=s_out)
            p.dma("sync", io["dbg_rkcol"], rkcol[:], sem=s_out)
            p.dma("sync", io["dbg_qn"], qn[0][:, 0:128], sem=s_out)
            p.dma("sync", io["dbg_kb"], kb[0][:, 0:128], sem=s_out)

        p.wait("tensor", s_v, NORM_D)
        p.wait("vector", s_v, NORM_D)

        def sl2(t, j):
            return t[:, 128 * j:128 * j + 128]

        def xpair(tpair):
            ccs = [4 * tpair + jj for jj in range(4)]
            p.wait("vector", s_t)
            psAT, piAT = pp.get()
            psA, piA = pp.get()
            psat, piat = pp.get()
            for j, cc in enumerate(ccs):
                co = cc * CH
                q0 = [qn[0][:, co:co + CH], qn[1][:, co:co + CH]]
                k0 = [kn[0][:, co:co + CH], kn[1][:, co:co + CH]]
                kb0 = [kb[0][:, co:co + CH], kb[1][:, co:co + CH]]
                p.op("tensor", "matmul", sl2(psAT, j), k0[0], kb0[0],
                     start=True, stop=False)
                p.op("tensor", "matmul", sl2(psAT, j), k0[1], kb0[1],
                     start=False, stop=True)
                p.op("tensor", "matmul", sl2(psA, j), kb0[0], k0[0],
                     start=True, stop=False)
                p.op("tensor", "matmul", sl2(psA, j), kb0[1], k0[1],
                     start=False, stop=True)
                p.op("tensor", "matmul", sl2(psat, j), k0[0], q0[0],
                     start=True, stop=False)
                p.op("tensor", "matmul", sl2(psat, j), k0[1], q0[1],
                     start=False, stop=True,
                     sem=s_t if j == 3 else None)
            p.wait("vector", s_t)

            def b2(mask):
                return mask[:].rearrange(
                    "p (o d) -> p o d", o=1).broadcast_to([128, 4, 128])

            def w2v(t):
                return t[:].rearrange("p (o d) -> p o d", d=128)

            p.opw("vector", s_v, "tensor_tensor", w2v(MTt[0]),
                  psAT[:, 0:512].rearrange("p (o d) -> p o d", d=128),
                  b2(mnu1), op=ALU.mult)
            pp.done(piAT, s_v)
            p.opw("vector", s_v, "tensor_tensor", w2v(Mt[0]),
                  psA[:, 0:512].rearrange("p (o d) -> p o d", d=128),
                  b2(mnl1), op=ALU.mult)
            pp.done(piA, s_v)
            p.opw("vector", s_v, "tensor_tensor", w2v(Qt[0]),
                  w2v(MTt[0]), b2(identbf), op=ALU.add)
            p.op("vector", "tensor_tensor", w2v(attnT),
                 psat[:, 0:512].rearrange("p (o d) -> p o d", d=128),
                 b2(mu0), op=ALU.mult, sem=s_v)
            pp.done(piat, s_v)
            cur = 0
            for lvl in range(6):
                nxt = 1 - cur
                p.wait("tensor", s_v)
                psm, pim = pp.get()
                psmt, pimt = pp.get()
                for j in range(4):
                    p.op("tensor", "matmul", sl2(psm, j), sl2(MTt[cur], j),
                         sl2(Mt[cur], j), start=True, stop=True)
                    p.op("tensor", "matmul", sl2(psmt, j), sl2(Mt[cur], j),
                         sl2(MTt[cur], j), start=True, stop=True,
                         sem=s_t if j == 3 else None)
                p.wait("vector", s_t)
                p.op("vector", "tensor_copy", Mt[nxt][:], psm[:, 0:512],
                     sem=s_v)
                pp.done(pim, s_v)
                p.op("vector", "tensor_copy", MTt[nxt][:], psmt[:, 0:512],
                     sem=s_v)
                pp.done(pimt, s_v)
                p.wait("tensor", s_v)
                psq, piq = pp.get()
                for j in range(4):
                    p.op("tensor", "matmul", sl2(psq, j), sl2(Mt[nxt], j),
                         sl2(Qt[cur], j), start=True, stop=False)
                    p.op("tensor", "matmul", sl2(psq, j), identbf[:],
                         sl2(Qt[cur], j), start=False, stop=True,
                         sem=s_t if j == 3 else None)
                p.wait("vector", s_t)
                p.op("vector", "tensor_copy", Qt[nxt][:], psq[:, 0:512],
                     sem=s_v)
                pp.done(piq, s_v)
                cur = nxt
            p.wait("vector", s_v)
            for j, cc in enumerate(ccs):
                p.op("vector", "tensor_scalar_mul", sl2(TTs, j),
                     sl2(Qt[cur], j), skbcol[:, cc:cc + 1], sem=s_v)
                p.op("vector", "tensor_scalar_mul", sl2(vbt, j),
                     vtmb[bb][:, cc * 128:cc * 128 + 128],
                     bcolb[bb][:, cc:cc + 1], sem=s_v)
            p.wait("vector", s_v)
            if cur != 0:
                p.opw("vector", s_v, "tensor_copy", Qt[0][:], Qt[cur][:])
            p.wait("tensor", s_v)
            psw, piw = pp.get()
            psw2, piw2 = pp.get()
            for j, cc in enumerate(ccs):
                ktm_c = [ktmb[bb][:, cc * 256:cc * 256 + 128],
                         ktmb[bb][:, cc * 256 + 128:cc * 256 + 256]]
                p.op("tensor", "matmul", sl2(psw, j), ktm_c[0], sl2(TTs, j),
                     start=True, stop=True)
                p.op("tensor", "matmul", sl2(psw2, j), ktm_c[1], sl2(TTs, j),
                     start=True, stop=True, sem=s_t if j == 3 else None)
            p.wait("vector", s_t)
            p.op("vector", "tensor_scalar_mul", nwf[0][:], psw[:, 0:512],
                 -1.0, sem=s_v)
            pp.done(piw, s_v)
            p.op("vector", "tensor_scalar_mul", nwf[1][:], psw2[:, 0:512],
                 -1.0, sem=s_v)
            pp.done(piw2, s_v)

        def ypart(cc):
            j = cc % 4
            c = b * DB + cc
            co = cc * CH
            q0 = [qn[0][:, co:co + CH], qn[1][:, co:co + CH]]
            ktm_c = [ktmb[bb][:, cc * 256:cc * 256 + 128],
                     ktmb[bb][:, cc * 256 + 128:cc * 256 + 256]]
            p.wait("tensor", s_v)
            psu, piu = pp.get()
            p.op("tensor", "matmul", psu[:, 0:128], sl2(Qt[0], j),
                 sl2(vbt, j), start=True, stop=(c == 0),
                 sem=s_t if c == 0 else None)
            if c > 0:
                p.op("tensor", "matmul", psu[:, 0:128], sl2(nwf[0], j),
                     S0t[c % 2][:], start=False, stop=False)
                p.op("tensor", "matmul", psu[:, 0:128], sl2(nwf[1], j),
                     S1t[c % 2][:], start=False, stop=True, sem=s_t)
            p.wait("vector", s_t)
            p.op("vector", "tensor_copy", upt[:], psu[:, 0:128], sem=s_v)
            p.op("vector", "tensor_scalar_mul", upst[:], psu[:, 0:128],
                 rkcol[:, cc:cc + 1], sem=s_v)
            pp.done(piu, s_v)
            p.wait("tensor", s_v)
            pso, pio = pp.get()
            if c > 0:
                p.op("tensor", "matmul", pso[:, 0:128], q0[0], S0t[c % 2][:],
                     start=True, stop=False)
                p.op("tensor", "matmul", pso[:, 0:128], q0[1], S1t[c % 2][:],
                     start=False, stop=False)
                p.op("tensor", "matmul", pso[:, 0:128], sl2(attnT, j),
                     upt[:], start=False, stop=True, sem=s_t)
            else:
                p.op("tensor", "matmul", pso[:, 0:128], sl2(attnT, j),
                     upt[:], start=True, stop=True, sem=s_t)
            p.wait("vector", s_t)
            p.op("vector", "tensor_copy", obat[:, cc * 128:cc * 128 + 128],
                 pso[:, 0:128], sem=s_v)
            pp.done(pio, s_v)
            p.op("tensor", "matmul", S0p[:, 0:128], ktm_c[0], upst[:],
                 start=(c == 0), stop=(c == NCHUNK - 1))
            p.op("tensor", "matmul", S1p[:, 0:128], ktm_c[1], upst[:],
                 start=(c == 0), stop=(c == NCHUNK - 1), sem=s_t)
            p.wait("scalar", s_t)
            nxtp = (c + 1) % 2
            p.op("scalar", "copy", S0t[nxtp][:], S0p[:, 0:128], sem=s_sc)
            p.op("scalar", "copy", S1t[nxtp][:], S1p[:, 0:128], sem=s_sc)
            p.wait("tensor", s_sc)

        for tpair in range(DB // 4):
            xpair(tpair)
            for jj in range(4):
                ypart(4 * tpair + jj)

        p.wait("sync", s_v)
        if "do_tm" in io:
            p.dma("sync",
                  io["do_tm"][b * DB * CH:(b + 1) * DB * CH, :].rearrange(
                      "(c p) d -> p c d", p=128),
                  obat[:].rearrange("p (c d) -> p c d", d=128), sem=s_out)
        p.wait("tensor", s_v)
        if b > 0:
            p.wait("vector", s_of, s_of.val - 16)
        for cg in range(DB // 4):
            ps, pi = ppt.get()
            for j in range(4):
                cc = 4 * cg + j
                p.op("tensor", "transpose", ps[:, 128 * j:128 * j + 128],
                     obat[:, cc * 128:cc * 128 + 128], identbf[:],
                     sem=s_t if j == 3 else None)
            p.wait("vector", s_t)
            p.op("vector", "tensor_copy", ofm[:, 512 * cg:512 * cg + 512],
                 ps[:, 0:512], sem=s_v)
            ppt.done(pi, s_v)
        p.wait("sync", s_v)
        p.dma("sync", io["do_fm"][:, b * DB * CH:(b + 1) * DB * CH], ofm[:],
              sem=s_of)
        p.wait("vector", s_out, s_out.val - 16 * 1)

    p.wait("sync", s_out)
    p.wait("sync", s_of)


STAGE_D_INS = {
    "qh_fm": ([256, L], BF16), "kh_fm": ([256, L], BF16),
    "kh_tm": ([L, 256], BF16), "vh_tm": ([L, 128], BF16),
    "bh_col": ([128, NCHUNK], F32), "bh_row": ([1, L], F32),
    "ident": ([128, 128], BF16), "ones2": ([128, 2], BF16),
    "onesrf": ([1, 128], F32),
    "mnu1": ([128, 128], F32), "mnl1": ([128, 128], F32),
    "mu0": ([128, 128], F32),
}
STAGE_D_OUTS = {"do_tm": ([L, 128], BF16), "do_fm": ([128, L], BF16),
                "dbg_rows": ([1, 3 * DB * CH], F32),
                "dbg_rkcol": ([128, DB], F32),
                "dbg_qn": ([128, 128], BF16), "dbg_kb": ([128, 128], BF16),
                "dbg_Q": ([128, 128], BF16), "dbg_AT": ([128, 128], BF16),
                "dbg_up": ([128, 128], BF16),
                "dbg_AT0": ([128, 128], BF16), "dbg_A0": ([128, 128], BF16),
                "dbg_Q0": ([128, 128], BF16)}


def stage_d_inputs(aouts, c):
    h, half = c >> 1, c & 1
    q_fm = np.concatenate([aouts["cq"][s][256 * h:256 * h + 256, :]
                           for s in range(NCORE)], axis=1)
    k_fm = np.concatenate([aouts["ck"][s][256 * h:256 * h + 256, :]
                           for s in range(NCORE)], axis=1)
    k_tm = np.concatenate([aouts["ckt"][s][:, 256 * h:256 * h + 256]], axis=0) \
        if False else np.concatenate(
            [aouts["ckt"][s][:, 256 * h:256 * h + 256] for s in range(NCORE)],
            axis=0)
    v_tm = np.concatenate(
        [aouts["cvt"][s][:, 256 * h + 128 * half:256 * h + 128 * half + 128]
         for s in range(NCORE)], axis=0)
    brow = np.concatenate([aouts["cb"][s][h:h + 1, :] for s in range(NCORE)],
                          axis=1).astype(np.float32)
    bcol = np.ascontiguousarray(brow.reshape(NCHUNK, CH).T)
    iu = np.triu(np.ones((128, 128), np.float32), 1)
    il = np.tril(np.ones((128, 128), np.float32), -1)
    return {
        "qh_fm": np.ascontiguousarray(q_fm), "kh_fm": np.ascontiguousarray(k_fm),
        "kh_tm": np.ascontiguousarray(k_tm), "vh_tm": np.ascontiguousarray(v_tm),
        "bh_col": bcol, "bh_row": np.ascontiguousarray(brow),
        "ident": np.eye(128, dtype=BF), "ones2": np.ones((128, 2), BF),
        "onesrf": np.ones((1, 128), np.float32),
        "mnu1": -iu, "mnl1": -il,
        "mu0": np.triu(np.ones((128, 128), np.float32)),
    }


def build_stage_d():
    nc = bass.Bass("TRN2", target_bir_lowering=False)
    io = {}
    for k, (shape, dt) in STAGE_D_INS.items():
        io[k] = nc.dram_tensor(k, shape, dt, kind="ExternalInput").ap()
    for k, (shape, dt) in STAGE_D_OUTS.items():
        io[k] = nc.dram_tensor(k, shape, dt, kind="ExternalOutput").ap()
    p = Prog(nc)
    stage_delta(p, io)
    p.emit_and_finish()
    return nc


QB = 1024
QS = 512


def stage_attn(p, io, qblocks):
    khead = [p.sbuf(f"T_kh{i}", [128, L], BF16) for i in range(2)]
    vhead = [p.sbuf(f"T_vh{i}", [128, L], BF16) for i in range(2)]
    qt = [p.sbuf(f"T_q{i}", [128, QB], BF16) for i in range(8)]
    pbuf = p.sbuf("T_pbuf", [128, 1024], BF16)
    mbuf = [p.sbuf(f"T_mbuf{i}", [128, 512], BF16) for i in range(2)]
    attnfin = [p.sbuf(f"T_af{i}", [128, QB], BF16) for i in range(8)]
    woat = [p.sbuf(f"T_wo{i}", [128, 1024], BF16) for i in range(8)]
    masks = [p.sbuf(f"T_mask{i}", [128, 512], BF16) for i in range(4)]
    denrep = p.sbuf("T_denrep", [128, 512], BF16)
    denrow = p.sbuf("T_denrow", [1, 512], F32)
    gout = p.sbuf("T_gout", [128, 512], BF16)
    onesrf = p.sbuf("T_onesrf", [1, 128], F32)
    onescol = p.sbuf("T_ones", [128, 2], BF16)
    mrow = p.sbuf("T_mrow", [1, 16], F32)
    mcol = p.sbuf("T_mcol", [128, 1], F32)

    pssc = [p.psum(f"T_sc{i}", [128, 1024]) for i in range(2)]
    pso = [p.psum(f"T_o{i}", [128, 512]) for i in range(2)]
    psd = [p.psum(f"T_d{i}", [1, 512]) for i in range(2)]

    s_in = p.sem("tin"); s_w = p.sem("tw"); s_c = p.sem("tc")
    s_v = p.sem("tv"); s_t = p.sem("tt"); s_sc = p.sem("tsc")
    s_out = p.sem("tout")

    for i in range(4):
        p.dma("sync", masks[i][:], io[f"mask{i}"], sem=s_c)
    p.dma("sync", onesrf[:], io["onesrf"], sem=s_c)
    p.dma("sync", onescol[:], io["ones2"], sem=s_c)
    p.dma("sync", mrow[0:1, 0:8], io["qmax8"], sem=s_c)
    p.dma("sync", mrow[0:1, 8:16], io["kmax8"], sem=s_c)
    for i in range(8):
        p.dma("gpsimd", woat[i][:], io["w_outT_attn"][128 * i:128 * i + 128, :],
              sem=s_w)

    p.wait("vector", s_c)
    p.opw("vector", s_v, "reduce_max", mrow[0:1, 0:1], mrow[0:1, 0:8],
          axis=AX.X)
    p.opw("vector", s_v, "reduce_max", mrow[0:1, 1:2], mrow[0:1, 8:16],
          axis=AX.X)
    p.wait("scalar", s_v)
    p.op("scalar", "activation", mrow[0:1, 2:4], mrow[0:1, 0:2], AF.Sqrt,
         sem=s_sc)
    p.wait("vector", s_sc)
    p.opw("vector", s_v, "tensor_mul", mrow[0:1, 4:5], mrow[0:1, 2:3],
          mrow[0:1, 3:4])
    p.opw("vector", s_v, "tensor_scalar_mul", mrow[0:1, 5:6], mrow[0:1, 4:5],
          -1.0)
    p.wait("tensor", s_v)
    ps, _ = (pso[0], 0)
    p.op("tensor", "matmul", ps[:, 0:1], onesrf[:], mrow[0:1, 5:6],
         start=True, stop=True, sem=s_t)
    p.wait("vector", s_t)
    p.op("vector", "tensor_copy", mcol[:], ps[:, 0:1], sem=s_v)
    p.wait("scalar", s_v)
    p.wait("tensor", s_v)

    def load_head(qb_i, h, buf):
        span = qblocks[qb_i] * QB + QB
        p.dma("sync", khead[buf][:, 0:span],
              io["ak"][128 * h:128 * h + 128, 0:span], sem=s_in)
        p.dma("sync", vhead[buf][:, 0:span].rearrange("p (t d) -> p t d", d=128),
              io["av"][0:span, 128 * h:128 * h + 128].rearrange(
                  "(t p) d -> p t d", p=128), sem=s_in)

    load_head(0, 0, 0)
    nload = 1
    first_mm = True
    for qb_i, qb in enumerate(qblocks):
        for i in range(8):
            p.dma("sync", qt[i][:], io["aq"][128 * i:128 * i + 128,
                                             qb_i * QB:(qb_i + 1) * QB],
                  sem=s_in)
        span = qb * QB + QB
        for h in range(8):
            buf = (qb_i * 8 + h) % 2
            nh = qb_i * 8 + h + 1
            if nh < 16:
                p.wait("sync", s_t)
                load_head(nh // 8, nh % 8, nh % 2)
                nload += 1
            p.wait("tensor", s_in, s_in.val)
            for qs in range(2):
                qs0 = qb * QB + qs * QS
                kt_hi = (qs0 + QS) // 128
                o_ps = pso[qs]
                d_ps = psd[qs]
                ngroups = (kt_hi + 1) // 2
                first = True
                for g in range(ngroups):
                    sc = pssc[g % 2]
                    kts = [kt for kt in (2 * g, 2 * g + 1) if kt < kt_hi]
                    for gi, kt in enumerate(kts):
                        p.op("tensor", "matmul", sc[:, gi * 512:gi * 512 + 512],
                             khead[buf][:, kt * 128:kt * 128 + 128],
                             qt[h][:, qs * QS:qs * QS + QS],
                             start=True, stop=True,
                             sem=s_t if gi == len(kts) - 1 else None)
                    p.wait("scalar", s_t)
                    width = len(kts) * 512
                    p.op("scalar", "activation", pbuf[:, 0:width],
                         sc[:, 0:width], AF.Exp, bias=mcol[:, 0:1], sem=s_sc)
                    av_src = []
                    for gi, kt in enumerate(kts):
                        off = kt * 128 - qs0
                        if off >= 0:
                            mi = off // 128
                            p.wait("vector", s_sc)
                            p.op("vector", "tensor_mul",
                                 mbuf[gi][:], pbuf[:, gi * 512:gi * 512 + 512],
                                 masks[mi][:], sem=s_v)
                            av_src.append(mbuf[gi][:, 0:512])
                        else:
                            av_src.append(pbuf[:, gi * 512:gi * 512 + 512])
                    p.wait("tensor", s_sc)
                    p.wait("tensor", s_v)
                    for gi, kt in enumerate(kts):
                        p.op("tensor", "matmul", o_ps[:, 0:512],
                             vhead[buf][:, kt * 128:kt * 128 + 128],
                             av_src[gi], start=first, stop=False)
                        p.op("tensor", "matmul", d_ps[0:1, 0:512],
                             onescol[:, 0:1], av_src[gi],
                             start=first, stop=False,
                             sem=s_t)
                        first = False
                p.wait("vector", s_t)
                p.opw("vector", s_v, "reciprocal", denrow[:], d_ps[0:1, 0:512])
                p.wait("tensor", s_v)
                p.op("tensor", "matmul", pssc[0][:, 0:512], onesrf[:],
                     denrow[:], start=True, stop=True, sem=s_t)
                p.wait("vector", s_t)
                p.opw("vector", s_v, "tensor_copy", denrep[:],
                      pssc[0][:, 0:512])
                p.op("vector", "tensor_mul",
                     attnfin[h][:, qs * QS:qs * QS + QS], o_ps[:, 0:512],
                     denrep[:], sem=s_v)
                p.wait("tensor", s_v)
        p.wait("tensor", s_w)
        for m in range(8):
            for qh in range(2):
                sc = pssc[qh % 2]
                for i in range(8):
                    p.op("tensor", "matmul", sc[:, 0:512],
                         woat[i][:, 128 * m:128 * m + 128],
                         attnfin[i][:, qh * 512:qh * 512 + 512],
                         start=(i == 0), stop=(i == 7),
                         sem=s_sc if False else (s_t if i == 7 else None))
                p.wait("scalar", s_t)
                p.op("scalar", "copy", gout[:], sc[:, 0:512], sem=s_sc)
                p.wait("sync", s_sc)
                p.dma("sync", io["ga"][128 * m:128 * m + 128,
                                       qb_i * QB + qh * 512:
                                       qb_i * QB + qh * 512 + 512],
                      gout[:], sem=s_out)
                p.wait("scalar", s_out)
                p.wait("tensor", s_sc)
    p.wait("sync", s_out)


STAGE_T_INS = {
    "aq": ([DM, 2 * QB], BF16), "ak": ([DM, L], BF16), "av": ([L, DM], BF16),
    "w_outT_attn": ([DM, DM], F32),
    "qmax8": ([1, 8], F32), "kmax8": ([1, 8], F32),
    "onesrf": ([1, 128], F32), "ones2": ([128, 2], BF16),
    "mask0": ([128, 512], BF16), "mask1": ([128, 512], BF16),
    "mask2": ([128, 512], BF16), "mask3": ([128, 512], BF16),
}
STAGE_T_OUTS = {"ga": ([DM, 2 * QB], BF16)}


def stage_t_inputs(aouts, c):
    qblocks = [c, 15 - c]
    aq = np.concatenate(
        [aouts["caq"][b >> 1][:, (b & 1) * QB:(b & 1) * QB + QB]
         for b in qblocks], axis=1)
    ak = np.concatenate([aouts["cak"][s] for s in range(NCORE)], axis=1)
    av = np.concatenate([aouts["cav"][s] for s in range(NCORE)], axis=0)
    qmax8 = np.concatenate([aouts["qkm"][s][:, 0:1] for s in range(NCORE)],
                           axis=1).astype(np.float32)
    kmax8 = np.concatenate([aouts["qkm"][s][:, 1:2] for s in range(NCORE)],
                           axis=1).astype(np.float32)
    d = {
        "aq": np.ascontiguousarray(aq), "ak": np.ascontiguousarray(ak),
        "av": np.ascontiguousarray(av),
        "w_outT_attn": None,
        "qmax8": qmax8, "kmax8": kmax8,
        "onesrf": np.ones((1, 128), np.float32),
        "ones2": np.ones((128, 2), BF),
    }
    for off in range(4):
        j = np.arange(128)[:, None]
        i = np.arange(512)[None, :]
        d[f"mask{off}"] = (j + off * 128 <= i).astype(BF)
    return d


def build_stage_t(qblocks):
    nc = bass.Bass("TRN2", target_bir_lowering=False)
    io = {}
    for k, (shape, dt) in STAGE_T_INS.items():
        io[k] = nc.dram_tensor(k, shape, dt, kind="ExternalInput").ap()
    for k, (shape, dt) in STAGE_T_OUTS.items():
        io[k] = nc.dram_tensor(k, shape, dt, kind="ExternalOutput").ap()
    p = Prog(nc)
    stage_attn(p, io, qblocks)
    p.emit_and_finish()
    return nc




def stage_m(p, io, stop_after=None, hook_a2=None):
    def marker(words):
        pass

    s_w = p.sem("mw"); s_c = p.sem("mc"); s_v = p.sem("mv")
    s_t = p.sem("mt"); s_sc = p.sem("msc"); s_out = p.sem("mout")
    s_in = p.sem("min"); s_b = p.sem("mb")

    onescol = p.sbuf("M_ones", [128, 2], BF16)
    onesrf = p.sbuf("M_onesrf", [1, 128], F32)
    onesrb = p.sbuf("M_onesrb", [1, 128], BF16)
    identbf = p.sbuf("M_id", [128, 128], BF16)
    ones128 = p.sbuf("M_o128", [128, 128], BF16)
    diag = [p.sbuf(f"M_diag{i}", [128, 128], BF16) for i in range(2)]
    gsil = [p.sbuf(f"M_gs{i}", [128, TPC], BF16) for i in range(8)]
    wstc = [p.sbuf(f"M_wst{i}", [4, 512], BF16) for i in range(3)]
    gb1c = [p.sbuf(f"M_gb1{i}", [128, 1], F32) for i in range(4)]
    gw2t = [p.sbuf(f"M_gw2{i}", [128, 4], BF16) for i in range(4)]
    gb2r = p.sbuf("M_gb2", [1, 4], BF16)
    prow = p.sbuf("M_prow", [1, 16], F32)
    epsfr = p.sbuf("M_epsf", [1, 16], F32)
    invtc = [p.sbuf(f"M_it{h}", [128, 1], F32) for h in range(4)]
    flrep = [p.sbuf(f"M_flr{h}", [128, 4], F32) for h in range(4)]
    onrm = [p.sbuf(f"M_onrm{i}", [128, 1], F32) for i in range(2)]
    emt = [p.sbuf(f"M_em{i}", [128, 4], BF16) for i in range(4)]

    pp = PP(p, "M_pp", 2, [128, 512])
    pp2 = PP(p, "M_pp2", 2, [128, 1024])
    ppt = PP(p, "M_ppt", 2, [128, 512], BF16)

    p.dma("sync", onescol[:], io["ones2"], sem=s_c)
    p.dma("sync", onesrf[:], io["onesrf"], sem=s_c)
    p.dma("sync", onesrb[:], io["onesrb"], sem=s_c)
    p.dma("sync", identbf[:], io["ident"], sem=s_c)
    p.dma("sync", ones128[:], io["ones128"], sem=s_c)
    for i in range(4):
        p.dma("sync", emt[i][:], io["emat"][128 * i:128 * i + 128, :], sem=s_c)
        p.dma("sync", gb1c[i][:], io["gb1"][128 * i:128 * i + 128, :], sem=s_c)
        p.dma("gpsimd", gw2t[i][:], io["gW2"][128 * i:128 * i + 128, :],
              sem=s_c)
    p.dma("gpsimd", gb2r[:], io["gb2"], sem=s_c)
    p.dma("sync", prow[0:1, 0:4], io["temp4"], sem=s_c)
    p.dma("sync", epsfr[:], io["epsf16"], sem=s_c)
    p.dma("sync", onrm[0][:], io["onorm"][0:128, :], sem=s_c)
    p.dma("sync", onrm[1][:], io["onorm"][128:256, :], sem=s_c)
    p.wait("vector", s_c)
    p.wait("tensor", s_c)
    eps5 = p.constcol(1e-5, s_v)
    p.opw("vector", s_v, "tensor_scalar_max", prow[0:1, 4:8], prow[0:1, 0:4],
          0.2)
    p.opw("vector", s_v, "tensor_scalar_min", prow[0:1, 4:8], prow[0:1, 4:8],
          10.0)
    p.opw("vector", s_v, "reciprocal", prow[0:1, 4:8], prow[0:1, 4:8])
    p.opw("vector", s_v, "tensor_scalar_max", epsfr[:], epsfr[:], 1e-7)
    p.opw("vector", s_v, "tensor_scalar_min", epsfr[:], epsfr[:], 0.1)
    p.wait("tensor", s_v)
    for h in range(4):
        ps, pi = pp.get()
        p.op("tensor", "matmul", ps[:, 0:1], onesrf[:], prow[0:1, 4 + h:5 + h],
             start=True, stop=True, sem=s_t)
        ps2, pi2 = pp.get()
        p.op("tensor", "matmul", ps2[:, 0:4], onesrf[:],
             epsfr[0:1, 4 * h:4 * h + 4], start=True, stop=True, sem=s_t)
        p.wait("vector", s_t)
        p.op("vector", "tensor_copy", invtc[h][:], ps[:, 0:1], sem=s_v)
        pp.done(pi, s_v)
        p.op("vector", "tensor_copy", flrep[h][:], ps2[:, 0:4], sem=s_v)
        pp.done(pi2, s_v)

    p.scope_open()
    gst = [p.sbuf(f"M_gst{i}", [128, 4], BF16) for i in range(24)]
    w1sb = [p.sbuf(f"M_w1s{i}", [128, 512], BF16) for i in range(2)]
    for i in range(24):
        p.dma("gpsimd", gst[i][:], io["gstat"][128 * i:128 * i + 128, :],
              sem=s_w)
    p.wait("gpsimd", s_w)
    for s in range(3):
        ps, pi = pp.get()
        for jj in range(8):
            br = jj // 2
            ch = jj % 2
            gi = (br * 3 + s) * 2 + ch
            p.dma("gpsimd", w1sb[jj % 2][:],
                  io["gW1"][1024 + gi * 128:1024 + gi * 128 + 128, :], sem=s_w)
            p.wait("tensor", s_w)
            p.op("tensor", "matmul", ps[0:4, 0:512], gst[gi][:],
                 w1sb[jj % 2][:], start=(jj == 0), stop=(jj == 7), sem=s_t)
            if jj < 7:
                p.wait("gpsimd", s_t, s_t.val - 1)
        p.wait("vector", s_t)
        p.op("vector", "tensor_copy", wstc[s][:], ps[0:4, 0:512], sem=s_v)
        pp.done(pi, s_v)
    p.scope_close(s_b)

    p.scope_open()
    qf = [p.sbuf(f"M_qf{i}", [128, TPC], BF16) for i in range(8)]
    qf_addr = [p.nc.lookup_mloc(t).addr for t in qf]
    ffacc = [p.sbuf(f"M_ff{i}", [128, TPC], F32) for i in range(8)]
    kfw = [p.sbuf(f"M_kfw{i}", [128, 8 * 256], BF16) for i in range(2)]
    vfw = [p.sbuf(f"M_vfw{i}", [128, 2 * 8 * 128], BF16) for i in range(2)]
    grep = p.sbuf("M_grep", [128, TPC], BF16)
    gt4 = p.sbuf("M_gt4", [128, TPC // 128 * 16], F32)
    abuf = [p.sbuf(f"M_abuf{i}", [128, 1024], BF16) for i in range(2)]
    abuf2 = p.sbuf("M_abuf2", [128, 1024], BF16)
    for i in range(8):
        p.dma("sync", qf[i][:], io["qf"][128 * i:128 * i + 128, :], sem=s_in)
        p.op("vector", "memset", ffacc[i][:], 0.0, sem=s_v)
    p.dma("sync", gt4[:].rearrange("p (l g) -> p l g", g=16),
          io["g16"].rearrange("(l p) g -> p l g", p=128), sem=s_in)
    FF_IN = s_in.val
    ACC_Z = s_v.val

    def load_e(e):
        p.dma("gpsimd", kfw[e % 2][:].rearrange("p (h s) -> p h s", s=256),
              io["kfwT"].rearrange("(h p) s -> p h s", p=128)
              [:, 8 * e:8 * e + 8, :], sem=s_w)
        p.dma("gpsimd", vfw[e % 2][:].rearrange("p (c d) -> p c d", d=128),
              io["vfw"].rearrange("(c p) d -> p c d", p=128)
              [:, 16 * e:16 * e + 16, :], sem=s_w)

    load_e(0)
    p.wait("tensor", s_in, FF_IN)
    p.wait("vector", s_in, FF_IN)
    p.wait("vector", s_v, ACC_Z)
    dgb = p.sbuf("M_dgb", [128, 1024], BF16)
    for e in range(16):
        if e + 1 < 16:
            p.wait("gpsimd", s_t)
            load_e(e + 1)
        p.wait("tensor", s_w, 16 * 2 * (e + 1))
        for half16 in range(2):
            p.wait("vector", s_t)
            for j in range(8):
                lt = 8 * half16 + j
                p.op("vector", "tensor_scalar_mul",
                     dgb[:, 128 * j:128 * j + 128], identbf[:],
                     gt4[:, 16 * lt + e:16 * lt + e + 1], sem=s_v)
            p.wait("vector", s_v)
            p.wait("tensor", s_v)
            ps2g, pi2g = pp2.get()
            p.op("tensor", "matmul", ps2g[:, 0:512], ones128[:],
                 dgb[:, 0:512], start=True, stop=True)
            p.op("tensor", "matmul", ps2g[:, 512:1024], ones128[:],
                 dgb[:, 512:1024], start=True, stop=True, sem=s_t)
            p.wait("vector", s_t)
            p.op("vector", "tensor_copy",
                 grep[:, 1024 * half16:1024 * half16 + 1024],
                 ps2g[:, 0:1024], sem=s_v)
            pp2.done(pi2g, s_v)
        p.wait("tensor", s_v)
        items = [(o, n, h) for (o, n) in nblocks(TPC) for h in range(8)]

        def kstep(idx):
            o, n, h = items[idx]
            ps2, pi2 = pp2.get()
            for ch in range(2):
                p.op("tensor", "matmul", ps2[:, ch * 512:ch * 512 + n],
                     kfw[e % 2][:, h * 256 + 128 * ch:
                                h * 256 + 128 * ch + 128],
                     qf[h][:, o:o + n], start=True, stop=True,
                     sem=s_t if ch == 1 else None)
            return ps2, pi2, s_t.val

        kcur = kstep(0)
        vvrec = []
        for idx, (o, n, h) in enumerate(items):
            ps2, pi2, ktv = kcur
            if idx + 1 < len(items):
                kcur = kstep(idx + 1)
            p.wait("scalar", s_t, ktv)
            if idx >= 2:
                p.wait("scalar", s_v, vvrec[idx - 2])
            ab, ab2 = abuf[idx % 2], abuf2
            p.op("scalar", "activation", ab[:, 0:1024], ps2[:, 0:1024],
                 AF.Gelu, sem=s_sc)
            pp2.done(pi2, s_sc)
            p.wait("vector", s_sc)
            for ch in range(2):
                p.op("vector", "tensor_mul",
                     ab2[:, ch * 512:ch * 512 + n],
                     ab[:, ch * 512:ch * 512 + n], grep[:, o:o + n],
                     sem=s_v)
            vvrec.append(s_v.val)
            p.wait("tensor", s_v, vvrec[-1])
            pso_, pio_ = pp.get()
            for ch in range(2):
                p.op("tensor", "matmul", pso_[:, 0:n],
                     vfw[e % 2][:, (2 * h + ch) * 128:
                                (2 * h + ch) * 128 + 128],
                     ab2[:, ch * 512:ch * 512 + n],
                     start=(ch == 0), stop=(ch == 1), sem=s_t)
            p.wait("vector", s_t)
            p.op("vector", "tensor_add", ffacc[h][:, o:o + n],
                 ffacc[h][:, o:o + n], pso_[:, 0:n], sem=s_v)
            pp.done(pio_, s_v)
    p.wait("vector", s_v)
    for i in range(8):
        p.op("vector", "tensor_copy", qf[i][:], ffacc[i][:], sem=s_v)
    p.wait("vector", s_v)
    p.scope_close(s_b)
    if stop_after == "A":
        return
    if hook_a2 is not None:
        hook_a2()
    p.scope_open()
    gfb = [p.sbuf(f"M_gfb{i}", [128, TPC], BF16) for i in range(8)]
    assert [p.nc.lookup_mloc(t).addr for t in gfb] == qf_addr
    woft = [p.sbuf(f"M_wof{i}", [128, 1024], BF16) for i in range(8)]
    woat = [p.sbuf(f"M_woa{i}", [128, 1024], BF16) for i in range(8)]
    gaft = [p.sbuf(f"M_gaf{i}", [128, TPC], BF16) for i in range(8)]
    for i in range(8):
        p.dma("gpsimd", woft[i][:], io["w_outT_ffwd"][128 * i:128 * i + 128, :],
              sem=s_w)
        p.dma("gpsimd", woat[i][:], io["woa"][128 * i:128 * i + 128, :],
              sem=s_w)
        p.dma("sync", gaft[i][:], io["gaf"][128 * i:128 * i + 128, :],
              sem=s_in)
    W_OF = s_w.val
    A2_IN = s_in.val
    p.wait("tensor", s_w, W_OF)
    p.wait("tensor", s_in, A2_IN)
    for m in range(8):
        for (o, n) in nblocks(TPC):
            ps, pi = pp.get()
            for i in range(8):
                p.op("tensor", "matmul", ps[:, 0:n],
                     woft[i][:, 128 * m:128 * m + 128], gfb[i][:, o:o + n],
                     start=(i == 0), stop=False)
            for i in range(8):
                p.op("tensor", "matmul", ps[:, 0:n],
                     woat[i][:, 128 * m:128 * m + 128], gaft[i][:, o:o + n],
                     start=False, stop=(i == 7), sem=s_t if i == 7 else None)
            p.wait("scalar", s_t)
            p.op("scalar", "activation", gsil[m][:, o:o + n], ps[:, 0:n],
                 AF.Silu, sem=s_sc)
            pp.done(pi, s_sc)
    p.wait("scalar", s_sc)
    p.wait("vector", s_sc)
    p.scope_close(s_b)
    if stop_after == "A2":
        return

    for hb in range(2):
        bo = hb * 1024
        W = 1024
        p.scope_open()
        cvh = [p.sbuf(f"B{hb}_cv{i}", [128, W + 30], BF16) for i in range(8)]
        fsh = [p.sbuf(f"B{hb}_fs{i}", [128, W], BF16) for i in range(8)]
        flh = [p.sbuf(f"B{hb}_fl{i}", [128, W], BF16) for i in range(8)]
        doh = [p.sbuf(f"B{hb}_do{i}", [128, W], BF16) for i in range(8)]
        zxh = [p.sbuf(f"B{hb}_zx{i}", [128, W], BF16) for i in range(4)]
        w1b = [p.sbuf(f"B{hb}_w1b{i}", [128, 512], BF16) for i in range(8)]
        wot = [p.sbuf(f"B{hb}_wot{i}", [128, 1024], BF16) for i in range(8)]
        fswt = [p.sbuf(f"B{hb}_fsw{i}", [128, 3], F32) for i in range(8)]
        flwt = [p.sbuf(f"B{hb}_flw{i}", [128, 31], F32) for i in range(8)]
        mrow = [p.sbuf(f"B{hb}_mr{h}", [4, W], BF16) for h in range(4)]
        rrow = [p.sbuf(f"B{hb}_rr{h}", [4, W], BF16) for h in range(4)]
        xrow = [p.sbuf(f"B{hb}_xr{h}", [4, W], BF16) for h in range(4)]
        wall = [p.sbuf(f"B{hb}_wall{h}", [128, 32], F32) for h in range(4)]
        hmid = [p.sbuf(f"B{hb}_hm{i}", [128, 512], BF16) for i in range(4)]
        sqbr = [p.sbuf(f"B{hb}_sqbr{i}", [128, 512], BF16) for i in range(2)]
        vtmt = p.sbuf(f"B{hb}_vtmt", [128, 1024], BF16)
        dotmt = p.sbuf(f"B{hb}_dotmt", [128, 1024], BF16)
        ftmt = p.sbuf(f"B{hb}_ftmt", [128, 1024], BF16)
        mx16 = p.sbuf(f"B{hb}_mx16", [128, 16], BF16)
        ltile = p.sbuf(f"B{hb}_lt", [128, 4], F32)
        lcol = p.sbuf(f"B{hb}_lc", [128, 4], F32)
        wtile = p.sbuf(f"B{hb}_wt", [128, 4], F32)
        wtile2 = p.sbuf(f"B{hb}_wt2", [128, 4], F32)
        lt16 = p.sbuf(f"B{hb}_lt16", [128, 16], F32)
        wt16 = p.sbuf(f"B{hb}_wt16", [128, 16], F32)
        gx4 = p.sbuf(f"B{hb}_gx4", [128, 4], F32)
        obuf1 = p.sbuf(f"B{hb}_ob1", [128, 512], BF16)
        obuf2 = p.sbuf(f"B{hb}_ob2", [128, 512], BF16)
        obuf3 = p.sbuf(f"B{hb}_ob3", [128, 512], BF16)
        wrep = p.sbuf(f"B{hb}_wrep", [128, 512], BF16)
        rrep = p.sbuf(f"B{hb}_rrep", [128, 512], BF16)
        rrw = p.sbuf(f"B{hb}_rrw", [1, W], F32)
        xrs = p.sbuf(f"B{hb}_xrs", [128, 512], F32)
        x1f = p.sbuf(f"B{hb}_x1f", [128, 512], F32)

        for i in range(8):
            p.dma("sync", cvh[i][:], io["cv"][128 * i:128 * i + 128,
                                              bo:bo + W + 30], sem=s_in)
            p.dma("sync", doh[i][:],
                  io["gdo_fm"][128 * i:128 * i + 128, bo:bo + W], sem=s_in)
            p.dma("sync", fswt[i][:], io["fir_s"][128 * i:128 * i + 128, :],
                  sem=s_in)
            p.dma("sync", flwt[i][:], io["fir_l"][128 * i:128 * i + 128, :],
                  sem=s_in)
            p.dma("gpsimd", w1b[i][:],
                  io["gW1"][4096 + 128 * i:4096 + 128 * i + 128, :], sem=s_w)
            p.dma("gpsimd", wot[i][:], io["Wo"][128 * i:128 * i + 128, :],
                  sem=s_w)
        for i in range(4):
            p.dma("sync", zxh[i][:], io["zx"][128 * i:128 * i + 128,
                                              bo:bo + W], sem=s_in)
        B_IN = s_in.val
        B_W = s_w.val
        p.wait("vector", s_in, B_IN)
        p.wait("tensor", s_in, B_IN)
        marker(1001 + hb)
        p.wait("vector", s_c)
        for ch in range(8):
            for (wtiles, ntap, dst) in ((fswt, 3, fsh), (flwt, 31, flh)):
                for (o, n) in nblocks(W):
                    ps2, pi2 = pp2.get()
                    dvals, tvals = [], []

                    def build_diag(j):
                        if j >= 2:
                            p.wait("vector", s_t, tvals[j - 2])
                        p.op("vector", "tensor_scalar_mul", diag[j % 2][:],
                             identbf[:], wtiles[ch][:, j:j + 1], sem=s_v)
                        dvals.append(s_v.val)

                    build_diag(0)
                    for j in range(ntap):
                        if j + 1 < ntap:
                            build_diag(j + 1)
                        p.wait("tensor", s_v, dvals[j])
                        p.op("tensor", "matmul", ps2[:, 0:n], diag[j % 2][:],
                             cvh[ch][:, o + j + 30 - ntap + 1:
                                     o + j + 30 - ntap + 1 + n],
                             start=(j == 0), stop=(j == ntap - 1), sem=s_t)
                        tvals.append(s_t.val)
                    p.wait("vector", s_t)
                    p.op("vector", "tensor_copy", dst[ch][:, o:o + n],
                         ps2[:, 0:n], sem=s_v)
                    pp2.done(pi2, s_v)
        FIR_D = s_v.val

        branches = [fsh, flh, doh, None]

        def brsl(br, gch, o, n):
            if branches[br] is None:
                return cvh[gch][:, 30 + o:30 + o + n]
            return branches[br][gch][:, o:o + n]

        p.wait("tensor", s_v, FIR_D)
        for h in range(4):
            for (o, n) in nblocks(W):
                psm, pim = pp.get()
                for br in range(4):
                    for ch in range(2):
                        p.op("tensor", "matmul", psm[0:4, 0:n], emt[br][:],
                             brsl(br, 2 * h + ch, o, n),
                             start=(br == 0 and ch == 0),
                             stop=(br == 3 and ch == 1),
                             sem=s_t if (br == 3 and ch == 1) else None)
                psr, pir = pp.get()
                for br in range(4):
                    for ch in range(2):
                        k = (br * 2 + ch) % 2
                        sl = brsl(br, 2 * h + ch, o, n)
                        if br + ch > 0:
                            p.wait("vector", s_t, s_t.val - 1)
                        p.op("vector", "tensor_mul", sqbr[k][:, 0:n], sl, sl,
                             sem=s_v)
                        p.wait("tensor", s_v)
                        p.op("tensor", "matmul", psr[0:4, 0:n], emt[br][:],
                             sqbr[k][:, 0:n], start=(br == 0 and ch == 0),
                             stop=(br == 3 and ch == 1), sem=s_t)
                p.wait("vector", s_t)
                p.op("vector", "tensor_copy", mrow[h][:, o:o + n],
                     psm[0:4, 0:n], sem=s_v)
                pp.done(pim, s_v)
                p.opw("vector", s_v, "tensor_scalar_max", sqbr[0][0:4, 0:n],
                      psr[0:4, 0:n], 1e-8)
                pp.done(pir, s_v)
                p.wait("scalar", s_v)
                p.op("scalar", "activation", rrow[h][:, o:o + n],
                     sqbr[0][0:4, 0:n], AF.Sqrt, sem=s_sc)
                p.wait("vector", s_sc)
        marker(1101 + hb)
        for ltl in range(W // 128):
            lo = bo + ltl * 128
            p.wait("sync", s_v)
            p.dma("sync", vtmt[:], io["cvt"][lo:lo + 128, :], sem=s_in)
            p.wait("vector", s_in)
            p.wait("tensor", s_in)
            def tp4(src_tiles, dst, half):
                ps, pi = ppt.get()
                for j in range(4):
                    ch = 4 * half + j
                    p.op("tensor", "transpose", ps[:, 128 * j:128 * j + 128],
                         src_tiles[ch][:, ltl * 128:ltl * 128 + 128],
                         identbf[:], sem=s_t if j == 3 else None)
                p.wait("vector", s_t)
                p.op("vector", "tensor_copy",
                     dst[:, 512 * half:512 * half + 512], ps[:, 0:512],
                     sem=s_v)
                ppt.done(pi, s_v)

            for half in range(2):
                tp4(doh, dotmt, half)
            for br, ft in ((0, fsh), (1, flh)):
                for half in range(2):
                    tp4(ft, ftmt, half)
                p.opw("vector", s_v, "tensor_reduce",
                      mx16.rearrange("p (g o) -> p g o", o=1)[:, br::4, :],
                      ftmt.rearrange("p (g d) -> p g d", d=256), AX.X, ALU.max)
            p.opw("vector", s_v, "tensor_reduce",
                  mx16.rearrange("p (g o) -> p g o", o=1)[:, 2::4, :],
                  dotmt.rearrange("p (g d) -> p g d", d=256), AX.X, ALU.max)
            p.opw("vector", s_v, "tensor_reduce",
                  mx16.rearrange("p (g o) -> p g o", o=1)[:, 3::4, :],
                  vtmt.rearrange("p (g d) -> p g d", d=256), AX.X, ALU.max)
            p.wait("tensor", s_v)
            for h in range(4):
                ps, pi = ppt.get()
                p.op("tensor", "transpose", ps[0:4, 0:128],
                     mx16[:, 4 * h:4 * h + 4], identbf[:], sem=s_t)
                p.wait("vector", s_t)
                p.op("vector", "tensor_copy",
                     xrow[h][:, ltl * 128:ltl * 128 + 128], ps[0:4, 0:128],
                     sem=s_v)
                ppt.done(pi, s_v)
        STATS_D = s_v.val

        marker(1201 + hb)
        p.wait("tensor", s_w, B_W)
        p.wait("tensor", s_v, STATS_D)
        for h in range(4):
            for bi, (o, n) in enumerate(nblocks(W)):
                for mch in range(4):
                    ps, pi = pp.get()
                    p.op("tensor", "matmul", ps[:, 0:n], identbf[:],
                         zxh[mch][:, o:o + n], start=True, stop=False)
                    for br in range(4):
                        for ch in range(2):
                            p.op("tensor", "matmul", ps[:, 0:n],
                                 w1b[br * 2 + ch][:, 128 * mch:128 * mch + 128],
                                 brsl(br, 2 * h + ch, o, n),
                                 start=False, stop=False)
                    p.op("tensor", "matmul", ps[:, 0:n],
                         wstc[0][:, 128 * mch:128 * mch + 128],
                         mrow[h][:, o:o + n], start=False, stop=False)
                    p.op("tensor", "matmul", ps[:, 0:n],
                         wstc[1][:, 128 * mch:128 * mch + 128],
                         rrow[h][:, o:o + n], start=False, stop=False)
                    p.op("tensor", "matmul", ps[:, 0:n],
                         wstc[2][:, 128 * mch:128 * mch + 128],
                         xrow[h][:, o:o + n], start=False, stop=True, sem=s_t)
                    p.wait("scalar", s_t)
                    p.op("scalar", "activation", hmid[mch][:, 0:n], ps[:, 0:n],
                         AF.Gelu, bias=gb1c[mch][:, 0:1], sem=s_sc)
                    pp.done(pi, s_sc)
                p.wait("tensor", s_sc)
                psg, pig = pp.get()
                for sub in range(n // 128):
                    for mch in range(4):
                        p.op("tensor", "matmul", psg[:, 4 * sub:4 * sub + 4],
                             hmid[mch][:, sub * 128:sub * 128 + 128],
                             gw2t[mch][:], start=(mch == 0), stop=False)
                    p.op("tensor", "matmul", psg[:, 4 * sub:4 * sub + 4],
                         onesrb[:], gb2r[:], start=False, stop=True,
                         sem=s_t if sub == n // 128 - 1 else None)
                p.wait("vector", s_t)
                p.opw("vector", s_v, "tensor_copy", lt16[:], psg[:, 0:16])
                pp.done(pig, s_v)

                def g4(t):
                    return t[:].rearrange("p (g f) -> p g f", f=4)

                def gcol(t):
                    return t[:].rearrange("p (g o) -> p g o", o=1)

                p.opw("vector", s_v, "tensor_reduce", gcol(gx4),
                      g4(lt16), AX.X, ALU.max)
                p.opw("vector", s_v, "tensor_sub", g4(lt16), g4(lt16),
                      gcol(gx4).broadcast_to([128, 4, 4]))
                p.opw("vector", s_v, "tensor_scalar_mul", lt16[:], lt16[:],
                      invtc[h][:, 0:1])
                p.wait("scalar", s_v)
                p.op("scalar", "activation", wt16[:], lt16[:], AF.Exp,
                     sem=s_sc)
                p.wait("vector", s_sc)
                p.opw("vector", s_v, "tensor_reduce", gcol(gx4),
                      g4(wt16), AX.X, ALU.add)
                p.opw("vector", s_v, "reciprocal", gx4[:], gx4[:])
                p.opw("vector", s_v, "tensor_mul", g4(wt16), g4(wt16),
                      gcol(gx4).broadcast_to([128, 4, 4]))
                p.opw("vector", s_v, "tensor_max", g4(wt16),
                      g4(wt16),
                      flrep[h][:].rearrange("p (g f) -> p g f", g=1)
                      .broadcast_to([128, 4, 4]))
                p.opw("vector", s_v, "tensor_reduce", gcol(gx4),
                      g4(wt16), AX.X, ALU.add)
                p.opw("vector", s_v, "reciprocal", gx4[:], gx4[:])
                p.opw("vector", s_v, "tensor_mul", g4(wt16), g4(wt16),
                      gcol(gx4).broadcast_to([128, 4, 4]))
                p.op("vector", "tensor_copy",
                     wall[h][:, 16 * bi:16 * bi + 16], wt16[:], sem=s_v)
        GATE_D = s_v.val

        marker(1301 + hb)
        p.wait("tensor", s_v, GATE_D)
        for h in range(4):
            for (o, n) in nblocks(W):
                for ch in range(2):
                    gch = 2 * h + ch
                    for br in range(4):
                        for sub in range(n // 128):
                            ltg = (o // 128) + sub
                            p.opw("vector", s_v, "tensor_scalar_mul",
                                  diag[sub % 2][:], identbf[:],
                                  wall[h][:, 4 * ltg + br:4 * ltg + br + 1])
                            p.wait("tensor", s_v)
                            ps, pi = pp.get()
                            p.op("tensor", "matmul", ps[:, 0:128], ones128[:],
                                 diag[sub % 2][:], start=True, stop=True,
                                 sem=s_t)
                            p.wait("vector", s_t)
                            p.op("vector", "tensor_copy",
                                 wrep[:, 128 * sub:128 * sub + 128],
                                 ps[:, 0:128], sem=s_v)
                            pp.done(pi, s_v)
                        p.wait("vector", s_v)
                        if br == 0:
                            p.opw("vector", s_v, "tensor_mul", obuf1[:, 0:n],
                                  brsl(br, gch, o, n), wrep[:, 0:n])
                        else:
                            p.opw("vector", s_v, "tensor_mul", obuf2[:, 0:n],
                                  brsl(br, gch, o, n), wrep[:, 0:n])
                            p.opw("vector", s_v, "tensor_add", obuf1[:, 0:n],
                                  obuf1[:, 0:n], obuf2[:, 0:n])
                    p.op("vector", "tensor_copy",
                         (obuf3 if ch == 1 else obuf2)[:, 0:n], obuf1[:, 0:n],
                         sem=s_v)
                    p.wait("vector", s_v)
                p.opw("vector", s_v, "tensor_mul", obuf1[:, 0:n],
                      obuf2[:, 0:n], obuf2[:, 0:n])
                p.opw("vector", s_v, "tensor_mul", wrep[:, 0:n],
                      obuf3[:, 0:n], obuf3[:, 0:n])
                p.wait("tensor", s_v)
                ps, pi = pp.get()
                p.op("tensor", "matmul", ps[0:1, 0:n], onescol[:, 0:1],
                     obuf1[:, 0:n], start=True, stop=False)
                p.op("tensor", "matmul", ps[0:1, 0:n], onescol[:, 0:1],
                     wrep[:, 0:n], start=False, stop=True, sem=s_t)
                p.wait("vector", s_t)
                p.opw("vector", s_v, "tensor_scalar_mul", rrw[0:1, o:o + n],
                      ps[0:1, 0:n], 1.0 / 256.0)
                pp.done(pi, s_v)
                p.wait("scalar", s_v)
                p.op("scalar", "activation", rrw[0:1, o:o + n],
                     rrw[0:1, o:o + n], AF.Sqrt, bias=eps5[0:1, 0:1],
                     sem=s_sc)
                p.wait("vector", s_sc)
                p.opw("vector", s_v, "reciprocal", rrw[0:1, o:o + n],
                      rrw[0:1, o:o + n])
                p.wait("tensor", s_v)
                ps, pi = pp.get()
                p.op("tensor", "matmul", ps[:, 0:n], onesrf[:],
                     rrw[0:1, o:o + n], start=True, stop=True, sem=s_t)
                p.wait("vector", s_t)
                p.opw("vector", s_v, "tensor_copy", rrep[:, 0:n], ps[:, 0:n])
                pp.done(pi, s_v)
                for ch, ob in ((0, obuf2), (1, obuf3)):
                    p.opw("vector", s_v, "tensor_mul", ob[:, 0:n], ob[:, 0:n],
                          rrep[:, 0:n])
                    p.opw("vector", s_v, "tensor_scalar_mul", ob[:, 0:n],
                          ob[:, 0:n], onrm[ch][:, 0:1])
                    p.op("vector", "tensor_copy",
                         fsh[2 * h + ch][:, o:o + n], ob[:, 0:n], sem=s_v)
                    p.wait("vector", s_v)
        ON_D = s_v.val

        p.wait("tensor", s_v, ON_D)
        for m in range(8):
            for (o, n) in nblocks(W):
                ps, pi = pp.get()
                for i in range(8):
                    p.op("tensor", "matmul", ps[:, 0:n],
                         wot[i][:, 128 * m:128 * m + 128], fsh[i][:, o:o + n],
                         start=(i == 0), stop=(i == 7),
                         sem=s_t if i == 7 else None)
                p.dma("sync", xrs[:, 0:n],
                      io["xTr"][128 * m:128 * m + 128, bo + o:bo + o + n],
                      sem=s_in)
                p.wait("vector", s_in)
                p.wait("vector", s_t)
                p.opw("vector", s_v, "tensor_mul", x1f[:, 0:n], ps[:, 0:n],
                      gsil[m][:, bo + o:bo + o + n])
                pp.done(pi, s_v)
                p.opw("vector", s_v, "tensor_add", x1f[:, 0:n], x1f[:, 0:n],
                      xrs[:, 0:n])
                p.op("vector", "tensor_copy", gsil[m][:, bo + o:bo + o + n],
                     x1f[:, 0:n], sem=s_v)
                p.wait("sync", s_v)
                p.dma("sync", io["x1"][128 * m:128 * m + 128, bo + o:bo + o + n],
                      x1f[:, 0:n], sem=s_out)
                p.wait("vector", s_out)
                p.wait("sync", s_out)
        marker(1401 + hb)
        p.scope_close(s_b)
    if stop_after == "B":
        return

    p.scope_open()
    x1b = gsil
    rows2 = p.sbuf("C_rows2", [1, 3 * TPC], F32)
    brep2 = [p.sbuf(f"C_brep{i}", [128, TPC], BF16) for i in range(2)]
    sqb2 = [p.sbuf(f"C_sqb{i}", [128, 512], BF16) for i in range(2)]
    for (o, n) in nblocks(TPC):
        ps, pi = pp.get()
        for i in range(8):
            p.op("tensor", "matmul", ps[0:1, 0:n], onescol[:, 0:1],
                 x1b[i][:, o:o + n], start=(i == 0), stop=(i == 7),
                 sem=s_t if i == 7 else None)
        p.wait("scalar", s_t)
        p.op("scalar", "copy", rows2[0:1, o:o + n], ps[0:1, 0:n], sem=s_sc)
        pp.done(pi, s_sc)
    sq_lastc = [None, None]
    for (o, n) in nblocks(TPC):
        ps, pi = pp.get()
        for i in range(8):
            bq = i % 2
            sq = sqb2[bq]
            if sq_lastc[bq] is not None:
                p.wait("vector", s_t, sq_lastc[bq])
            p.op("vector", "tensor_mul", sq[:, 0:n], x1b[i][:, o:o + n],
                 x1b[i][:, o:o + n], sem=s_v)
            p.wait("tensor", s_v)
            p.op("tensor", "matmul", ps[0:1, 0:n], onescol[:, 0:1],
                 sq[:, 0:n], start=(i == 0), stop=(i == 7), sem=s_t)
            sq_lastc[bq] = s_t.val
        p.wait("scalar", s_t)
        p.op("scalar", "copy", rows2[0:1, TPC + o:TPC + o + n], ps[0:1, 0:n],
             sem=s_sc)
        pp.done(pi, s_sc)
    R0, R1, R2 = 0, TPC, 2 * TPC
    p.wait("vector", s_sc)
    p.opw("vector", s_v, "tensor_scalar_mul", rows2[0:1, R0:R0 + TPC],
          rows2[0:1, R0:R0 + TPC], 1.0 / DM)
    p.opw("vector", s_v, "tensor_mul", rows2[0:1, R2:R2 + TPC],
          rows2[0:1, R0:R0 + TPC], rows2[0:1, R0:R0 + TPC])
    p.opw("vector", s_v, "tensor_scalar_mul", rows2[0:1, R1:R1 + TPC],
          rows2[0:1, R1:R1 + TPC], 1.0 / DM)
    p.op("vector", "tensor_sub", rows2[0:1, R1:R1 + TPC],
         rows2[0:1, R1:R1 + TPC], rows2[0:1, R2:R2 + TPC], sem=s_v)
    p.wait("scalar", s_v)
    p.op("scalar", "activation", rows2[0:1, R1:R1 + TPC],
         rows2[0:1, R1:R1 + TPC], AF.Sqrt, bias=eps5[0:1, 0:1], sem=s_sc)
    p.wait("vector", s_sc)
    p.opw("vector", s_v, "reciprocal", rows2[0:1, R1:R1 + TPC],
          rows2[0:1, R1:R1 + TPC])
    p.opw("vector", s_v, "tensor_mul", rows2[0:1, R2:R2 + TPC],
          rows2[0:1, R0:R0 + TPC], rows2[0:1, R1:R1 + TPC])
    p.op("vector", "tensor_scalar_mul", rows2[0:1, R2:R2 + TPC],
         rows2[0:1, R2:R2 + TPC], -1.0, sem=s_v)
    p.wait("tensor", s_v)
    for r in range(2):
        base = R1 if r == 0 else R2
        for (o, n) in nblocks(TPC):
            ps, pi = pp.get()
            p.op("tensor", "matmul", ps[:, 0:n], onesrf[:],
                 rows2[0:1, base + o:base + o + n], start=True, stop=True,
                 sem=s_t)
            p.wait("vector", s_t)
            p.op("vector", "tensor_copy", brep2[r][:, o:o + n], ps[:, 0:n],
                 sem=s_v)
            pp.done(pi, s_v)
    for i in range(8):
        p.opw("vector", s_v, "tensor_mul", sqb2[0][:, 0:512],
              x1b[i][:, 0:512], x1b[i][:, 0:512]) if False else None
        p.opw("vector", s_v, "tensor_mul", x1b[i][:], x1b[i][:], brep2[0][:])
        p.op("vector", "tensor_add", x1b[i][:], x1b[i][:], brep2[1][:],
             sem=s_v)
        p.wait("sync", s_v)
        p.dma("sync", io["xn2"][128 * i:128 * i + 128, :], x1b[i][:],
              sem=s_out)
    p.wait("sync", s_out)
    p.scope_close(s_b)


STAGE_M_INS = {
    "qf": ([DM, TPC], BF16), "g16": ([TPC, 16], F32),
    "cv": ([DM, TPC + 30], BF16), "cvt": ([TPC, DM], BF16),
    "gdo_fm": ([DM, TPC], BF16), "gdo_tm": ([TPC, DM], BF16),
    "gaf": ([DM, TPC], BF16), "woa": ([DM, DM], F32),
    "zx": ([512, TPC], BF16),
    "xTr": ([DM, TPC], F32),
    "kfwT": ([16 * 8 * 128, 256], F32), "vfw": ([16 * 8 * 256, 128], F32),
    "w_outT_ffwd": ([DM, DM], F32),
    "gW1": ([5120, 512], F32), "gb1": ([512, 1], F32),
    "gW2": ([512, 4], F32), "gb2": ([1, 4], F32),
    "temp4": ([1, 4], F32), "epsf16": ([1, 16], F32),
    "onorm": ([256, 1], F32), "Wo": ([DM, DM], F32),
    "fir_s": ([DM, 3], F32), "fir_l": ([DM, 31], F32),
    "emat": ([4 * 128, 4], BF16), "gstat": ([24 * 128, 4], BF16),
    "ones2": ([128, 2], BF16), "onesrf": ([1, 128], F32),
    "onesrb": ([1, 128], BF16), "ident": ([128, 128], BF16),
    "ones128": ([128, 128], BF16),
}
STAGE_M_OUTS = {
    "x1": ([DM, TPC], F32), "xn2": ([DM, TPC], BF16),
}


def stage_m_inputs(inputs, aouts, do_full, gaf_own, c):
    t0 = c * TPC
    do_tm = do_full[t0:t0 + TPC, :]
    emat = np.zeros((4 * 128, 4), np.float32)
    for br in range(4):
        emat[128 * br:128 * br + 128, br] = 1.0 / 256.0
    gstat = np.zeros((24 * 128, 4), np.float32)
    for gi in range(24):
        gstat[128 * gi:128 * gi + 128, gi // 6] = 1.0
    return {
        "qf": aouts["qf"][c], "g16": aouts["g16"][c], "cv": aouts["cv"][c],
        "cvt": aouts["cvt"][c],
        "gdo_fm": np.ascontiguousarray(do_tm.T), "gdo_tm": do_tm,
        "gaf": gaf_own,
        "zx": aouts["zx"][c],
        **stage_m_ext(inputs, c),
    }


def stage_m_ext(inputs, c):
    t0 = c * TPC
    emat = np.zeros((4 * 128, 4), np.float32)
    for br in range(4):
        emat[128 * br:128 * br + 128, br] = 1.0 / 256.0
    gstat = np.zeros((24 * 128, 4), np.float32)
    for gi in range(24):
        gstat[128 * gi:128 * gi + 128, gi // 6] = 1.0
    return {
        "woa": np.ascontiguousarray(inputs["L0_ffn_w_out"].T[0:1024, :]),
        "xTr": np.ascontiguousarray(inputs["L0_attn_x"][0][t0:t0 + TPC].T),
        "kfwT": np.ascontiguousarray(
            inputs["L0_ffn_k_ffwd"].transpose(0, 1, 3, 2).reshape(
                16 * 8 * 128, 256)),
        "vfw": inputs["L0_ffn_v_ffwd"].reshape(16 * 8 * 256, 128),
        "w_outT_ffwd": np.ascontiguousarray(
            inputs["L0_ffn_w_out"].T[1024:2048, :]),
        "gW1": inputs["L0_attn_gW1"],
        "gb1": inputs["L0_attn_gb1"].reshape(512, 1),
        "gW2": inputs["L0_attn_gW2"],
        "gb2": inputs["L0_attn_gb2"].reshape(1, 4),
        "temp4": inputs["L0_attn_temp"].reshape(1, 4),
        "epsf16": inputs["L0_attn_eps_floor"].reshape(1, 16),
        "onorm": inputs["L0_attn_onorm_w"].reshape(256, 1),
        "Wo": inputs["L0_attn_Wo"],
        "fir_s": inputs["L0_attn_fir_s"].reshape(1024, 3),
        "fir_l": inputs["L0_attn_fir_l"].reshape(1024, 31),
        "emat": emat.astype(BF), "gstat": gstat.astype(BF),
        "ones2": np.ones((128, 2), BF),
        "onesrf": np.ones((1, 128), np.float32),
        "onesrb": np.ones((1, 128), BF),
        "ident": np.eye(128, dtype=BF),
        "ones128": np.ones((128, 128), BF),
    }


def build_stage_m(stop_after=None):
    nc = bass.Bass("TRN2", target_bir_lowering=False)
    io = {}
    for k, (shape, dt) in STAGE_M_INS.items():
        io[k] = nc.dram_tensor(k, shape, dt, kind="ExternalInput").ap()
    for k, (shape, dt) in STAGE_M_OUTS.items():
        io[k] = nc.dram_tensor(k, shape, dt, kind="ExternalOutput").ap()
    p = Prog(nc)
    stage_m(p, io, stop_after=stop_after)
    p.emit_and_finish()
    return nc


def bitonic_passes():
    out = []
    k = 2
    while k <= 32:
        j = k // 2
        while j >= 1:
            out.append((k, j))
            j //= 2
        k *= 2
    return out


def ap_set(buf, base, k, j, G):
    outer = 32 // (2 * k) if k < 32 else 1
    mid = k // (2 * j)
    low = j
    return buf[:, base:].rearrange("p (x) -> p x")[:, 0:1] if False else None


def stage_l1(p, io, fz=None):
    s_w = p.sem("lw"); s_c = p.sem("lc"); s_v = p.sem("lv")
    s_t = p.sem("lt"); s_sc = p.sem("lsc"); s_out = p.sem("lout")
    s_in = p.sem("lin"); s_b = p.sem("lb")

    onescol = p.sbuf("L_ones", [128, 2], BF16)
    onesrf = p.sbuf("L_onesrf", [1, 128], F32)
    onesrb = p.sbuf("L_onesrb", [1, 128], BF16)
    identbf = p.sbuf("L_id", [128, 128], BF16)
    ones128 = p.sbuf("L_o128", [128, 128], BF16)
    diag = [p.sbuf(f"L_diag{i}", [128, 128], BF16) for i in range(2)]
    kramp = p.sbuf("L_kramp", [128, 512], F32)
    s1t = [p.sbuf(f"L_s1{i}", [128, TPC], BF16) for i in range(8)]

    pp = PP(p, "L_pp", 2, [128, 512])
    pp2 = PP(p, "L_pp2", 2, [128, 1024])
    ppt = PP(p, "L_ppt", 2, [128, 128], BF16)

    p.dma("sync", onescol[:], io["ones2"], sem=s_c)
    p.dma("sync", onesrf[:], io["onesrf"], sem=s_c)
    p.dma("sync", onesrb[:], io["onesrb"], sem=s_c)
    p.dma("sync", identbf[:], io["ident"], sem=s_c)
    p.dma("sync", ones128[:], io["ones128"], sem=s_c)
    p.dma("sync", kramp[:], io["kramp"], sem=s_c)
    p.wait("vector", s_c)
    p.wait("tensor", s_c)

    p.scope_open()
    kpT = [p.sbuf(f"L_kp{i}", [128, 32], BF16) for i in range(8)]
    wkt = [p.sbuf(f"L_wk{i}", [128, 1024], BF16) for i in range(8)]
    keystm = p.sbuf("L_keystm", [32, 1024], BF16)
    keysfm = [p.sbuf(f"L_kf{i}", [128, 32], BF16) for i in range(8)]
    wkh = [p.sbuf(f"L_wkh{i}", [128, 128], BF16) for i in range(8)]
    bkhr = p.sbuf("L_bkhr", [1, 128], BF16)
    kslt = p.sbuf("L_kslt", [32, 128], BF16)
    kslf = p.sbuf("L_kslf", [128, 32], BF16)
    valstm = p.sbuf("L_valstm", [32, 128], BF16)
    bkr = p.sbuf("L_bkr", [1, 1024], BF16)
    bqc = [p.sbuf(f"L_bq{i}", [128, 1], F32) for i in range(1)]
    boc = p.sbuf("L_boc", [128, 8], F32)
    q2 = p.sbuf("L_q2", [128, L], BF16)
    atp = p.sbuf("L_atp", [32, L], BF16)
    sbat = p.sbuf("L_sbat", [128, 512], F32)
    zsort = p.sbuf("L_zsort", [128, 512], F32)
    ppng = [p.sbuf(f"L_png{i}", [128, 512], F32) for i in range(2)]
    cum = p.sbuf("L_cum", [128, 512], F32)
    trow = p.sbuf("L_trow", [128, 16], F32)
    trow2 = p.sbuf("L_trow2", [128, 16], F32)
    pbf = p.sbuf("L_pbf", [128, 512], BF16)
    opm = [p.sbuf(f"L_opm{i}", [128, 1024], BF16) for i in range(8)]

    for i in range(8):
        p.dma("gpsimd", kpT[i][:], io["kpT"][128 * i:128 * i + 128, :], sem=s_w)
        p.dma("gpsimd", wkt[i][:], io["Wk1"][128 * i:128 * i + 128, :], sem=s_w)
    p.dma("gpsimd", bkr[:], io["bk1"], sem=s_w)
    p.dma("sync", bqc[0][:], io["bq1h"], sem=s_w)
    p.dma("sync", boc[:], io["bo1"], sem=s_w)
    W_KV = s_w.val
    p.wait("tensor", s_w, W_KV)
    for half in range(2):
        ps2, pi2 = pp2.get()
        for i in range(8):
            p.op("tensor", "matmul", ps2[0:32, 0:512], kpT[i][:],
                 wkt[i][:, 512 * half:512 * half + 512], start=(i == 0),
                 stop=False)
        p.op("tensor", "matmul", ps2[0:32, 0:512], onesrb[0:1, 0:32],
             bkr[0:1, 512 * half:512 * half + 512], start=False, stop=True,
             sem=s_t)
        p.wait("vector", s_t)
        p.op("vector", "tensor_copy", keystm[:, 512 * half:512 * half + 512],
             ps2[0:32, 0:512], sem=s_v)
        pp2.done(pi2, s_v)
    p.wait("tensor", s_v)
    for i in range(8):
        ps, pi = ppt.get()
        p.op("tensor", "transpose", ps[0:128, 0:32],
             keystm[:, 128 * i:128 * i + 128], identbf[0:32, 0:32], sem=s_t)
        p.wait("vector", s_t)
        p.op("vector", "tensor_copy", keysfm[i][:], ps[0:128, 0:32], sem=s_v)
        ppt.done(pi, s_v)
    for i in range(8):
        p.dma("gpsimd", wkh[i][:], io["Wk1h"][128 * i:128 * i + 128, :],
              sem=s_w)
    p.dma("gpsimd", bkhr[:], io["bk1h"], sem=s_w)
    W_KH = s_w.val
    p.wait("tensor", s_w, W_KH)
    psk, pik = pp.get()
    for i in range(8):
        p.op("tensor", "matmul", psk[0:32, 0:128], kpT[i][:], wkh[i][:],
             start=(i == 0), stop=False)
    p.op("tensor", "matmul", psk[0:32, 0:128], onesrb[0:1, 0:32], bkhr[:],
         start=False, stop=True, sem=s_t)
    p.wait("vector", s_t)
    p.op("vector", "tensor_copy", kslt[:], psk[0:32, 0:128], sem=s_v)
    pp.done(pik, s_v)
    p.wait("tensor", s_v)
    psk2, pik2 = ppt.get()
    p.op("tensor", "transpose", psk2[0:128, 0:32], kslt[:],
         identbf[0:32, 0:32], sem=s_t)
    p.wait("vector", s_t)
    p.op("vector", "tensor_copy", kslf[:], psk2[0:128, 0:32], sem=s_v)
    ppt.done(pik2, s_v)
    p.wait("gpsimd", s_v)
    for i in range(8):
        p.dma("gpsimd", wkh[i][:], io["Wv1h"][128 * i:128 * i + 128, :],
              sem=s_w)
    p.dma("gpsimd", bkhr[:], io["bv1h"], sem=s_w)
    W_V1 = s_w.val
    p.wait("tensor", s_w, W_V1)
    psv, piv = pp.get()
    for i in range(8):
        p.op("tensor", "matmul", psv[0:32, 0:128], keysfm[i][:], wkh[i][:],
             start=(i == 0), stop=False)
    p.op("tensor", "matmul", psv[0:32, 0:128], onesrb[0:1, 0:32], bkhr[:],
         start=False, stop=True, sem=s_t)
    p.wait("vector", s_t)
    p.op("vector", "tensor_copy", valstm[:], psv[0:32, 0:128], sem=s_v)
    pp.done(piv, s_v)
    p.wait("gpsimd", s_v)
    for i in range(8):
        p.dma("gpsimd", wkt[i][:, 0:128],
              io["Wq1h"][128 * i:128 * i + 128, :], sem=s_w)
    W_Q1 = s_w.val
    p.wait("vector", s_w, W_Q1)
    p.opw("vector", s_v, "tensor_scalar_mul", bqc[0][:], bqc[0][:], 0.125)
    p.wait("tensor", s_w, W_Q1)
    gxn = io["gxn"]
    xch = [[p.sbuf(f"L_xch{bb}_{i}", [128, 512], BF16) for i in range(8)]
           for bb in range(2)]
    Q_IN0 = s_in.val
    Q_V0 = s_v.val

    def q2_load(blk):
        bb = blk % 2
        if blk >= 2:
            p.wait("sync", s_v, Q_V0 + blk - 1)
        sblk, lblk = divmod(blk, TPC // 512)
        for i in range(8):
            if fz is None:
                p.dma("sync", xch[bb][i][:],
                      gxn[128 * i:128 * i + 128,
                          512 * blk:512 * blk + 512], sem=s_in)
            else:
                p.dma("sync", xch[bb][i][:],
                      fz["Gx"][sblk * DM + 128 * i:sblk * DM + 128 * i + 128,
                               512 * lblk:512 * lblk + 512], sem=s_in)

    q2_load(0)
    for blk in range(L // 512):
        bb = blk % 2
        if blk + 1 < L // 512:
            q2_load(blk + 1)
        ps, pi = pp.get()
        p.wait("tensor", s_in)
        for i in range(8):
            p.op("tensor", "matmul", ps[:, 0:512], wkt[i][:, 0:128],
                 xch[bb][i][:], start=(i == 0), stop=(i == 7),
                 sem=s_t if i == 7 else None)
        p.wait("vector", s_t)
        p.op("vector", "tensor_scalar", q2[:, 512 * blk:512 * blk + 512],
             ps[:, 0:512], 0.125, bqc[0][:, 0:1], ALU.mult, op1=ALU.add,
             sem=s_v)
        pp.done(pi, s_v)
    Q2_D = s_v.val
    p.wait("tensor", s_v, Q2_D)

    for hl in range(2):
        kslice = kslf
        kbase = 64 * hl
        for bat in range(L // 2048):
            for k16 in range(16):
                lt = bat * 16 + k16
                ps, pi = pp.get()
                p.op("tensor", "matmul", ps[:, 0:32],
                     q2[64 * hl:64 * hl + 64, 128 * lt:128 * lt + 128],
                     kslice[kbase:kbase + 64, :], start=True, stop=True,
                     sem=s_t)
                p.wait("vector", s_t)
                p.op("vector", "tensor_copy", sbat[:, 32 * k16:32 * k16 + 32],
                     ps[:, 0:32], sem=s_v)
                pp.done(pi, s_v)
            p.wait("vector", s_v)
            cur = sbat
            pb = 0
            for (kk, jj) in bitonic_passes():
                nxt = ppng[pb]
                outer = 32 // (2 * kk)
                mid = kk // (2 * jj)

                def vsel(buf, base):
                    v = buf[:].rearrange(
                        "p (g o x m l) -> p g o x m l",
                        g=16, o=max(outer, 1), x=(2 if outer >= 1 else 2),
                        m=mid, l=jj) if False else None
                    return v
                def view(buf, dbit, bbit):
                    r = buf[:].rearrange(
                        "p (g o d m b l) -> p g o d m b l",
                        g=16, o=max(outer, 1), d=2 if outer >= 1 else 1,
                        m=mid, b=2, l=jj)
                    return r[:, :, :, dbit:dbit + 1, :, bbit:bbit + 1, :]
                for dbit in range(2 if kk < 32 else 1):
                    lo_in = view(cur, dbit, 0)
                    hi_in = view(cur, dbit, 1)
                    lo_out = view(nxt, dbit, 0)
                    hi_out = view(nxt, dbit, 1)
                    desc = (dbit == 0)
                    p.op("vector", "tensor_tensor", lo_out, lo_in, hi_in,
                         op=(ALU.max if desc else ALU.min), sem=s_v)
                    p.op("vector", "tensor_tensor", hi_out, lo_in, hi_in,
                         op=(ALU.min if desc else ALU.max), sem=s_v)
                p.wait("vector", s_v)
                cur = nxt
                pb = 1 - pb
            p.opw("vector", s_v, "tensor_copy", zsort[:], cur[:])
            for st in range(5):
                d = 1 << st
                nxt = ppng[pb]
                p.op("vector", "tensor_copy",
                     nxt[:].rearrange("p (g x) -> p g x", x=32)[:, :, 0:d],
                     cur[:].rearrange("p (g x) -> p g x", x=32)[:, :, 0:d],
                     sem=s_v)
                p.op("vector", "tensor_tensor",
                     nxt[:].rearrange("p (g x) -> p g x", x=32)[:, :, d:32],
                     cur[:].rearrange("p (g x) -> p g x", x=32)[:, :, d:32],
                     cur[:].rearrange("p (g x) -> p g x", x=32)[:, :, 0:32 - d],
                     op=ALU.add, sem=s_v)
                p.wait("vector", s_v)
                cur = nxt
                pb = 1 - pb
            p.opw("vector", s_v, "tensor_mul", ppng[pb][:], zsort[:], kramp[:])
            p.opw("vector", s_v, "tensor_scalar_add", ppng[pb][:],
                  ppng[pb][:], 1.0)
            p.opw("vector", s_v, "tensor_tensor", ppng[pb][:], ppng[pb][:],
                  cur[:], op=ALU.is_gt)
            p.opw("vector", s_v, "tensor_reduce",
                  trow[:].rearrange("p (g o) -> p g o", o=1),
                  ppng[pb][:].rearrange("p (g x) -> p g x", x=32), AX.X,
                  ALU.add)
            p.opw("vector", s_v, "tensor_mul", cum[:], zsort[:], ppng[pb][:])
            p.opw("vector", s_v, "tensor_reduce",
                  trow2[:].rearrange("p (g o) -> p g o", o=1),
                  cum[:].rearrange("p (g x) -> p g x", x=32), AX.X, ALU.add)
            p.opw("vector", s_v, "tensor_scalar_add", trow2[:], trow2[:], -1.0)
            p.opw("vector", s_v, "reciprocal", trow[:], trow[:])
            p.opw("vector", s_v, "tensor_mul", trow2[:], trow2[:], trow[:])
            p.opw("vector", s_v, "tensor_sub",
                  cum[:].rearrange("p (g x) -> p g x", x=32),
                  sbat[:].rearrange("p (g x) -> p g x", x=32),
                  trow2[:].rearrange("p (g o) -> p g o", o=1)
                  .broadcast_to([128, 16, 32]))
            p.opw("vector", s_v, "tensor_scalar_max", cum[:], cum[:], 0.0)
            p.op("vector", "tensor_copy", pbf[:], cum[:], sem=s_v)
            p.wait("tensor", s_v)
            for k16 in range(16):
                lt = bat * 16 + k16
                ps, pi = ppt.get()
                p.op("tensor", "transpose", ps[0:32, 0:128],
                     pbf[:, 32 * k16:32 * k16 + 32], identbf[:], sem=s_t)
                p.wait("vector", s_t)
                p.op("vector", "tensor_copy",
                     atp[0:32, :].rearrange("p (t i) -> p t i", t=16)
                     [:, :, lt * 8:lt * 8 + 8],
                     ps[0:32, 0:128].rearrange("p (a b) -> p b a", b=16),
                     sem=s_v)
                ppt.done(pi, s_v)
        p.wait("tensor", s_v)
        for jc in range(8):
            for ib in range(2):
                ps, pi = pp.get()
                p.op("tensor", "matmul", ps[0:64, 0:512],
                     valstm[:, 64 * hl:64 * hl + 64],
                     atp[0:32, (2 * jc) * 1024 + 512 * ib:
                         (2 * jc) * 1024 + 512 * ib + 512],
                     start=True, stop=True)
                p.op("tensor", "matmul", ps[64:128, 0:512],
                     valstm[:, 64 * hl:64 * hl + 64],
                     atp[0:32, (2 * jc + 1) * 1024 + 512 * ib:
                         (2 * jc + 1) * 1024 + 512 * ib + 512],
                     start=True, stop=True, sem=s_t)
                p.wait("vector", s_t)
                p.op("vector", "tensor_copy",
                     opm[jc][:, 512 * ib:512 * ib + 512], ps[:, 0:512],
                     sem=s_v)
                pp.done(pi, s_v)
        p.wait("gpsimd", s_v)
        for i in range(8):
            p.dma("gpsimd", wkt[i][:], io["Wo1"][128 * i:128 * i + 128, :],
                  sem=s_w)
        p.wait("tensor", s_w)
        p.wait("tensor", s_v)
        for m in range(8):
            for ib in range(2):
                ps, pi = pp.get()
                for jc in range(8):
                    p.op("tensor", "matmul", ps[:, 0:512],
                         wkt[jc][:, 128 * m:128 * m + 128],
                         opm[jc][:, 512 * ib:512 * ib + 512],
                         start=(jc == 0), stop=(jc == 7),
                         sem=s_t if jc == 7 else None)
                p.wait("vector", s_t)
                p.op("vector", "tensor_scalar_add",
                     s1t[m][:, hl * 1024 + 512 * ib:hl * 1024 + 512 * ib + 512],
                     ps[:, 0:512], boc[:, m:m + 1], sem=s_v)
                pp.done(pi, s_v)
    p.wait("sync", s_v)
    p.scope_close(s_b)

    p.scope_open()
    xnb = [p.sbuf(f"L_xnb{i}", [128, 512], BF16) for i in range(8)]
    gwt = [p.sbuf(f"L_gwt{i}", [128, 8], BF16) for i in range(8)]
    mall = p.sbuf("L_mall", [128, 128], F32)
    mfm = p.sbuf("L_mfm", [8, TPC], BF16)
    ltile = p.sbuf("L_lt", [128, 8], F32)
    lcol = p.sbuf("L_lc", [128, 4], F32)
    ltile2 = p.sbuf("L_lt2", [128, 8], F32)
    ltile3 = p.sbuf("L_lt3", [128, 8], F32)
    w1t = [p.sbuf(f"L_w1t{i}", [128, 512], BF16) for i in range(16)]
    b2t = p.sbuf("L_b2t", [8, 1024], BF16)
    b1all = p.sbuf("L_b1all", [128, 32], F32)
    hsb = [p.sbuf(f"L_hsb{i}", [128, 512], BF16) for i in range(32)]
    mrep = p.sbuf("L_mrep", [128, 512], BF16)
    x1c = p.sbuf("L_x1c", [128, 512], F32)
    of = p.sbuf("L_of", [128, 512], F32)

    for i in range(8):
        p.dma("gpsimd", gwt[i][:], io["gate_w1"][128 * i:128 * i + 128, :],
              sem=s_w)
    p.dma("gpsimd", b2t[:], io["b2_1"], sem=s_w)
    GW_D = s_w.val
    p.wait("tensor", s_w, GW_D)
    for lt in range(TPC // 128):
        ps, pi = pp.get()
        for i in range(8):
            p.dma("sync", xnb[i][:, 0:128],
                  io["xnloc"][128 * i:128 * i + 128,
                              128 * lt:128 * lt + 128], sem=s_in)
            p.wait("tensor", s_in)
            p.op("tensor", "matmul", ps[:, 0:8], xnb[i][:, 0:128], gwt[i][:],
                 start=(i == 0), stop=(i == 7), sem=s_t)
            if i < 7:
                p.wait("sync", s_t, s_t.val)
        p.wait("vector", s_t)
        p.opw("vector", s_v, "tensor_copy", ltile[:], ps[:, 0:8])
        pp.done(pi, s_v)
        p.opw("vector", s_v, "reduce_max", lcol[:, 0:1], ltile[:], axis=AX.X)
        p.op("vector", "tensor_scalar_mul", lcol[:, 0:1], lcol[:, 0:1], -1.0,
             sem=s_v)
        p.wait("scalar", s_v)
        p.op("scalar", "activation", ltile2[:], ltile[:], AF.Exp,
             bias=lcol[:, 0:1], sem=s_sc)
        p.wait("vector", s_sc)
        p.opw("vector", s_v, "tensor_reduce", lcol[:, 1:2], ltile2[:], AX.X,
              ALU.add)
        p.opw("vector", s_v, "reciprocal", lcol[:, 1:2], lcol[:, 1:2])
        p.opw("vector", s_v, "tensor_scalar_mul", ltile2[:], ltile2[:],
              lcol[:, 1:2])
        p.opw("vector", s_v, "reduce_max", lcol[:, 2:3], ltile2[:], axis=AX.X)
        p.opw("vector", s_v, "tensor_scalar", ltile3[:], ltile2[:],
              lcol[:, 2:3], None, ALU.is_ge)
        p.opw("vector", s_v, "scalar_tensor_tensor", ltile[:], ltile3[:],
              -1e9, ltile2[:], op0=ALU.mult, op1=ALU.add)
        p.opw("vector", s_v, "reduce_max", lcol[:, 3:4], ltile[:], axis=AX.X)
        p.opw("vector", s_v, "tensor_scalar", ltile3[:], ltile2[:],
              lcol[:, 3:4], None, ALU.is_ge)
        p.opw("vector", s_v, "tensor_add", lcol[:, 2:3], lcol[:, 2:3],
              lcol[:, 3:4])
        p.opw("vector", s_v, "tensor_scalar_mul", mall[:, 8 * lt:8 * lt + 8],
              ltile3[:], lcol[:, 2:3])
        p.opw("vector", s_v, "tensor_copy", ltile3[:],
              mall[:, 8 * lt:8 * lt + 8])
        p.wait("tensor", s_v)
        ps2, pi2 = ppt.get()
        p.opw("vector", s_v, "tensor_copy", mrep[:, 0:8],
              mall[:, 8 * lt:8 * lt + 8])
        p.wait("tensor", s_v)
        p.op("tensor", "transpose", ps2[0:8, 0:128], mrep[:, 0:8],
             identbf[:], sem=s_t)
        p.wait("vector", s_t)
        p.op("vector", "tensor_copy", mfm[:, 128 * lt:128 * lt + 128],
             ps2[0:8, 0:128], sem=s_v)
        ppt.done(pi2, s_v)
    TOP2_D = s_v.val

    p.dma("sync", b1all[:], io["b1c"], sem=s_c)
    p.wait("scalar", s_c)
    for bi, (o, n) in enumerate(nblocks(TPC)):
        for i in range(8):
            p.dma("sync", xnb[i][:, 0:n],
                  io["xnloc"][128 * i:128 * i + 128, o:o + n], sem=s_in)
        B_IN = s_in.val
        p.wait("tensor", s_in, B_IN)
        for e in range(8):
            p.wait("gpsimd", s_t)
            for i in range(8):
                p.dma("gpsimd", w1t[(e % 2) * 8 + i][:],
                      io["w1_1"][1024 * e + 128 * i:1024 * e + 128 * i + 128, :],
                      sem=s_w)
            p.wait("tensor", s_w)
            for mch in range(4):
                ps, pi = pp.get()
                for i in range(8):
                    p.op("tensor", "matmul", ps[:, 0:n],
                         w1t[(e % 2) * 8 + i][:, 128 * mch:128 * mch + 128],
                         xnb[i][:, 0:n], start=(i == 0), stop=(i == 7),
                         sem=s_t if i == 7 else None)
                p.wait("scalar", s_t)
                p.op("scalar", "activation", hsb[4 * e + mch][:, 0:n],
                     ps[:, 0:n], AF.Gelu,
                     bias=b1all[:, 4 * e + mch:4 * e + mch + 1], sem=s_sc)
                pp.done(pi, s_sc)
            p.wait("vector", s_sc)
            for sub in range(n // 128):
                ltg = (o // 128) + sub
                p.opw("vector", s_v, "tensor_scalar_mul", diag[sub % 2][:],
                      identbf[:], mall[:, 8 * ltg + e:8 * ltg + e + 1])
                p.wait("tensor", s_v)
                ps, pi = pp.get()
                p.op("tensor", "matmul", ps[:, 0:128], ones128[:],
                     diag[sub % 2][:], start=True, stop=True, sem=s_t)
                p.wait("vector", s_t)
                p.op("vector", "tensor_copy",
                     mrep[:, 128 * sub:128 * sub + 128], ps[:, 0:128], sem=s_v)
                pp.done(pi, s_v)
            p.wait("vector", s_v)
            for mch in range(4):
                p.opw("vector", s_v, "tensor_mul", hsb[4 * e + mch][:, 0:n],
                      hsb[4 * e + mch][:, 0:n], mrep[:, 0:n])
        HS_D = s_v.val
        p.wait("gpsimd", s_v, HS_D)
        p.wait("tensor", s_v, HS_D)
        seq = [(m, e, mch) for m in range(8) for e in range(8)
               for mch in range(4)]
        W2_W0 = s_w.val
        trec = []

        def w2_load(k):
            m, e, mch = seq[k]
            if k >= 16:
                p.wait("gpsimd", s_t, trec[k - 16])
            p.dma("gpsimd", w1t[k % 16][:, 0:128],
                  io["w2_1"][512 * e + 128 * mch:512 * e + 128 * mch + 128,
                             128 * m:128 * m + 128], sem=s_w)

        PREF = 8
        for k in range(PREF):
            w2_load(k)
        ps = pi = None
        for k, (m, e, mch) in enumerate(seq):
            if k + PREF < len(seq):
                w2_load(k + PREF)
            if mch == 0 and e == 0:
                ps, pi = pp.get()
            p.wait("tensor", s_w)
            p.op("tensor", "matmul", ps[:, 0:n],
                 w1t[k % 16][:, 0:128], hsb[4 * e + mch][:, 0:n],
                 start=(e == 0 and mch == 0), stop=False, sem=s_t)
            trec.append(s_t.val)
            if e == 7 and mch == 3:
                p.op("tensor", "matmul", ps[:, 0:n],
                     b2t[:, 128 * m:128 * m + 128],
                     mfm[:, o:o + n], start=False, stop=True, sem=s_t)
                p.wait("vector", s_t)
                p.wait("sync", s_v)
                p.dma("sync", x1c[:, 0:n],
                      io["x1"][128 * m:128 * m + 128, o:o + n], sem=s_in)
                p.wait("scalar", s_in)
                p.wait("scalar", s_t)
                p.op("scalar", "activation", mrep[:, 0:n], ps[:, 0:n],
                     AF.Silu, sem=s_sc)
                pp.done(pi, s_sc)
                p.wait("vector", s_sc)
                p.opw("vector", s_v, "tensor_mul", of[:, 0:n], mrep[:, 0:n],
                      s1t[m][:, o:o + n])
                p.opw("vector", s_v, "tensor_add", of[:, 0:n], of[:, 0:n],
                      x1c[:, 0:n])
                p.wait("sync", s_v)
                p.dma("sync", io["outT"][128 * m:128 * m + 128, o:o + n],
                      of[:, 0:n], sem=s_out)
                p.wait("vector", s_out)
    p.wait("sync", s_out)
    p.scope_close(s_b)


STAGE_L_INS = {
    "gxn": ([DM, L], BF16), "xnloc": ([DM, TPC], BF16),
    "x1": ([DM, TPC], F32),
    "kpT": ([DM, 32], F32), "Wk1": ([DM, DM], F32), "bk1": ([1, DM], F32),
    "Wk1h": ([DM, 128], F32), "bk1h": ([1, 128], F32),
    "Wv1h": ([DM, 128], F32), "bv1h": ([1, 128], F32),
    "Wq1h": ([DM, 128], F32), "bq1h": ([128, 1], F32),
    "Wo1": ([DM, DM], F32), "bo1": ([128, 8], F32),
    "gate_w1": ([DM, 8], F32), "w1_1": ([8 * DM, 512], F32),
    "w2_1": ([8 * 512, DM], F32), "b1c": ([128, 32], F32),
    "b2_1": ([8, DM], F32),
    "kramp": ([128, 512], F32),
    "ones2": ([128, 2], BF16), "onesrf": ([1, 128], F32),
    "onesrb": ([1, 128], BF16), "ident": ([128, 128], BF16),
    "ones128": ([128, 128], BF16),
}
STAGE_L_OUTS = {"outT": ([DM, TPC], F32)}


def stage_l_inputs(inputs, gxn, x1, c):
    return {
        "gxn": gxn, "xnloc": np.ascontiguousarray(gxn[:, c * TPC:(c + 1) * TPC]),
        "x1": x1,
        **stage_l_ext(inputs, c),
    }


def stage_l_ext(inputs, c):
    kr = np.tile(np.arange(1, 33, dtype=np.float32), 16)[None, :].repeat(128, 0)
    sl = slice(128 * c, 128 * c + 128)
    return {
        "kpT": np.ascontiguousarray(
            np.broadcast_to(inputs["L1_attn_key_param"][0], (32, DM)).T),
        "Wk1": inputs["L1_attn_Wk"], "bk1": inputs["L1_attn_bk"].reshape(1, DM),
        "Wk1h": np.ascontiguousarray(inputs["L1_attn_Wk"][:, sl]),
        "bk1h": inputs["L1_attn_bk"][sl].reshape(1, 128),
        "Wv1h": np.ascontiguousarray(inputs["L1_attn_Wv"][:, sl]),
        "bv1h": inputs["L1_attn_bv"][sl].reshape(1, 128),
        "Wq1h": np.ascontiguousarray(inputs["L1_attn_Wq"][:, sl]),
        "bq1h": inputs["L1_attn_bq"][sl].reshape(128, 1),
        "Wo1": inputs["L1_attn_Wo"],
        "bo1": np.ascontiguousarray(inputs["L1_attn_bo"].reshape(8, 128).T),
        "gate_w1": inputs["L1_ffn_gate_w"],
        "w1_1": inputs["L1_ffn_w1"].reshape(8 * DM, 512),
        "w2_1": inputs["L1_ffn_w2"].reshape(8 * 512, DM),
        "b1c": np.ascontiguousarray(
            inputs["L1_ffn_b1"].reshape(8, 4, 128).transpose(2, 0, 1)
            .reshape(128, 32)),
        "b2_1": inputs["L1_ffn_b2"],
        "kramp": np.ascontiguousarray(kr),
        "ones2": np.ones((128, 2), BF),
        "onesrf": np.ones((1, 128), np.float32),
        "onesrb": np.ones((1, 128), BF),
        "ident": np.eye(128, dtype=BF),
        "ones128": np.ones((128, 128), BF),
    }


def build_stage_l():
    nc = bass.Bass("TRN2", target_bir_lowering=False)
    io = {}
    for k, (shape, dt) in STAGE_L_INS.items():
        io[k] = nc.dram_tensor(k, shape, dt, kind="ExternalInput").ap()
    for k, (shape, dt) in STAGE_L_OUTS.items():
        io[k] = nc.dram_tensor(k, shape, dt, kind="ExternalOutput").ap()
    p = Prog(nc)
    stage_l1(p, io)
    p.emit_and_finish()
    return nc


def stage_t2(p, io, fz=None):
    s_in = p.sem("t2in"); s_c = p.sem("t2c"); s_v = p.sem("t2v")
    s_t = p.sem("t2t"); s_sc = p.sem("t2sc"); s_out = p.sem("t2out")

    kh = p.sbuf("T2_kh", [128, L], BF16)
    vh = p.sbuf("T2_vh", [128, L], BF16)
    qh = p.sbuf("T2_qh", [128, L], BF16)
    pbuf = [p.sbuf(f"T2_pbuf{i}", [128, 1024], BF16) for i in range(2)]
    mbuf = [[p.sbuf(f"T2_mb{j}_{i}", [128, 512], BF16) for i in range(2)]
            for j in range(2)]
    masks = [p.sbuf(f"T2_mask{i}", [128, 512], BF16) for i in range(4)]
    denrep = p.sbuf("T2_denrep", [128, 512], BF16)
    denrow = p.sbuf("T2_denrow", [1, 512], F32)
    afin = p.sbuf("T2_afin", [128, 512], BF16)
    onesrf = p.sbuf("T2_onesrf", [1, 128], F32)
    onescol = p.sbuf("T2_ones", [128, 2], BF16)
    mrow = p.sbuf("T2_mrow", [1, 32], F32)
    mcol = p.sbuf("T2_mcol", [128, 1], F32)

    pssc = [p.psum(f"T2_sc{i}", [128, 1024]) for i in range(2)]
    pso = p.psum("T2_o", [128, 512])
    psd = p.psum("T2_d", [1, 512])

    for i in range(4):
        p.dma("sync", masks[i][:], io[f"mask{i}"], sem=s_c)
    p.dma("sync", onesrf[:], io["onesrf"], sem=s_c)
    p.dma("sync", onescol[:], io["ones2"], sem=s_c)
    if fz is None:
        p.dma("sync", mrow[0:1, 0:8], io["qmax8"], sem=s_c)
        p.dma("sync", mrow[0:1, 8:16], io["kmax8"], sem=s_c)
    else:
        for s in range(NCORE):
            p.dma("sync", mrow[0:1, s:s + 1], fz["Gqkm"][s:s + 1, 0:1],
                  sem=s_c)
            p.dma("sync", mrow[0:1, 8 + s:9 + s], fz["Gqkm"][s:s + 1, 1:2],
                  sem=s_c)
    p.dma("sync", kh[:], io["akh"], sem=s_in)
    p.dma("sync", qh[:], io["aqh"], sem=s_in)
    p.dma("sync", vh[:].rearrange("p (t d) -> p t d", d=128),
          io["avh"].rearrange("(t p) d -> p t d", p=128), sem=s_in)
    IN_D = s_in.val

    p.wait("vector", s_c)
    p.opw("vector", s_v, "reduce_max", mrow[0:1, 16:17], mrow[0:1, 0:8],
          axis=AX.X)
    p.opw("vector", s_v, "reduce_max", mrow[0:1, 17:18], mrow[0:1, 8:16],
          axis=AX.X)
    p.wait("scalar", s_v)
    p.op("scalar", "activation", mrow[0:1, 18:20], mrow[0:1, 16:18], AF.Sqrt,
         sem=s_sc)
    p.wait("vector", s_sc)
    p.opw("vector", s_v, "tensor_mul", mrow[0:1, 20:21], mrow[0:1, 18:19],
          mrow[0:1, 19:20])
    p.opw("vector", s_v, "tensor_scalar_mul", mrow[0:1, 21:22],
          mrow[0:1, 20:21], -1.0)
    p.wait("tensor", s_v)
    p.op("tensor", "matmul", pso[:, 0:1], onesrf[:], mrow[0:1, 21:22],
         start=True, stop=True, sem=s_t)
    p.wait("vector", s_t)
    p.op("vector", "tensor_copy", mcol[:], pso[:, 0:1], sem=s_v)
    p.wait("scalar", s_v)
    p.wait("tensor", s_v)

    p.wait("tensor", s_in, IN_D)
    p.wait("scalar", s_in, IN_D)
    for qs in range(L // 512):
        qs0 = qs * 512
        kt_hi = (qs0 + 512) // 128
        ngroups = (kt_hi + 1) // 2
        first = True

        def grp(g):
            return [kt for kt in (2 * g, 2 * g + 1) if kt < kt_hi]

        def qk(g):
            kts = grp(g)
            sc = pssc[g % 2]
            for gi, kt in enumerate(kts):
                p.op("tensor", "matmul", sc[:, gi * 512:gi * 512 + 512],
                     kh[:, kt * 128:kt * 128 + 128],
                     qh[:, qs0:qs0 + 512], start=True, stop=True,
                     sem=s_t if gi == len(kts) - 1 else None)
            return s_t.val

        qkv = qk(0)
        for g in range(ngroups):
            kts = grp(g)
            sc = pssc[g % 2]
            pb = pbuf[g % 2]
            qkv_cur = qkv
            if g + 1 < ngroups:
                qkv = qk(g + 1)
            p.wait("scalar", s_t, qkv_cur)
            width = len(kts) * 512
            p.op("scalar", "activation", pb[:, 0:width], sc[:, 0:width],
                 AF.Exp, bias=mcol[:, 0:1], sem=s_sc)
            av_src = []
            for gi, kt in enumerate(kts):
                off = kt * 128 - qs0
                if off >= 0:
                    p.wait("vector", s_sc)
                    p.op("vector", "tensor_mul", mbuf[g % 2][gi][:],
                         pb[:, gi * 512:gi * 512 + 512],
                         masks[off // 128][:], sem=s_v)
                    av_src.append(mbuf[g % 2][gi][:, 0:512])
                else:
                    av_src.append(pb[:, gi * 512:gi * 512 + 512])
            p.wait("tensor", s_sc)
            p.wait("tensor", s_v)
            for gi, kt in enumerate(kts):
                p.op("tensor", "matmul", pso[:, 0:512],
                     vh[:, kt * 128:kt * 128 + 128], av_src[gi],
                     start=first, stop=False)
                p.op("tensor", "matmul", psd[0:1, 0:512], onescol[:, 0:1],
                     av_src[gi], start=first, stop=False, sem=s_t)
                first = False
        p.wait("vector", s_t)
        p.opw("vector", s_v, "reciprocal", denrow[:], psd[0:1, 0:512])
        p.wait("tensor", s_v)
        p.op("tensor", "matmul", pssc[0][:, 0:512], onesrf[:], denrow[:],
             start=True, stop=True, sem=s_t)
        p.wait("vector", s_t)
        p.opw("vector", s_v, "tensor_copy", denrep[:], pssc[0][:, 0:512])
        p.op("vector", "tensor_mul", afin[:], pso[:, 0:512], denrep[:],
             sem=s_v)
        p.wait("sync", s_v)
        p.dma("sync", io["af"][:, qs0:qs0 + 512], afin[:], sem=s_out)
        p.wait("vector", s_out)
        p.wait("tensor", s_v)
        first = True
    p.wait("sync", s_out)


STAGE_T2_INS = {
    "aqh": ([128, L], BF16), "akh": ([128, L], BF16), "avh": ([L, 128], BF16),
    "qmax8": ([1, 8], F32), "kmax8": ([1, 8], F32),
    "onesrf": ([1, 128], F32), "ones2": ([128, 2], BF16),
    "mask0": ([128, 512], BF16), "mask1": ([128, 512], BF16),
    "mask2": ([128, 512], BF16), "mask3": ([128, 512], BF16),
}
STAGE_T2_OUTS = {"af": ([128, L], BF16)}


def stage_t2_inputs(aouts, c):
    aqh = np.concatenate([aouts["caq"][s][128 * c:128 * c + 128, :]
                          for s in range(NCORE)], axis=1)
    akh = np.concatenate([aouts["cak"][s][128 * c:128 * c + 128, :]
                          for s in range(NCORE)], axis=1)
    avh = np.concatenate([aouts["cav"][s][:, 128 * c:128 * c + 128]
                          for s in range(NCORE)], axis=0)
    qmax8 = np.concatenate([aouts["qkm"][s][:, 0:1] for s in range(NCORE)],
                           axis=1).astype(np.float32)
    kmax8 = np.concatenate([aouts["qkm"][s][:, 1:2] for s in range(NCORE)],
                           axis=1).astype(np.float32)
    d = {"aqh": np.ascontiguousarray(aqh), "akh": np.ascontiguousarray(akh),
         "avh": np.ascontiguousarray(avh), "qmax8": qmax8, "kmax8": kmax8,
         "onesrf": np.ones((1, 128), np.float32),
         "ones2": np.ones((128, 2), BF)}
    for off in range(4):
        j = np.arange(128)[:, None]
        i = np.arange(512)[None, :]
        d[f"mask{off}"] = (j + off * 128 <= i).astype(BF)
    return d


def build_stage_t2():
    nc = bass.Bass("TRN2", target_bir_lowering=False)
    io = {}
    for k, (shape, dt) in STAGE_T2_INS.items():
        io[k] = nc.dram_tensor(k, shape, dt, kind="ExternalInput").ap()
    for k, (shape, dt) in STAGE_T2_OUTS.items():
        io[k] = nc.dram_tensor(k, shape, dt, kind="ExternalOutput").ap()
    p = Prog(nc)
    stage_t2(p, io)
    p.emit_and_finish()
    return nc


_BUILD_CACHE = {}


def _get(name, fn):
    if name not in _BUILD_CACHE:
        _BUILD_CACHE[name] = fn()
    return _BUILD_CACHE[name]


def kernel_multi(**inputs):
    cores = list(range(NCORE))
    ncA = _get("A", build_stage_a)
    resA = run_bass_kernel_spmd(
        ncA, [stage_a_inputs(inputs, c) for c in cores], core_ids=cores)
    aouts = {k: [np.asarray(resA.results[c][k]) for c in cores]
             for k in STAGE_A_OUTS}
    ncD = _get("D", build_stage_d)
    resD = run_bass_kernel_spmd(
        ncD, [stage_d_inputs(aouts, c) for c in cores], core_ids=cores)
    do_full = np.concatenate([np.asarray(resD.results[c]["do_tm"])
                              for c in cores], axis=1)
    ncT = _get("T2", build_stage_t2)
    resT = run_bass_kernel_spmd(
        ncT, [stage_t2_inputs(aouts, c) for c in cores], core_ids=cores)
    af_full = np.concatenate([np.asarray(resT.results[c]["af"])
                              for c in cores], axis=0)
    ncM = _get("M", build_stage_m)
    in_maps = []
    for c in cores:
        gaf_own = np.ascontiguousarray(af_full[:, c * TPC:(c + 1) * TPC])
        in_maps.append(stage_m_inputs(inputs, aouts, do_full, gaf_own, c))
    resM = run_bass_kernel_spmd(ncM, in_maps, core_ids=cores)
    gxn = np.ascontiguousarray(np.concatenate(
        [np.asarray(resM.results[c]["xn2"]) for c in cores], axis=1))
    ncL = _get("L", build_stage_l)
    in_maps = []
    for c in cores:
        x1 = np.ascontiguousarray(np.asarray(resM.results[c]["x1"]))
        in_maps.append(stage_l_inputs(inputs, gxn, x1, c))
    resL = run_bass_kernel_spmd(ncL, in_maps, core_ids=cores)
    out = np.empty((L, DM), np.float32)
    for c in cores:
        out[c * TPC:(c + 1) * TPC] = np.asarray(resL.results[c]["outT"]).T
    return out.reshape(1, L, DM)


RG8 = [list(range(8))]


def build_fused(debug=False):
    nc = bass.Bass("TRN2", target_bir_lowering=False)
    ET = mybir.EngineType
    pid = nc.partition_id([ET.SP, ET.Activation, ET.Pool])
    hh = pid >> 1
    hf = pid & 1

    ext = {}

    def xin(name, shape, dt):
        if name not in ext:
            ext[name] = nc.dram_tensor(name, shape, dt,
                                       kind="ExternalInput").ap()
        return ext[name]

    def itn(name, shape, dt, shared=False):
        kw = {"addr_space": "Shared"} if shared else {}
        return nc.dram_tensor(name, shape, dt, kind="Internal", **kw)

    cq_i = itn("cq_i", [DM, TPC], BF16)
    ck_i = itn("ck_i", [DM, TPC], BF16)
    caq_i = itn("caq_i", [DM, TPC], BF16)
    cak_i = itn("cak_i", [DM, TPC], BF16)
    ckt_i = itn("ckt_i", [TPC, DM], BF16)
    cvt_i = itn("cvt_i", [TPC, DM], BF16)
    cav_i = itn("cav_i", [TPC, DM], BF16)
    cb_i = itn("cb_i", [4, TPC], BF16)
    qkm_i = itn("qkm_i", [1, 2], F32)
    dfm_i = itn("dfm_i", [128, L], BF16)
    af_i = itn("af_i", [128, L], BF16)
    x2_i = itn("x2_i", [DM, TPC], BF16)
    Gcq = itn("Gcq", [8 * DM, TPC], BF16, True)
    Gck = itn("Gck", [8 * DM, TPC], BF16, True)
    Gcaq = itn("Gcaq", [8 * DM, TPC], BF16, True)
    Gcak = itn("Gcak", [8 * DM, TPC], BF16, True)
    Gckt = itn("Gckt", [L, DM], BF16, True)
    Gcvt = itn("Gcvt", [L, DM], BF16, True)
    Gcav = itn("Gcav", [L, DM], BF16, True)
    Gcb = itn("Gcb", [32, TPC], BF16, True)
    Gqkm = itn("Gqkm", [8, 2], F32, True)
    Gdofm = itn("Gdofm", [DM, L], BF16, True)
    Gaf = itn("Gaf", [DM, L], BF16, True)
    Gx = itn("Gx", [8 * DM, TPC], BF16, True)
    ff_i = itn("ff_i", [1, 1], F32)
    Gff = itn("Gff", [8, 1], F32, True)
    qh_l = itn("qh_l", [256, L], BF16)
    kh_l = itn("kh_l", [256, L], BF16)
    ktm_l = itn("ktm_l", [L, 256], BF16)
    vtm_l = itn("vtm_l", [L, 128], BF16)
    brow_l = itn("brow_l", [1, L], F32)
    aqh_l = itn("aqh_l", [128, L], BF16)
    akh_l = itn("akh_l", [128, L], BF16)
    avh_l = itn("avh_l", [L, 128], BF16)
    gaf_l = itn("gaf_l", [DM, TPC], BF16)
    gdofm_l = itn("gdofm_l", [DM, TPC], BF16)
    qf_l = itn("qf_l", [DM, TPC], BF16)
    g16_l = itn("g16_l", [TPC, 16], F32)
    zx_l = itn("zx_l", [512, TPC], BF16)
    cv_l = itn("cv_l", [DM, TH - 3], BF16)
    x1_l = itn("x1_l", [DM, TPC], F32)
    bhcol_l = itn("bhcol_l", [128, NCHUNK], F32)
    dbg = {}
    for k, (shape, dt) in list(STAGE_A_OUTS.items()) + \
            list(STAGE_D_OUTS.items()):
        if k.startswith("dbg_"):
            dbg[k] = itn("f_" + k, shape, dt).ap()

    outT = nc.dram_tensor("outT", [DM, TPC], F32, kind="ExternalOutput").ap()

    fz = {"pid": pid, "h": hh, "half": hf, "Gqkm": Gqkm.ap(), "Gx": Gx.ap()}

    ioA = {k: xin(k, *v) for k, v in STAGE_A_INS.items()}
    ioA.update({"cq": cq_i.ap(), "ck": ck_i.ap(), "ckt": ckt_i.ap(),
                "cvt": cvt_i.ap(), "cv": cv_l.ap(), "cb": cb_i.ap(),
                "caq": caq_i.ap(), "cak": cak_i.ap(), "cav": cav_i.ap(),
                "qf": qf_l.ap(), "g16": g16_l.ap(), "zx": zx_l.ap(),
                "qkm": qkm_i.ap(), **dbg})
    ioD = {k: xin(k, *STAGE_D_INS[k])
           for k in ("ident", "ones2", "onesrf", "mnu1", "mnl1", "mu0")}
    ioD.update({"qh_fm": qh_l.ap(), "kh_fm": kh_l.ap(), "kh_tm": ktm_l.ap(),
                "vh_tm": vtm_l.ap(), "bh_row": brow_l.ap(),
                "bh_col": bhcol_l.ap(), "do_fm": dfm_i.ap(), **dbg})
    ioT = {k: xin(k, *STAGE_T2_INS[k])
           for k in ("onesrf", "ones2", "mask0", "mask1", "mask2", "mask3")}
    ioT.update({"aqh": aqh_l.ap(), "akh": akh_l.ap(), "avh": avh_l.ap(),
                "af": af_i.ap()})
    M_DEV = {"qf", "g16", "cv", "cvt", "gdo_fm", "gdo_tm", "gaf", "zx"}
    ioM = {k: xin(k, *v) for k, v in STAGE_M_INS.items() if k not in M_DEV}
    ioM.update({"qf": qf_l.ap(), "g16": g16_l.ap(), "cv": cv_l.ap(),
                "cvt": cvt_i.ap(), "zx": zx_l.ap(), "gaf": gaf_l.ap(),
                "gdo_fm": gdofm_l.ap(),
                "x1": x1_l.ap(), "xn2": x2_i.ap()})
    L_DEV = {"gxn", "xnloc", "x1"}
    ioL = {k: xin(k, *v) for k, v in STAGE_L_INS.items() if k not in L_DEV}
    ioL.update({"gxn": Gx.ap(), "xnloc": x2_i.ap(), "x1": x1_l.ap(),
                "outT": outT})

    p = Prog(nc)
    s_fb = p.sem("fbar")
    s_ag = p.sem("fag")
    s_st = p.sem("fst")

    def ag(t_in, t_out):
        p.op("gpsimd", "collective_compute", "AllGather",
             mybir.AluOpType.bypass, replica_groups=RG8,
             ins=[t_in.ap()], outs=[t_out.ap()], sem=s_ag)

    def ag_fence():
        for _ in range(2):
            p.wait("gpsimd", s_ag)
            p.op("gpsimd", "collective_compute", "AllGather",
                 mybir.AluOpType.bypass, replica_groups=RG8,
                 ins=[ff_i.ap()], outs=[Gff.ap()], sem=s_ag)
        p.wait("gpsimd", s_ag)

    def hook_tm(s_out):
        p.wait("gpsimd", s_out)
        for t_in, t_out in ((cq_i, Gcq), (ck_i, Gck), (ckt_i, Gckt),
                            (cvt_i, Gcvt), (cb_i, Gcb)):
            ag(t_in, t_out)

    def hook_attn(s_out):
        p.wait("gpsimd", s_out)
        for t_in, t_out in ((caq_i, Gcaq), (cak_i, Gcak), (cav_i, Gcav)):
            ag(t_in, t_out)

    p.scope_open()
    stage_a(p, ioA, hook_tm, hook_attn)
    p.scope_close(s_fb)
    ag(qkm_i, Gqkm)
    ag_fence()
    p.barrier(s_fb)

    s_stq = p.sem("fstq")
    s_sts = p.sem("fsts")
    s_stg = p.sem("fstg")
    fz["sem_q"], fz["sem_s"], fz["sem_g"] = s_stq, s_sts, s_stg
    fz["stv_q"], fz["stv_s"], fz["stv_g"] = [], [], []
    for s in range(NCORE):
        cols = slice(s * TPC, (s + 1) * TPC)
        p.dma("gpsimd", brow_l.ap()[0:1, cols],
              Gcb.ap().rearrange("(s r) c -> s r c", r=4)[
                  s:s + 1, bass.ds(hh, 1), :].rearrange("s r c -> (s r) c"),
              sem=s_stg)
        p.dma("sync", qh_l.ap()[:, cols],
              Gcq.ap().rearrange("(s r) c -> s r c", r=DM)[
                  s:s + 1, bass.ds(256 * hh, 256), :].rearrange(
                  "s r c -> (s r) c"), sem=s_stq)
        p.dma("scalar", kh_l.ap()[:, cols],
              Gck.ap().rearrange("(s r) c -> s r c", r=DM)[
                  s:s + 1, bass.ds(256 * hh, 256), :].rearrange(
                  "s r c -> (s r) c"), sem=s_sts)
        p.dma("sync", ktm_l.ap()[cols, :],
              Gckt.ap()[cols, bass.ds(256 * hh, 256)], sem=s_stq)
        p.dma("scalar", vtm_l.ap()[cols, :],
              Gcvt.ap()[cols, bass.ds(256 * hh + 128 * hf, 128)], sem=s_sts)
        fz["stv_q"].append(s_stq.val)
        fz["stv_s"].append(s_sts.val)
        fz["stv_g"].append(s_stg.val)
    p.scope_open()
    stage_delta(p, ioD, fz)
    for ha in range(2):
        rows = slice(ha * (L // 2), (ha + 1) * (L // 2))
        p.dma("gpsimd", avh_l.ap()[rows, :],
              Gcav.ap()[rows, bass.ds(128 * pid, 128)], sem=s_st)
    p.dma("gpsimd", aqh_l.ap().rearrange("r (s c) -> s r c", c=TPC),
          Gcaq.ap().rearrange("(s r) c -> s r c", r=DM)[
              :, bass.ds(128 * pid, 128), :], sem=s_st)
    p.dma("gpsimd", akh_l.ap().rearrange("r (s c) -> s r c", c=TPC),
          Gcak.ap().rearrange("(s r) c -> s r c", r=DM)[
              :, bass.ds(128 * pid, 128), :], sem=s_st)
    VAL_T = s_st.val
    p.scope_close(s_fb)
    p.wait("sync", s_st, VAL_T)
    ag(dfm_i, Gdofm)

    p.scope_open()
    stage_t2(p, ioT, fz)
    p.scope_close(s_fb)
    ag(af_i, Gaf)

    def hook_a2():
        ag_fence()
        p.dma("sync", gaf_l.ap(), Gaf.ap()[:, bass.ts(pid, TPC)], sem=s_st)
        p.wait("sync", s_st)
        p.dma("sync", gdofm_l.ap(), Gdofm.ap()[:, bass.ts(pid, TPC)],
              sem=s_st)
        p.wait("sync", s_st)

    p.scope_open()
    stage_m(p, ioM, hook_a2=hook_a2)
    p.scope_close(s_fb)
    ag(x2_i, Gx)
    ag_fence()
    p.barrier(s_fb)

    p.scope_open()
    stage_l1(p, ioL, fz)
    p.scope_close(s_fb)

    if debug:
        dumps = {"qh_l": qh_l, "kh_l": kh_l, "ktm_l": ktm_l, "vtm_l": vtm_l,
                 "brow_l": brow_l, "bhcol_l": bhcol_l, "aqh_l": aqh_l,
                 "akh_l": akh_l, "avh_l": avh_l, "gaf_l": gaf_l,
                 "gdofm_l": gdofm_l,
                 "dfm_i": dfm_i, "af_i": af_i, "x2_i": x2_i, "x1_l": x1_l,
                 "qkm_i": qkm_i, "Gqkm": Gqkm, "cq_i": cq_i, "cav_i": cav_i}
        s_dbg = p.sem("fdbg")
        for nm, t in dumps.items():
            o = nc.dram_tensor("dump_" + nm, t.shape, t.dtype,
                               kind="ExternalOutput").ap()
            p.dma("sync", o, t.ap(), sem=s_dbg)
        p.wait("sync", s_dbg)
    p.emit_and_finish()
    return nc


def fused_inputs(inputs, c):
    m = dict(stage_a_inputs(inputs, c))
    iu = np.triu(np.ones((128, 128), np.float32), 1)
    il = np.tril(np.ones((128, 128), np.float32), -1)
    m.update({"mnu1": -iu, "mnl1": -il,
              "mu0": np.triu(np.ones((128, 128), np.float32))})
    for off in range(4):
        j = np.arange(128)[:, None]
        i = np.arange(512)[None, :]
        m[f"mask{off}"] = (j + off * 128 <= i).astype(BF)
    m.update(stage_m_ext(inputs, c))
    m.update(stage_l_ext(inputs, c))
    m.setdefault("onesrb", np.ones((1, 128), BF))
    m.setdefault("ones128", np.ones((128, 128), BF))
    return m


def kernel(**inputs):
    cores = list(range(NCORE))
    ncF = _get("F", build_fused)
    res = run_bass_kernel_spmd(ncF, [fused_inputs(inputs, c) for c in cores],
                               core_ids=cores)
    out = np.empty((L, DM), np.float32)
    for c in cores:
        out[c * TPC:(c + 1) * TPC] = np.asarray(res.results[c]["outT"]).T
    return out.reshape(1, L, DM)
